# version 01
# baseline (speedup 1.0000x reference)
import numpy as np
import ml_dtypes
import concourse.bass as bass
import concourse.mybir as mybir
from concourse.bass_utils import run_bass_kernel_spmd

F32 = mybir.dt.float32
BF16 = mybir.dt.bfloat16
AF = mybir.ActivationFunctionType
ALU = mybir.AluOpType
AX = mybir.AxisListType
BF16NP = ml_dtypes.bfloat16

D, T, H, DH, FC, TC = 1024, 2048, 16, 64, 8, 16
HID, NBLK, BLK, TOPK, U, NCORE = 4096, 16, 256, 4, 1228, 8


def _prep_shared(I):
    g = {}
    C = np.ascontiguousarray

    def chunkcol(b, n=8):
        return C(b.reshape(n, 128).T)

    g["ident_b"] = np.eye(128, dtype=np.float32)
    g["ones_b"] = np.ones((1, 128), dtype=np.float32)
    g["ones_bcol"] = np.ones((128, 1), dtype=np.float32)
    g["ones_fcol"] = np.ones((128, 1), dtype=np.float32)
    g["ones_frow"] = np.ones((1, 128), dtype=np.float32)

    g["aWq"] = C(I["L0_attn_a_Wq"]); g["aWk"] = C(I["L0_attn_a_Wk"])
    g["aBq"] = chunkcol(I["L0_attn_a_bq"]); g["aBk"] = chunkcol(I["L0_attn_a_bk"])
    g["aWvb"] = C(np.vstack([I["L0_attn_a_Wv"], I["L0_attn_a_bv"][None, :]]))
    g["aWo"] = C(I["L0_attn_a_Wo"]); g["aBo"] = chunkcol(I["L0_attn_a_bo"])

    g["fW1T"] = C(I["L0_ffn_w1"].T)
    g["fB1"] = C(I["L0_ffn_b1"].reshape(32, 128).T)
    g["fW2T"] = C(I["L0_ffn_w2"].T)
    g["fB2"] = chunkcol(I["L0_ffn_b2"])
    g["fRWTb"] = C(np.vstack([I["L0_ffn_router_w"].T,
                              I["L0_ffn_router_b"][None, :]]))

    g["bWqT"] = C(I["L0_attn_b_wq"].T)
    g["bWkT"] = C(I["L0_attn_b_wk"].T)
    g["bWvT"] = C(I["L0_attn_b_wv"].T)
    g["bWogTb"] = C(np.vstack([I["L0_attn_b_wog"].T,
                               I["L0_attn_b_bog"][None, :]]))
    g["bWg1T"] = C(I["L0_attn_b_wg1"].T)
    g["bBg1"] = C(I["L0_attn_b_bg1"].reshape(16, 1))
    g["bBg1r"] = C(I["L0_attn_b_bg1"].reshape(1, 16))
    g["bWg2Tb"] = C(np.vstack([I["L0_attn_b_wg2"].T,
                               I["L0_attn_b_bg2"][None, :]]))
    g["bWoT"] = C(I["L0_attn_b_wo"].T)
    g["bWgT"] = C(I["L0_attn_b_w_gate"].T)
    g["bWuT"] = C(I["L0_attn_b_w_up"].T)
    g["bWdT"] = C(I["L0_attn_b_w_down"].T)
    g["bS1"] = chunkcol(I["L0_attn_b_scale1"])
    g["bS2"] = chunkcol(I["L0_attn_b_scale2"])
    return g


def kernel(**inputs):
    return _run(inputs)


def _run(inputs, taps=None, stop_after=None):
    taps = taps or []
    shared = _prep_shared(inputs)
    x = inputs["L0_attn_a_x"]
    in_maps = []
    for c in range(NCORE):
        m = dict(shared)
        m["xT"] = np.ascontiguousarray(x[c].T)
        in_maps.append(m)

    nc = _build(taps=taps, stop_after=stop_after)
    res = run_bass_kernel_spmd(nc, in_maps, core_ids=list(range(NCORE)))
    if taps or stop_after:
        return res

    out = np.empty((8, 2048, 1024), dtype=np.float32)
    for c in range(NCORE):
        yT = res.results[c]["yT"]
        out[:, c * 256:(c + 1) * 256, :] = yT.T.reshape(8, 256, 1024)
    return out


BASE = 16384 + 1024
SB = dict(
    RA=BASE,
    RXH=BASE + 65536,
    RBV=BASE + 98304,
    RCC=BASE + 163840,
)
RCC_END = BASE + 196608


def _build(taps=(), stop_after=None):
    nc = bass.Bass("TRN2", target_bir_lowering=False)
    cx = Ctx()
    cx.nc = nc
    cx.taps = list(taps)
    cx.tap_outs = {}
    cx.stop_after = stop_after
    cx.semid = 0

    dr = {}
    dr["xT"] = nc.dram_tensor("xT", [1024, 2048], F32, kind="ExternalInput").ap()
    for n, shp in [
        ("ident_b", [128, 128]), ("ones_b", [1, 128]),
        ("ones_bcol", [128, 1]), ("ones_fcol", [128, 1]), ("ones_frow", [1, 128]),
        ("aWq", [1024, 1024]), ("aWk", [1024, 1024]),
        ("aBq", [128, 8]), ("aBk", [128, 8]),
        ("aWvb", [1025, 1024]), ("aWo", [1024, 1024]), ("aBo", [128, 8]),
        ("fW1T", [1024, 4096]), ("fB1", [128, 32]),
        ("fW2T", [4096, 1024]), ("fB2", [128, 8]), ("fRWTb", [1025, 16]),
        ("bWqT", [1024, 1024]), ("bWkT", [1024, 1024]), ("bWvT", [1024, 1024]),
        ("bWogTb", [1025, 1024]), ("bWg1T", [1024, 16]), ("bBg1", [16, 1]),
        ("bBg1r", [1, 16]),
        ("bWg2Tb", [17, 1024]), ("bWoT", [1024, 1024]),
        ("bWgT", [1024, 4096]), ("bWuT", [1024, 4096]), ("bWdT", [4096, 1024]),
        ("bS1", [128, 8]), ("bS2", [128, 8]),
    ]:
        dr[n] = nc.dram_tensor(n, shp, F32, kind="ExternalInput").ap()
    dr["yT"] = nc.dram_tensor("yT", [1024, 2048], F32, kind="ExternalOutput").ap()
    dr["x1d"] = nc.dram_tensor("x1d", [1024, 2048], F32, kind="Internal").ap()
    dr["yd"] = nc.dram_tensor("yd", [1024, 2048], F32, kind="Internal").ap()
    dr["z1d"] = nc.dram_tensor("z1d", [1024, 2048], F32, kind="Internal").ap()
    dr["x4d"] = nc.dram_tensor("x4d", [1024, 2048], F32, kind="Internal").ap()
    dr["cc_in"] = nc.dram_tensor("cc_in", [8, 1024, 256], F32, kind="Internal").ap()
    dr["cc_out"] = nc.dram_tensor("cc_out", [8, 8, 1024, 256], F32,
                                  kind="Internal", addr_space="Shared").ap()
    dr["x2mine"] = nc.dram_tensor("x2mine", [8, 1024, 256], F32,
                                  kind="Internal").ap()
    cx.dr = dr

    A = {}
    def at(name, off, shape, dtype):
        A[name] = nc.alloc_sbuf_tensor_at(name, shape, dtype, offset=off)
        return A[name]

    at("Qfm", SB["RA"], [128, 16384], BF16)
    at("Kfm", SB["RA"] + 32768, [128, 16384], BF16)
    at("cmask", SB["RA"] + 32768, [128, 16384], BF16)
    at("cfm", SB["RCC"], [128, 16384], BF16)

    at("xhat", SB["RXH"], [128, 16384], BF16)
    at("wbX", SB["RXH"], [128, 8192], BF16)
    at("wbY", SB["RXH"] + 16384, [128, 8192], BF16)
    at("Vtok", SB["RBV"], [128, 16640], BF16)
    at("hb", SB["RBV"], [128, 32768], BF16)
    at("PT0", SB["RBV"] + 33280, [128, 4096], BF16)
    at("PT1", SB["RBV"] + 41472, [128, 4096], BF16)
    at("worot0", SB["RXH"] + 16384, [128, 512], F32)
    at("worot1", SB["RXH"] + 18432, [128, 512], F32)
    at("lnr0", SB["RBV"] + 33280, [128, 2048], F32)
    at("lnr1", SB["RBV"] + 41472, [128, 2048], F32)
    at("cct", SB["RCC"], [128, 16384], BF16)
    at("rrep", SB["RCC"], [128, 2048], F32)
    at("mrep", SB["RCC"] + 8192, [128, 2048], F32)
    at("tmpF", SB["RCC"] + 16384, [128, 2048], F32)
    at("tmpG", SB["RCC"] + 24576, [128, 2048], F32)
    at("wbA", SB["RCC"], [128, 8192], BF16)
    at("wbB2", SB["RCC"] + 16384, [128, 8192], BF16)
    at("tmpA", SB["RXH"] + 24576, [128, 512], BF16)
    at("tmpB", SB["RXH"] + 25600, [128, 512], BF16)
    at("tmpC", SB["RXH"] + 26624, [128, 512], BF16)
    at("tmpD", SB["RXH"] + 27648, [128, 512], BF16)
    at("wbB", SB["RA"], [128, 8192], BF16)
    at("mrepb", SB["RA"] + 16384, [128, 2048], BF16)
    at("maskT", SB["RA"] + 20480, [16, 2048], BF16)
    at("f_tmpA", SB["RA"] + 24576, [128, 512], BF16)
    at("f_tmpB", SB["RA"] + 25600, [128, 512], BF16)
    at("maskrow", SB["RA"] + 26624, [1, 2048], BF16)
    at("ytmp", SB["RA"] + 30720, [128, 2048], F32)
    at("frot0", SB["RA"] + 38912, [128, 512], F32)
    at("frot1", SB["RA"] + 40960, [128, 512], F32)
    at("sq2", SB["RA"], [128, 8192], BF16)
    at("msr", SB["RA"], [16, 1024], F32)
    at("sk2", SB["RA"] + 16384, [128, 8192], BF16)
    at("rstdb", SB["RA"] + 16384, [16, 1024], BF16)
    at("msb", SB["RA"] + 18432, [16, 1024], BF16)
    at("sv2", SB["RA"] + 32768, [128, 8192], BF16)
    at("zgs", SB["RA"] + 32768, [128, 8192], BF16)
    at("m1r", SB["RA"] + 32768, [16, 1024], F32)
    at("m2r", SB["RA"] + 36864, [16, 1024], F32)
    at("meanr", SB["RA"] + 40960, [16, 1024], F32)
    at("rstdr", SB["RA"] + 45056, [16, 1024], F32)
    at("so2", SB["RA"] + 49152, [128, 8192], BF16)
    at("aroll0", SB["RA"] + 49152, [128, 1024], F32)
    at("aroll1", SB["RA"] + 53248, [128, 1024], F32)
    at("rroll0", SB["RA"] + 57344, [128, 1024], F32)
    at("rroll1", SB["RA"] + 61440, [128, 1024], F32)
    at("ogbr", SB["RA"] + 49152, [1, 1024], BF16)
    at("ogstmp", SB["RA"] + 51200, [128, 512], BF16)
    at("ghb", SB["RBV"] + 32768, [128, 128], BF16)
    at("sg16", SB["RBV"] + 33024, [128, 16], BF16)
    at("cumA", SB["RBV"], [128, 8192], BF16)
    at("Pm", SB["RBV"], [16, 4608], BF16)
    at("cumR", SB["RBV"] + 16384, [128, 8192], BF16)
    at("mtm", SB["RBV"] + 16384, [128, 1024], BF16)
    at("mtm2", SB["RBV"] + 18432, [128, 1024], BF16)
    at("otA", SB["RBV"] + 20480, [128, 1024], F32)
    at("otB", SB["RBV"] + 24576, [128, 1024], F32)
    at("mscr", SB["RBV"] + 28672, [128, 1024], F32)
    at("og2", SB["RBV"] + 32768, [128, 8192], BF16)
    at("asb", SB["RBV"] + 32768, [128, 8192], BF16)
    at("zsl0", SB["RBV"] + 32768, [128, 1024], F32)
    at("zsl1", SB["RBV"] + 36864, [128, 1024], F32)
    at("yr0", SB["RBV"] + 40960, [128, 1024], F32)
    at("yr1", SB["RBV"] + 45056, [128, 1024], F32)
    at("u2", SB["RBV"] + 49152, [128, 8192], BF16)
    at("hb2", BASE, [128, 65536], BF16)
    at("gr0", BASE, [128, 2048], F32)
    at("gr1", BASE + 8192, [128, 2048], F32)
    at("nrm2", SB["RBV"] + 32768, [128, 16384], BF16)
    at("gslot", SB["RCC"], [128, 2048], F32)
    at("wgb0", SB["RCC"] + 8192, [128, 1024], BF16)
    at("wgb1", SB["RCC"] + 10240, [128, 1024], BF16)
    at("wub0", SB["RCC"] + 12288, [128, 1024], BF16)
    at("wub1", SB["RCC"] + 14336, [128, 1024], BF16)
    at("wdb0", SB["RCC"] + 16384, [128, 256], BF16)
    at("wdb1", SB["RCC"] + 16896, [128, 256], BF16)
    at("sstmp0", SB["RCC"] + 17408, [128, 512], BF16)
    at("sstmp1", SB["RCC"] + 18432, [128, 512], BF16)
    at("x2r0", SB["RCC"] + 19456, [128, 512], F32)
    at("x2r1", SB["RCC"] + 21504, [128, 512], F32)
    at("z1r0", SB["RCC"] + 23552, [128, 512], F32)
    at("z1r1", SB["RCC"] + 25600, [128, 512], F32)
    at("tglu0", SB["RCC"] + 27648, [128, 512], F32)
    at("tglu1", SB["RCC"] + 29696, [128, 512], F32)
    at("gwA", SB["RCC"], [128, 8192], BF16)
    at("gwB", SB["RCC"] + 16384, [128, 8192], BF16)
    at("wg2s", SB["RCC"] + 16384, [16, 1024], BF16)
    at("bg2r", SB["RCC"] + 18432, [1, 1024], BF16)

    cx.mo = RCC_END
    def misc(name, shape, dtype):
        nb = int(np.prod(shape[1:])) * mybir.dt.size(dtype)
        nb = (nb + 31) // 32 * 32
        A[name] = nc.alloc_sbuf_tensor_at(name, shape, dtype, offset=cx.mo)
        cx.mo += nb
        return A[name]
    misc("identb", [128, 128], BF16)
    misc("onesb", [1, 128], BF16)
    misc("onesbc", [128, 1], BF16)
    misc("onesf", [128, 1], F32)
    misc("biasq", [128, 8], F32)
    misc("biask", [128, 8], F32)
    misc("biaso", [128, 8], F32)
    misc("biasf1", [128, 32], F32)
    misc("biasf2", [128, 8], F32)
    misc("sc1", [128, 8], F32)
    misc("sc2", [128, 8], F32)
    misc("bg1", [16, 1], F32)
    misc("kbar", [128, 16], F32)
    misc("kbarb", [128, 16], BF16)
    misc("cbcol", [128, 2], BF16)
    misc("c1b", [1, 2], BF16)
    misc("cntrow", [1, 16], F32)
    misc("onesfr", [1, 128], F32)
    misc("bisx", [128, 160], F32)
    misc("m_all", [128, 256], F32)
    misc("sprs", [128, 256], F32)
    misc("amask", [128, 256], F32)
    misc("bis", [128, 64], F32)
    misc("negC", [128, 16], F32)
    misc("gtmp", [128, 32], F32)
    misc("rwT", [128, 128], BF16)
    misc("rbrow", [1, 16], BF16)
    misc("ltok", [128, 64], F32)
    misc("m8", [128, 32], F32)
    misc("mtokf", [128, 64], F32)
    misc("mtok", [128, 256], BF16)
    misc("c16", [16, 2], F32)
    misc("meancols", [128, 256], F32)
    misc("epsln", [128, 1], F32)
    misc("epsrms", [128, 1], F32)
    misc("selfP", [128, 128], BF16)
    misc("selfT", [16, 1024], BF16)
    misc("ghT", [16, 1024], BF16)
    misc("wg1s", [128, 128], BF16)
    misc("bg1r", [1, 16], BF16)
    misc("ones5", [1, 512], BF16)
    assert cx.mo <= 229376, f"misc overflow {cx.mo}"
    A["rowsb"] = None
    cx.A = A


    P = {}
    for i in range(8):
        P[i] = nc.alloc_psum_tensor(f"pb{i}", [128, 512], F32)
    cx.P = P

    cx.pools = {
        "eng": [nc.semaphore(f"pe{i}").__enter__() for i in range(56)],
        "hw": [nc.semaphore(f"ph{i}").__enter__() for i in range(20)],
    }
    cx.pool_i = {"eng": 0, "hw": 0}
    cx.sw_n = 0
    cx.s_reset = nc.semaphore("sreset").__enter__()
    cx.s_gather = nc.semaphore("sgather").__enter__()
    cx.nresets = 0

    def _sem(kind, tag):
        i = cx.pool_i[kind]
        assert i < len(cx.pools[kind]), f"{kind} sem pool exhausted at {tag}"
        cx.pool_i[kind] += 1
        return cx.pools[kind][i]
    cx.sem = lambda tag: _sem("eng", tag)
    cx.sem_hw = lambda tag: _sem("hw", tag)

    def sem_sw(tag):
        cx.sw_n += 1
        assert cx.sw_n <= 24, "sw sems exhausted"
        return nc.semaphore(f"sw{cx.sw_n}_{tag}").__enter__()
    cx.sem_sw = sem_sw
    cx.ffn_sw = None
    cx.gla_sw = None

    def stage_boundary(after):
        used = {k: cx.pool_i[k] for k in cx.pools}
        cx.nresets += 1
        k = cx.nresets
        with nc.Block(f"rst{k}") as b:
            for eng in ("scalar", "tensor"):
                def mk(engname):
                    def f(e):
                        e.wait_ge(*after)
                        e.nop().then_inc(cx.s_gather, 1)
                    return f
                getattr(b, eng)(mk(eng))

            @b.vector
            def _(v):
                v.wait_ge(*after)
                v.nop().then_inc(cx.s_gather, 1)
                v.wait_ge(cx.s_gather, 5 * k)
                for i in range(used["eng"]):
                    v.sem_clear(cx.pools["eng"][i])
                v.nop().then_inc(cx.s_reset, 1)

            @b.sync
            def _(sy):
                sy.wait_ge(*after)
                sy.nop().then_inc(cx.s_gather, 1)
                sy.wait_ge(cx.s_gather, 5 * k)
                for i in range(used["hw"]):
                    sy.sem_clear(cx.pools["hw"][i])
                sy.nop().then_inc(cx.s_reset, 1)

            @b.gpsimd
            def _(g):
                g.wait_ge(*after)
                g.nop().then_inc(cx.s_gather, 1)
                g.wait_ge(cx.s_gather, 5 * k)
                g.nop().then_inc(cx.s_reset, 1)
        for kind in cx.pools:
            cx.pool_i[kind] = 0
        return (cx.s_reset, 3 * k)
    cx.stage_boundary = stage_boundary

    _emit(cx)
    return nc


class Ctx:
    pass


def _tap(cx, name, src_ap, shape, dtype=F32):
    if name not in cx.taps:
        return
    t = cx.nc.dram_tensor("tap_" + name, shape, dtype, kind="ExternalOutput").ap()
    cx.tap_outs[name] = (t, src_ap)


def _finish(cx, done_sem, done_val):
    nc = cx.nc
    st = cx.sem_sw("tapst")
    with nc.Block("taps") as b:
        @b.gpsimd
        def _(g):
            g.wait_ge(done_sem, done_val)
            n = 0
            for name, (t, src) in cx.tap_outs.items():
                g.dma_start(out=t, in_=src).then_inc(st, 16)
                n += 1
            if n:
                g.wait_ge(st, 16 * n)


def fm(t, fc, a, b):
    return t[:, fc * 2048 + a: fc * 2048 + b]


def _emit(cx):
    nc, A, P, dr = cx.nc, cx.A, cx.P, cx.dr

    s_const = cx.sem_hw("const")
    s_constg = cx.sem_sw("constg")
    s_init = cx.sem("init2")
    with nc.Block("init") as b:
        @b.sync
        def _(sy):
            for src, dst in [("ones_fcol", "onesf"), ("ones_frow", "onesfr"),
                             ("aBq", "biasq"), ("aBk", "biask"), ("aBo", "biaso"),
                             ("fB1", "biasf1"), ("fB2", "biasf2"),
                             ("bS1", "sc1"), ("bS2", "sc2"), ("bBg1", "bg1")]:
                sy.dma_start(out=A[dst][:, :], in_=dr[src][:, :]).then_inc(s_const, 16)

        @b.gpsimd
        def _(g):
            for src, dst in [("ident_b", "identb"), ("ones_b", "onesb"),
                             ("ones_bcol", "onesbc")]:
                g.dma_start(out=A[dst][:, :], in_=dr[src][:, :]).then_inc(s_constg, 16)

        @b.vector
        def _(v):
            v.wait_ge(s_const, 10 * 16)
            v.wait_ge(s_constg, 3 * 16)
            ones_view = A["Vtok"][:, :].rearrange(
                "p (tc h e) -> p tc h e", tc=16, h=16, e=65)[:, :, :, 64:65]
            v.memset(ones_view, 1.0)
            v.memset(A["epsln"][:, :], 1e-5)
            v.memset(A["epsrms"][:, :], 1e-6)
            v.tensor_scalar_mul(A["biasq"][:, :], A["biasq"][:, :], 0.125)
            v.nop().then_inc(s_init, 1)

    s_ln = _norm(cx, "ln1", src_hbm=dr["xT"], src_sb=None, dst=A["xhat"],
                 kind="ln", wait=(s_init, 1))
    _tap(cx, "xhat", A["xhat"][:, :], [128, 16384])
    if cx.stop_after == "A":
        _finish(cx, s_ln, 1); return

    s_qkv = _qkv(cx, s_ln)
    _tap(cx, "Qfm", A["Qfm"][:, :], [128, 16384])
    _tap(cx, "Kfm", A["Kfm"][:, :], [128, 16384])
    _tap(cx, "Vtok", A["Vtok"][:, :], [128, 16640])
    if cx.stop_after == "B":
        _finish(cx, s_qkv, 1); return

    s_x1 = _attention(cx, s_qkv)
    _tap(cx, "x1", dr["x1d"][:, :], [1024, 2048])
    _tap(cx, "m_all", A["m_all"][:, :], [128, 256])
    _tap(cx, "sprs", A["sprs"][:, :], [128, 256])
    _tap(cx, "amask", A["amask"][:, :], [128, 256])
    _tap(cx, "cmask", A["cmask"][:, :], [128, 16384])
    if cx.stop_after == "D":
        _finish(cx, s_x1, 1); return

    rt1 = cx.stage_boundary((s_x1, 1))
    s_x2 = _ffn(cx, wait=rt1, src_hbm=dr["x1d"], tag="f1", out="cc")
    _tap(cx, "x2", dr["cc_in"][:, :, :], [8, 1024, 256])
    if cx.stop_after == "E":
        _finish(cx, s_x2, 1); return

    rt2 = cx.stage_boundary((s_x2, 1))
    s_cc = cx.sem_sw("cc")
    s_ccp = cx.sem_sw("ccp")
    with nc.Block("reshard") as b:
        @b.gpsimd
        def _(g):
            g.wait_ge(*rt2)
            g.collective_compute(
                "AllGather", ALU.bypass,
                ins=[dr["cc_in"][:, :, :]], outs=[dr["cc_out"][:, :, :, :]],
                replica_groups=[[0, 1, 2, 3, 4, 5, 6, 7]]).then_inc(s_cc, 1)
            g.wait_ge(s_cc, 1)
            for arm in g.switch_core_id(128):
                try:
                    lg = arm.logical % 8
                except Exception:
                    lg = arm % 8
                for bb in range(8):
                    g.dma_start(out=dr["x2mine"][bb, :, :],
                                in_=dr["cc_out"][bb, lg, :, :]
                                ).then_inc(s_ccp, 16)

    s_y2 = _norm(cx, "lny2", src_hbm=dr["x2mine"], src_sb=None, dst=A["xhat"],
                 kind="ln", wait=(s_ccp, 8 * 16), cc_src=True, y_out=dr["yd"])
    _tap(cx, "y2", dr["yd"][:, :], [1024, 2048])
    rt3 = cx.stage_boundary((s_y2, 1))
    s_rms1 = _norm(cx, "rms1", src_hbm=dr["yd"], src_sb=None, dst=A["xhat"],
                   kind="rms", wait=rt3, scale_col=A["sc1"])
    _tap(cx, "xg", A["xhat"][:, :], [128, 16384])
    if cx.stop_after == "G":
        _finish(cx, s_rms1, 1); return

    rt4 = cx.stage_boundary((s_rms1, 1))
    if cx.stop_after in ("H0G", "H0P", "H0O", "H0U"):
        upto = {"H0G": "G", "H0P": "P", "H0O": "O", "H0U": "U"}[cx.stop_after]
        fs, fv = _gla_half(cx, 0, rt4, upto=upto)
        _tap(cx, "cumA", A["cumA"][:, :], [128, 8192])
        _tap(cx, "cumR", A["cumR"][:, :], [128, 8192])
        _tap(cx, "sq2", A["sq2"][:, :], [128, 8192])
        _tap(cx, "sk2", A["sk2"][:, :], [128, 8192])
        _tap(cx, "sv2", A["sv2"][:, :], [128, 8192])
        _tap(cx, "og2", A["og2"][:, :], [128, 8192])
        _tap(cx, "Pm", A["Pm"][:, :], [16, 4608])
        _tap(cx, "so2", A["so2"][:, :], [128, 8192])
        _tap(cx, "u2", A["u2"][:, :], [128, 8192])
        _finish(cx, fs, fv)
        return
    sA = _gla_half(cx, 0, rt4)
    if cx.stop_after == "H0":
        _tap(cx, "z1h", dr["z1d"][:, :].rearrange(
            "p (t c) -> p t c", t=8)[:, :, 0:128], [1024, 8, 128])
        _finish(cx, *sA)
        return
    rt5 = cx.stage_boundary(sA)
    sB = _gla_half(cx, 1, rt5)
    _tap(cx, "z1", dr["z1d"][:, :], [1024, 2048])
    if cx.stop_after == "H":
        _finish(cx, *sB)
        return

    rt6 = cx.stage_boundary(sB)
    s_n2 = _norm(cx, "rms2", src_hbm=dr["z1d"], src_sb=None, dst=A["nrm2"],
                 kind="rms", wait=rt6, scale_col=A["sc2"], rots=("gr0", "gr1"))
    s_glu = _glu(cx, (s_n2, 1))
    _tap(cx, "x4", dr["x4d"][:, :], [1024, 2048])
    if cx.stop_after == "I":
        _finish(cx, s_glu, 1)
        return

    rt7 = cx.stage_boundary((s_glu, 1))
    s_out = _ffn(cx, wait=rt7, src_hbm=dr["x4d"], tag="f2", out="yT")
    _finish(cx, s_out, 1)


def _norm(cx, name, src_hbm, src_sb, dst, kind, wait, scale_col=None,
          cc_src=False, y_out=None, rots=("lnr0", "lnr1")):
    nc, A, P = cx.nc, cx.A, cx.P
    rows = A["rrep"]
    rowsb = A["mrep"][:, :].bitcast(BF16)
    s_ld0 = cx.sem_hw(name + "ld0")
    s_ld1 = cx.sem_hw(name + "ld1")
    s_ld = [s_ld0, s_ld1]
    s_sq = cx.sem(name + "sq")
    s_xb = cx.sem(name + "xb")
    s_mm = cx.sem(name + "mm")
    s_mmf = cx.sem(name + "mmf")
    s_rw = cx.sem(name + "rw")
    s_done = cx.sem(name + "dn")
    xr0 = A[rots[0]][:, :]
    xr1 = A[rots[1]][:, :]

    def emit_load(sy, fc, nt, xr, ld_sem):
        if not cc_src:
            sy.dma_start(out=xr[:, nt * 512:(nt + 1) * 512],
                         in_=src_hbm[fc * 128:(fc + 1) * 128,
                                     nt * 512:(nt + 1) * 512]
                         ).then_inc(ld_sem, 16)
        else:
            for d2 in range(2):
                b = nt * 2 + d2
                sy.dma_start(out=xr[:, nt * 512 + d2 * 256:
                                    nt * 512 + (d2 + 1) * 256],
                             in_=src_hbm[b, fc * 128:(fc + 1) * 128, :]
                             ).then_inc(ld_sem, 16)
    ldn = 2 if cc_src else 1

    def src_tile(fc, nt):
        if src_sb is not None:
            return fm(src_sb, fc, nt * 512, (nt + 1) * 512)
        xr = xr0 if fc % 2 == 0 else xr1
        return xr[:, nt * 512:(nt + 1) * 512]

    with nc.Block(name) as b:
        if src_hbm is not None:
            @b.sync
            def _(sy):
                sy.wait_ge(*wait)
                for i, (nt, fc) in enumerate([(nt, fc) for nt in range(4)
                                              for fc in range(8)]):
                    if i >= 2:
                        sy.wait_ge(s_sq, i - 1)
                        sy.wait_ge(s_xb, i - 1)
                    xr = xr0 if fc % 2 == 0 else xr1
                    emit_load(sy, fc, nt, xr, s_ld[i % 2])

        @b.scalar
        def _(s):
            if src_hbm is None:
                s.wait_ge(*wait)
            for i, (nt, fc) in enumerate([(nt, fc) for nt in range(4)
                                          for fc in range(8)]):
                if src_hbm is not None:
                    s.wait_ge(s_ld[i % 2], 16 * ldn * (i // 2 + 1))
                if i >= 2:
                    s.wait_ge(s_mmf, i - 1)
                sq = A["tmpA"] if fc % 2 == 0 else A["tmpB"]
                s.activation(sq[:, :], src_tile(fc, nt), AF.Square).then_inc(s_sq, 1)

        @b.vector
        def _(v):
            if src_hbm is None:
                v.wait_ge(*wait)
            for i, (nt, fc) in enumerate([(nt, fc) for nt in range(4)
                                          for fc in range(8)]):
                if src_hbm is not None:
                    v.wait_ge(s_ld[i % 2], 16 * ldn * (i // 2 + 1))
                if i >= 2:
                    v.wait_ge(s_mmf, i - 1)
                xb = A["tmpC"][:, :] if fc % 2 == 0 else A["tmpD"][:, :]
                v.tensor_copy(xb, src_tile(fc, nt)).then_inc(s_xb, 1)

        @b.tensor
        def _(t):
            for nt in range(4):
                psA = [P[6], P[4], P[2], P[0]][nt]
                psB = [P[7], P[5], P[3], P[1]][nt]
                for fc in range(8):
                    i = nt * 8 + fc
                    t.wait_ge(s_sq, i + 1)
                    t.wait_ge(s_xb, i + 1)
                    xb = A["tmpC"][:, :] if fc % 2 == 0 else A["tmpD"][:, :]
                    sq = A["tmpA"] if fc % 2 == 0 else A["tmpB"]
                    if kind == "ln":
                        t.matmul(psA[0:1, :], A["onesbc"][:, :], xb,
                                 start=(fc == 0), stop=(fc == 7))
                    t.matmul(psB[0:1, :], A["onesbc"][:, :], sq[:, :],
                             start=(fc == 0), stop=(fc == 7)).then_inc(s_mmf, 1)

        tF, tG = A["tmpF"], A["tmpG"]
        @b.vector
        def _(v):
            for nt in range(4):
                v.wait_ge(s_mmf, 8 * (nt + 1))
                psA = [P[6], P[4], P[2], P[0]][nt]
                sl = slice(nt * 512, (nt + 1) * 512)
                if kind == "ln":
                    v.tensor_scalar_mul(rows[0:1, sl], psA[0:1, :], 1.0 / D)
            if kind == "ln":
                v.drain()
                for nt in range(4):
                    sl = slice(nt * 512, (nt + 1) * 512)
                    v.tensor_mul(tF[0:1, sl], rows[0:1, sl], rows[0:1, sl])
                v.drain()
                for nt in range(4):
                    psB = [P[7], P[5], P[3], P[1]][nt]
                    sl = slice(nt * 512, (nt + 1) * 512)
                    ins = v.scalar_tensor_tensor(tG[0:1, sl], psB[0:1, :], 1.0 / D,
                                                 tF[0:1, sl], ALU.mult, ALU.subtract)
                    if nt == 3:
                        ins.then_inc(s_rw, 14)
            else:
                v.nop().then_inc(s_rw, 14)

        @b.scalar
        def _(s):
            s.wait_ge(s_rw, 14)
            if kind == "ln":
                for nt in range(4):
                    sl = slice(nt * 512, (nt + 1) * 512)
                    s.activation(tF[0:1, sl], tG[0:1, sl], AF.Ln,
                                 bias=A["epsln"][0:1, 0:1])
                s.drain()
                for nt in range(4):
                    sl = slice(nt * 512, (nt + 1) * 512)
                    ins = s.activation(tG[0:1, sl], tF[0:1, sl], AF.Exp, scale=-0.5)
                    if nt == 3:
                        ins.then_inc(s_rw, 100)
            else:
                for nt in range(4):
                    psB = [P[7], P[5], P[3], P[1]][nt]
                    sl = slice(nt * 512, (nt + 1) * 512)
                    s.activation(tF[0:1, sl], psB[0:1, :], AF.Sqrt, scale=1.0 / D)
                s.drain()
                for nt in range(4):
                    sl = slice(nt * 512, (nt + 1) * 512)
                    s.activation(tG[0:1, sl], tF[0:1, sl], AF.Ln,
                                 bias=A["epsrms"][0:1, 0:1])
                s.drain()
                for nt in range(4):
                    sl = slice(nt * 512, (nt + 1) * 512)
                    ins = s.activation(tF[0:1, sl], tG[0:1, sl], AF.Exp, scale=-1.0)
                    if nt == 3:
                        ins.then_inc(s_rw, 100)

        @b.vector
        def _(v):
            v.wait_ge(s_rw, 114)
            rstd_row = tG if kind == "ln" else tF
            last = None
            for nt in range(4):
                sl = slice(nt * 512, (nt + 1) * 512)
                sl2 = slice(2048 + nt * 512, 2048 + (nt + 1) * 512)
                last = v.tensor_copy(rowsb[0:1, sl2], rstd_row[0:1, sl])
            if kind == "ln":
                for nt in range(4):
                    sl = slice(nt * 512, (nt + 1) * 512)
                    last = v.tensor_mul(rowsb[0:1, sl], rows[0:1, sl],
                                        rstd_row[0:1, sl])
            last.then_inc(s_rw, 100)

        s_rp = cx.sem(name + "rp")
        s_evn = cx.sem(name + "evn")
        @b.tensor
        def _(t):
            t.wait_ge(s_rw, 214)
            for nt in range(4):
                ins = t.matmul(P[nt][:, :], A["onesb"][0:1, 0:128],
                               rowsb[0:1, 2048 + nt * 512:2048 + (nt + 1) * 512],
                               start=True, stop=True)
                if kind == "ln":
                    ins = t.matmul(P[4 + nt][:, :], A["onesb"][0:1, 0:128],
                                   rowsb[0:1, nt * 512:(nt + 1) * 512],
                                   start=True, stop=True)
                ins.then_inc(s_rp, 1)

        @b.vector
        def _(v):
            v.wait_ge(s_rp, 4)
            for nt in range(4):
                sl = slice(nt * 512, (nt + 1) * 512)
                last = v.tensor_copy(A["rrep"][:, sl], P[nt][:, :])
                if kind == "ln":
                    last = v.tensor_copy(A["mrep"][:, sl], P[4 + nt][:, :])
                last.then_inc(s_evn, 1)

    s_ld2a = cx.sem_hw(name + "ld2a")
    s_ld2b = cx.sem_hw(name + "ld2b")
    s_ld2 = [s_ld2a, s_ld2b]
    s_ap = cx.sem(name + "ap")
    s_yo = [cx.sem_hw(name + "yoA"), cx.sem_hw(name + "yoB")]
    with nc.Block(name + "_apply") as b:
        if src_hbm is not None:
            @b.sync
            def _(sy):
                sy.wait_ge(s_evn, 4)
                for i, (fc, nt) in enumerate([(fc, nt) for fc in range(8)
                                              for nt in range(4)]):
                    if i >= 2:
                        sy.wait_ge(s_ap, 2 * (i - 1))
                    xr = xr0 if fc % 2 == 0 else xr1
                    emit_load(sy, fc, nt, xr, s_ld2[i % 2])
                    if y_out is not None and i >= 1:
                        k = i - 1
                        fck, ntk = divmod(k, 4)
                        sy.wait_ge(s_ap, 2 * (k + 1))
                        if k >= 2:
                            sy.wait_ge(s_yo[k % 2], 16 * (k // 2))
                        sy.dma_start(out=y_out[fck * 128:(fck + 1) * 128,
                                               ntk * 512:(ntk + 1) * 512],
                                     in_=A["ytmp"][:, (k % 4) * 512:
                                                   (k % 4) * 512 + 512]
                                     ).then_inc(s_yo[k % 2], 16)
                if y_out is not None:
                    k = 31
                    sy.wait_ge(s_ap, 2 * (k + 1))
                    sy.wait_ge(s_yo[k % 2], 16 * (k // 2))
                    sy.dma_start(out=y_out[(k // 4) * 128:(k // 4 + 1) * 128,
                                           (k % 4) * 512:(k % 4 + 1) * 512],
                                 in_=A["ytmp"][:, (k % 4) * 512:
                                               (k % 4) * 512 + 512]
                                 ).then_inc(s_yo[k % 2], 16)

        @b.vector
        def _(v):
            v.wait_ge(s_evn, 4)
            for i, (fc, nt) in enumerate([(fc, nt) for fc in range(8)
                                          for nt in range(4)]):
                if src_hbm is not None:
                    v.wait_ge(s_ld2[i % 2], 16 * ldn * (i // 2 + 1))
                sl = slice(nt * 512, (nt + 1) * 512)
                xt = src_tile(fc, nt)
                dtile = fm(dst, fc, nt * 512, (nt + 1) * 512)
                tsl = slice((i % 4) * 512, (i % 4) * 512 + 512)
                ttile = A["tmpF"][:, tsl]
                if y_out is not None:
                    if i >= 4:
                        v.wait_ge(s_yo[i % 2], 16 * (i // 2 - 1))
                    yt = A["ytmp"][:, (i % 4) * 512:(i % 4) * 512 + 512]
                    v.tensor_mul(ttile, xt, A["rrep"][:, sl])
                    v.drain()
                    v.tensor_sub(yt, ttile, A["mrep"][:, sl])
                    v.drain()
                    v.tensor_copy(dtile, yt).then_inc(s_ap, 2)
                elif kind == "ln":
                    v.tensor_mul(ttile, xt, A["rrep"][:, sl])
                    v.drain()
                    v.tensor_sub(dtile, ttile, A["mrep"][:, sl]).then_inc(s_ap, 2)
                else:
                    v.tensor_mul(ttile, xt, A["rrep"][:, sl])
                    v.drain()
                    v.tensor_scalar(dtile, ttile,
                                    scale_col[:, fc:fc + 1], None,
                                    ALU.mult).then_inc(s_ap, 2)

    s_fin = cx.sem(name + "fin")
    with nc.Block(name + "_fin") as b:
        @b.vector
        def _(v):
            v.wait_ge(s_ap, 64)
            if y_out is not None:
                v.wait_ge(s_yo[0], 16 * 16)
                v.wait_ge(s_yo[1], 16 * 16)
            v.nop().then_inc(s_fin, 1)
    return s_fin


def _qkv(cx, s_ln):
    nc, A, P, dr = cx.nc, cx.A, cx.P, cx.dr
    vbias = A["PT0"][0:1, 0:1024]
    s_wlA = cx.sem_sw("qkvwlA")
    s_wlB = cx.sem_sw("qkvwlB")
    s_wlC = cx.sem_sw("qkvwlC")
    s_mm = cx.sem("qkvmm")
    s_evQK = cx.sem("qkvevqk")
    s_evV = cx.sem("qkvevv")
    s_done = cx.sem("qkvdn")

    with nc.Block("qkv") as b:
        @b.gpsimd
        def _(g):
            g.wait_ge(s_ln, 1)
            g.dma_start(out=A["wbA"][:, :].rearrange("p (c n) -> p c n", c=8),
                        in_=dr["aWq"][:, :].rearrange("(c p) n -> p c n", p=128)
                        ).then_inc(s_wlA, 16)
            g.dma_start(out=A["wbB2"][:, :].rearrange("p (c n) -> p c n", c=8),
                        in_=dr["aWk"][:, :].rearrange("(c p) n -> p c n", p=128)
                        ).then_inc(s_wlB, 16)
            g.wait_ge(s_evQK, 32)
            g.dma_start(out=A["wbA"][:, :].rearrange("p (c n) -> p c n", c=8),
                        in_=dr["aWvb"][0:1024, :].rearrange("(c p) n -> p c n", p=128)
                        ).then_inc(s_wlA, 16)
            g.dma_start(out=vbias, in_=dr["aWvb"][1024:1025, :]
                        ).then_inc(s_wlC, 16)

        @b.tensor
        def _(t):
            for wi, (wbuf, wsem) in enumerate([(A["wbA"], s_wlA), (A["wbB2"], s_wlB)]):
                t.wait_ge(wsem, 16)
                for oc in range(8):
                    for nt in range(4):
                        idx = wi * 32 + oc * 4 + nt
                        ps = P[idx % 4]
                        if idx >= 4:
                            t.wait_ge(s_evQK, idx - 3)
                        for kc in range(8):
                            ins = t.matmul(ps[:, :],
                                     wbuf[:, kc * 1024 + oc * 128:kc * 1024 + oc * 128 + 128],
                                     fm(A["xhat"], kc, nt * 512, (nt + 1) * 512),
                                     start=(kc == 0), stop=(kc == 7))
                        ins.then_inc(s_mm, 1)
            t.wait_ge(s_wlA, 32)
            t.wait_ge(s_wlC, 16)
            for tc in range(16):
                for half in range(2):
                    vi = tc * 2 + half
                    ps = P[vi % 4]
                    if vi < 4:
                        t.wait_ge(s_evQK, 61 + vi)
                    else:
                        t.wait_ge(s_evV, vi - 3)
                    for kc in range(8):
                        t.matmul(ps[:, :],
                                 fm(A["xhat"], kc, tc * 128, (tc + 1) * 128),
                                 A["wbA"][:, kc * 1024 + half * 512:
                                          kc * 1024 + half * 512 + 512],
                                 start=(kc == 0), stop=False)
                    t.matmul(ps[:, :], A["onesb"][0:1, 0:128],
                             vbias[0:1, half * 512:(half + 1) * 512],
                             start=False, stop=True).then_inc(s_mm, 1)

        @b.scalar
        def _(s):
            for wi, (dstt, biast, scl) in enumerate(
                    [(A["Qfm"], A["biasq"], 0.125), (A["Kfm"], A["biask"], 1.0)]):
                for oc in range(8):
                    for nt in range(4):
                        idx = wi * 32 + oc * 4 + nt
                        s.wait_ge(s_mm, idx + 1)
                        ps = P[idx % 4]
                        s.activation(fm(dstt, oc, nt * 512, (nt + 1) * 512),
                                     ps[:, :], AF.Identity,
                                     bias=biast[:, oc:oc + 1], scale=scl
                                     ).then_inc(s_evQK, 1)

        @b.vector
        def _(v):
            for tc in range(16):
                for half in range(2):
                    idx = 64 + tc * 2 + half
                    v.wait_ge(s_mm, idx + 1)
                    ps = P[idx % 4]
                    dst = A["Vtok"][:, :].rearrange(
                        "p (tcc h e) -> p tcc h e", tcc=16, h=16, e=65)[
                        :, tc, half * 8:(half + 1) * 8, 0:64]
                    v.tensor_copy(dst, ps[:, :].rearrange("p (h e) -> p h e", h=8)
                                  ).then_inc(s_evV, 1)
            v.wait_ge(s_evQK, 64)
            v.wait_ge(s_evV, 32)
            v.nop().then_inc(s_done, 1)
    return s_done


def _attention(cx, s_qkv):
    nc, A, P, dr = cx.nc, cx.A, cx.P, cx.dr
    sem = cx.sem
    s_red = sem("ared")
    s_p1t = sem("ap1t")
    s_mean = sem("amean")
    s_mev = sem("amev")
    s_spr = sem("aspr")
    s_cv1 = sem("acv1"); s_ct1 = sem("act1")
    s_cv2 = sem("acv2"); s_ct2 = sem("act2"); s_cv3 = sem("acv3")
    s_p2t = sem("ap2t")
    s_exp = sem("aexp")
    s_avt = sem("aavt")
    s_avq = sem("aavq")
    s_ave = sem("aave")
    s_kb = sem("akb")
    s_bv = sem("abv"); s_bt1 = sem("abt1"); s_bv2 = sem("abv2")
    s_bt2 = sem("abt2"); s_bpred = sem("abpred")
    s_msk = sem("amsk")
    s_tev = sem("atev")
    s_ttr = sem("attr")
    s_wo = cx.sem_sw("awo")
    s_xr = [cx.sem_hw("axr0"), cx.sem_hw("axr1")]
    s_womm = sem("awomm")
    s_woev = sem("awoev")
    s_fin = sem("afin")
    NB = 36

    Qf, Kf, Vt = A["Qfm"], A["Kfm"], A["Vtok"]
    PT = [A["PT0"], A["PT1"]]
    gt = A["gtmp"]
    bx = A["bisx"]

    def qslice(h, a, b):
        return Qf[(h % 2) * 64:(h % 2) * 64 + 64,
                  (h // 2) * 2048 + a:(h // 2) * 2048 + b]

    def kslice(h, a, b):
        return Kf[(h % 2) * 64:(h % 2) * 64 + 64,
                  (h // 2) * 2048 + a:(h // 2) * 2048 + b]

    with nc.Block("attn") as b:
        @b.gpsimd
        def _(g):
            g.wait_ge(s_qkv, 1)
            g.dma_start(out=A["wbX"][:, :].rearrange("p (c n) -> p c n", c=8),
                        in_=dr["aWo"][:, :].rearrange("(c p) n -> p c n", p=128)
                        ).then_inc(s_wo, 16)

        s_wod = [cx.sem_hw("awod0"), cx.sem_hw("awod1"), cx.sem_hw("awod2"), cx.sem_hw("awod3")]
        @b.sync
        def _(sy):
            for i in range(33):
                if i < 32:
                    oc, nt = divmod(i, 4)
                    if i >= 2:
                        sy.wait_ge(s_woev, i - 1)
                    xr = A["worot0"] if i % 2 == 0 else A["worot1"]
                    sy.dma_start(out=xr[:, :],
                                 in_=dr["xT"][oc * 128:(oc + 1) * 128,
                                              nt * 512:(nt + 1) * 512]
                                 ).then_inc(s_xr[i % 2], 16)
                if i >= 1:
                    j = i - 1
                    oc2, nt2 = divmod(j, 4)
                    sy.wait_ge(s_woev, j + 1)
                    sy.dma_start(out=dr["x1d"][oc2 * 128:(oc2 + 1) * 128,
                                               nt2 * 512:(nt2 + 1) * 512],
                                 in_=A["ytmp"][:, (j % 4) * 512:
                                               (j % 4) * 512 + 512]
                                 ).then_inc(s_wod[j % 4], 16)
            for q in range(4):
                sy.wait_ge(s_wod[q], 8 * 16)
            sy.nop().then_inc(s_fin, 1)

        @b.tensor
        def _(t):
            t.wait_ge(s_qkv, 1)

            def pass2av_qt(h, qt):
                gq = h * 8 + qt
                for j in range(9):
                    gp = h * 64 + qt * 8 + j
                    if j < 8:
                        if gp >= 1:
                            t.wait_ge(s_exp, gp)
                        t.matmul(P[6][:, 0:256], kslice(h, 2 * j * 128,
                                                        2 * j * 128 + 128),
                                 qslice(h, qt * 256, (qt + 1) * 256),
                                 start=True, stop=True)
                        t.matmul(P[6][:, 256:512],
                                 kslice(h, (2 * j + 1) * 128,
                                        (2 * j + 1) * 128 + 128),
                                 qslice(h, qt * 256, (qt + 1) * 256),
                                 start=False, stop=True,
                                 skip_group_check=True).then_inc(s_p2t, 1)
                    if j >= 1:
                        jj = j - 1
                        t.wait_ge(s_exp, h * 64 + qt * 8 + jj + 1)
                        if jj == 0 and gq >= 1:
                            t.wait_ge(s_ave, gq)
                        ins = None
                        for kk2 in range(2):
                            kk = 2 * jj + kk2
                            for qc2 in range(2):
                                ins = t.matmul(
                                    P[7][:, qc2 * 256:qc2 * 256 + 65],
                                    PT[qt % 2][:, kk * 256 + qc2 * 128:
                                                kk * 256 + qc2 * 128 + 128],
                                    Vt[:, kk * 1040 + h * 65:
                                       kk * 1040 + h * 65 + 65],
                                    start=(kk == 0 and qc2 == 0), stop=True,
                                    skip_group_check=True)
                        if jj == 7:
                            ins.then_inc(s_avq, 1)
                        else:
                            ins.then_inc(s_avt, 1)

            def bis_unit(k):
                t.wait_ge(s_bv, k + 1)
                t.matmul(P[3][0:1, 0:16], A["onesf"][:, :], bx[:, 96:112],
                         start=True, stop=True).then_inc(s_bt1, 1)
                t.wait_ge(s_bv2, k + 1)
                t.matmul(P[3][:, 16:32], A["onesfr"][0:1, :],
                         A["cntrow"][0:1, :],
                         start=True, stop=True).then_inc(s_bt2, 1)

            def cchain_t(h):
                t.wait_ge(s_cv1, h + 1)
                t.matmul(P[3][0:1, 0:128], A["cbcol"][:, 0:1],
                         A["identb"][:, :], start=True, stop=True
                         ).then_inc(s_ct1, 1)
                t.wait_ge(s_cv2, h + 1)
                t.matmul(P[3][:, 128:129], A["onesb"][0:1, :],
                         A["c1b"][0:1, 0:1], start=True, stop=True
                         ).then_inc(s_ct2, 1)

            bis_done = 0
            for h in range(16):
                for qc in range(16):
                    if qc == 4 and h >= 1:
                        cchain_t(h - 1)
                    ti0 = (h * 16 + qc) * 4
                    for nt in range(4):
                        ti = ti0 + nt
                        if ti >= 3:
                            t.wait_ge(s_red, ti - 2)
                        t.matmul(P[ti % 3][:, :], qslice(h, qc * 128,
                                                         (qc + 1) * 128),
                                 kslice(h, nt * 512, (nt + 1) * 512),
                                 start=True, stop=True).then_inc(s_p1t, 1)
                    gqc = h * 16 + qc
                    if gqc >= 2:
                        t.wait_ge(s_mev, gqc - 1)
                    if h == 0 and qc == 0:
                        t.wait_ge(s_kb, 1)
                    t.matmul(P[4 + gqc % 2][:, 0:1], qslice(h, qc * 128,
                                                            (qc + 1) * 128),
                             A["kbarb"][(h % 2) * 64:(h % 2) * 64 + 64,
                                        h:h + 1],
                             start=True, stop=True).then_inc(s_mean, 1)
                    if h >= 1 and 6 <= qc <= 13:
                        pass2av_qt(h - 1, qc - 6)
            cchain_t(15)
            for qt in range(8):
                pass2av_qt(15, qt)
            while bis_done < NB:
                bis_unit(bis_done); bis_done += 1

            for tc in range(16):
                t.wait_ge(s_msk, 16)
                for gjj in range(2):
                    gi = tc * 2 + gjj
                    if gi >= 2:
                        t.wait_ge(s_tev, gi - 1)
                    ins = None
                    for j4 in range(4):
                        j = gjj * 4 + j4
                        ins = t.matmul(P[6 + gi % 2][:, j4 * 128:(j4 + 1) * 128],
                                       A["cmask"][:, tc * 1024 + j * 128:
                                                  tc * 1024 + j * 128 + 128],
                                       A["identb"][:, :],
                                       start=(j4 == 0), stop=True,
                                       skip_group_check=True)
                    ins.then_inc(s_ttr, 1)

            t.wait_ge(s_wo, 16)
            for i in range(32):
                oc, nt = divmod(i, 4)
                ps = P[i % 3]
                if i >= 3:
                    t.wait_ge(s_woev, i - 2)
                ins = None
                for kc in range(8):
                    ins = t.matmul(ps[:, :],
                                   A["wbX"][:, kc * 1024 + oc * 128:
                                            kc * 1024 + oc * 128 + 128],
                                   A["cfm"][:, kc * 2048 + nt * 512:
                                            kc * 2048 + (nt + 1) * 512],
                                   start=(kc == 0), stop=(kc == 7))
                ins.then_inc(s_womm, 1)

        @b.scalar
        def _(s):
            for h in range(16):
                s.wait_ge(s_cv3, h + 1)
                for qt in range(8):
                    for j in range(8):
                        gp = h * 64 + qt * 8 + j
                        s.wait_ge(s_p2t, gp + 1)
                        if h * 8 + qt >= 2 and j == 0:
                            s.wait_ge(s_avt, 7 * (h * 8 + qt - 1))
                            s.wait_ge(s_avq, h * 8 + qt - 1)
                        s.activation(PT[qt % 2][:, j * 512:(j + 1) * 512],
                                     P[6][:, :], AF.Exp,
                                     bias=A["negC"][:, h:h + 1]
                                     ).then_inc(s_exp, 1)

        @b.vector
        def _(v):
            v.wait_ge(s_qkv, 1)
            v.memset(A["kbar"][:, :], 0.0)
            v.drain()
            for h in range(16):
                r0 = (h % 2) * 64
                v.reduce_sum(A["kbar"][r0:r0 + 64, h:h + 1],
                             kslice(h, 0, 2048), axis=AX.X)
            v.drain()
            ins = v.tensor_copy(A["kbarb"][:, :], A["kbar"][:, :])
            ins.then_inc(s_kb, 1)


            def do_stage2(gqc):
                p = gqc % 3
                v.tensor_max(gt[:, 12 + 2 * p:13 + 2 * p],
                             gt[:, 4 * p + 0:4 * p + 1],
                             gt[:, 4 * p + 1:4 * p + 2])
                v.tensor_max(gt[:, 13 + 2 * p:14 + 2 * p],
                             gt[:, 4 * p + 2:4 * p + 3],
                             gt[:, 4 * p + 3:4 * p + 4])

            def do_stage3(gqc):
                p = gqc % 3
                v.tensor_max(A["m_all"][:, gqc:gqc + 1],
                             gt[:, 12 + 2 * p:13 + 2 * p],
                             gt[:, 13 + 2 * p:14 + 2 * p])

            def do_stage4(gqc):
                v.scalar_tensor_tensor(A["sprs"][:, gqc:gqc + 1],
                                       A["meancols"][:, gqc:gqc + 1],
                                       -1.0 / 2048,
                                       A["m_all"][:, gqc:gqc + 1],
                                       ALU.mult, ALU.add).then_inc(s_spr, 1)

            def do_mev(gqc):
                v.wait_ge(s_mean, gqc + 1)
                v.tensor_copy(A["meancols"][:, gqc:gqc + 1],
                              P[4 + gqc % 2][:, 0:1]).then_inc(s_mev, 1)

            def do_cchain(h):
                v.reduce_max(gt[:, 18:19], A["m_all"][:, h * 16:(h + 1) * 16],
                             axis=AX.X)
                v.drain()
                v.tensor_copy(A["cbcol"][:, 0:1], gt[:, 18:19]).then_inc(s_cv1, 1)
                v.wait_ge(s_ct1, h + 1)
                v.reduce_max(A["c16"][0:1, 0:1], P[3][0:1, 0:128], axis=AX.X)
                v.drain()
                v.tensor_copy(A["c1b"][0:1, 0:1], A["c16"][0:1, 0:1]
                              ).then_inc(s_cv2, 1)
                v.wait_ge(s_ct2, h + 1)
                v.tensor_scalar_mul(A["negC"][:, h:h + 1], P[3][:, 128:129],
                                    -1.0).then_inc(s_cv3, 1)

            def do_epi(h, qt):
                gq = h * 8 + qt
                v.wait_ge(s_avq, gq + 1)
                p = gq % 2
                v.reciprocal(gt[:, 20 + 2 * p:21 + 2 * p], P[7][:, 64:65])
                v.reciprocal(gt[:, 21 + 2 * p:22 + 2 * p], P[7][:, 320:321])
                v.drain()
                v.tensor_scalar(A["cct"][:, (qt * 2) * 1024 + h * 64:
                                         (qt * 2) * 1024 + h * 64 + 64],
                                P[7][:, 0:64], gt[:, 20 + 2 * p:21 + 2 * p],
                                None, ALU.mult)
                v.tensor_scalar(A["cct"][:, (qt * 2 + 1) * 1024 + h * 64:
                                         (qt * 2 + 1) * 1024 + h * 64 + 64],
                                P[7][:, 256:320],
                                gt[:, 21 + 2 * p:22 + 2 * p],
                                None, ALU.mult).then_inc(s_ave, 1)

            for r in range(261):
                if r >= 255:
                    v.drain()
                if r >= 23 and (r - 7) % 16 <= 7 and 1 <= (r - 7) // 16 <= 15:
                    do_epi((r - 7) // 16 - 1, (r - 7) % 16)
                if 1 <= r and r - 1 < 256:
                    do_mev(r - 1)
                if 4 <= r and r - 4 < 256:
                    do_stage4(r - 4)
                if 3 <= r and r - 3 < 256:
                    do_stage3(r - 3)
                if 2 <= r and r - 2 < 256:
                    do_stage2(r - 2)
                if r < 256:
                    p = r % 3
                    for nt in range(4):
                        ti = r * 4 + nt
                        v.wait_ge(s_p1t, ti + 1)
                        v.reduce_max(gt[:, 4 * p + nt:4 * p + nt + 1],
                                     P[ti % 3][:, :], axis=AX.X
                                     ).then_inc(s_red, 1)
                if r >= 19 and (r - 19) % 16 == 0:
                    do_cchain((r - 19) // 16)

            for qt in range(8):
                do_epi(15, qt)

            v.memset(bx[:, 0:16], 0.0)
            v.memset(bx[:, 16:32], 1024.0)
            v.drain()
            for k in range(NB):
                lo = bx[:, 32 * (k % 2):32 * (k % 2) + 16]
                hi = bx[:, 32 * (k % 2) + 16:32 * (k % 2) + 32]
                lon = bx[:, 32 * ((k + 1) % 2):32 * ((k + 1) % 2) + 16]
                hin = bx[:, 32 * ((k + 1) % 2) + 16:32 * ((k + 1) % 2) + 32]
                mid = bx[:, 64:80]; pred = bx[:, 80:96]
                t1 = bx[:, 96:112]; u = bx[:, 112:128]
                v.tensor_add(t1, lo, hi)
                v.drain()
                v.tensor_scalar_mul(mid, t1, 0.5)
                v.drain()
                sp3 = A["sprs"][:, :].rearrange("p (hh q) -> p hh q", hh=16)
                v.tensor_tensor(A["amask"][:, :].rearrange(
                    "p (hh q) -> p hh q", hh=16), sp3,
                    mid.unsqueeze(2).broadcast_to([128, 16, 16]), ALU.is_ge)
                v.drain()
                v.reduce_sum(bx[:, 96:112].unsqueeze(2),
                             A["amask"][:, :].rearrange(
                                 "p (hh q) -> p hh q", hh=16),
                             axis=AX.X).then_inc(s_bv, 1)
                v.wait_ge(s_bt1, k + 1)
                v.tensor_copy(A["cntrow"][0:1, :], P[3][0:1, 0:16]
                              ).then_inc(s_bv2, 1)
                v.wait_ge(s_bt2, k + 1)
                v.tensor_single_scalar(pred, P[3][:, 16:32], float(U), ALU.is_ge)
                v.drain()
                v.tensor_sub(t1, mid, lo)
                v.tensor_sub(u, mid, hi)
                v.drain()
                v.tensor_mul(bx[:, 128:144], pred, t1)
                v.tensor_mul(bx[:, 144:160], pred, u)
                v.drain()
                v.tensor_add(lon, lo, bx[:, 128:144])
                v.tensor_sub(hin, mid, bx[:, 144:160])
                v.drain()
            lof = bx[:, 32 * (NB % 2):32 * (NB % 2) + 16]
            sp3 = A["sprs"][:, :].rearrange("p (hh q) -> p hh q", hh=16)
            v.tensor_tensor(A["amask"][:, :].rearrange(
                "p (hh q) -> p hh q", hh=16), sp3,
                lof.unsqueeze(2).broadcast_to([128, 16, 16]), ALU.is_ge)
            v.drain()

            for tc in range(16):
                if tc == 0:
                    v.wait_ge(s_ave, 128)
                msk3 = A["amask"][:, :].rearrange(
                    "p (hh q) -> p hh q", hh=16)[:, :, tc:tc + 1]
                v.tensor_tensor(
                    A["cmask"][:, tc * 1024:(tc + 1) * 1024].rearrange(
                        "p (hh e) -> p hh e", hh=16),
                    A["cct"][:, tc * 1024:(tc + 1) * 1024].rearrange(
                        "p (hh e) -> p hh e", hh=16),
                    msk3.broadcast_to([128, 16, 64]), ALU.mult
                ).then_inc(s_msk, 1)

            for gi in range(32):
                tc, gjj = divmod(gi, 2)
                v.wait_ge(s_ttr, gi + 1)
                dst = A["cfm"][:, :].rearrange(
                    "p (c tt) -> p c tt", c=8)[:, gjj * 4:(gjj + 1) * 4,
                                               tc * 128:(tc + 1) * 128]
                v.tensor_copy(dst, P[6 + gi % 2][:, :].rearrange(
                    "p (j e) -> p j e", j=4)).then_inc(s_tev, 1)

            for i in range(32):
                oc, nt = divmod(i, 4)
                v.wait_ge(s_womm, i + 1)
                v.wait_ge(s_xr[i % 2], 16 * (i // 2 + 1))
                if i >= 4:
                    v.wait_ge(s_wod[i % 4], 16 * (i // 4))
                ps = P[i % 3]
                xr = A["worot0"] if i % 2 == 0 else A["worot1"]
                v.scalar_tensor_tensor(A["ytmp"][:, (i % 4) * 512:
                                                 (i % 4) * 512 + 512],
                                       ps[:, :], A["biaso"][:, oc:oc + 1],
                                       xr[:, :], ALU.add, ALU.add
                                       ).then_inc(s_woev, 1)
    return s_fin


GLA_PAIRS = [(t, sp) for t in range(8) for sp in range(t + 1)]


def _gla_half(cx, half, wait, upto=None):
    nc, A, P, dr = cx.nc, cx.A, cx.P, cx.dr
    H = half
    hx = H * 128

    if cx.gla_sw is None:
        cx.gla_sw = [[cx.sem_sw(f"glawl{i}"), 0] for i in range(3)]
    GS0, GS1, GS2 = cx.gla_sw
    b0, b1, b2 = GS0[1], GS1[1], GS2[1]

    sem = cx.sem
    s_slf = sem("slf")
    s_gh = sem("gh"); s_ghe = sem("ghe"); s_ghv = sem("ghv")
    s_ght = sem("ght"); s_ghtc = sem("ghtc")
    s_zg = sem("zg"); s_ze = sem("ze")
    s_sgb = sem("sgb"); s_ra = sem("ra"); s_rac = sem("rac")
    s_exb = sem("exb"); s_rr = sem("rr"); s_rrc = sem("rrc")
    s_pq = sem("pq"); s_peq = sem("peq"); s_pev2 = sem("pev2")
    s_peo = sem("peo"); s_oge = sem("oge")
    s_pm = sem("pm"); s_pp = sem("pp"); s_pev = sem("pev")
    s_ob = sem("ob"); s_ov = sem("ov"); s_oa = sem("oa")
    s_osq = sem("osq"); s_m1 = sem("m1"); s_m2 = sem("m2")
    s_mre = sem("mre"); s_rwm = sem("rwm"); s_rst = sem("rst")
    s_ms = sem("ms"); s_rb = sem("rb")
    s_orp = sem("orp"); s_apv = sem("apv")
    s_wm = sem("wm"); s_zev = sem("zev")
    s_yl = [cx.sem_hw("yla"), cx.sem_hw("ylb")]
    s_zd = [cx.sem_hw("zda"), cx.sem_hw("zdb")]
    s_fin = sem("gfin")

    xhat4 = A["xhat"][:, :].rearrange("p (f t c) -> p f t c", f=8, t=8)

    def sqkv_ap(t8, which):
        return which[:, :].rearrange("p (f t c) -> p f t c", f=8, t=8)[
            :, :, t8, :]

    GS0[1] = b0 + 80
    GS1[1] = b1 + 48
    GS2[1] = b2 + 32

    if H == 0:
        with nc.Block("glaself") as b:
            @b.vector
            def _(v):
                v.wait_ge(*wait)
                v.memset(A["selfP"][:, :], 0.0)
                v.memset(A["ones5"][0:1, :], 1.0)
                v.drain()
                last = None
                for fc in range(8):
                    v.memset(A["selfP"][0:64, fc * 16 + 2 * fc:
                                        fc * 16 + 2 * fc + 1], 1.0)
                    last = v.memset(A["selfP"][64:128, fc * 16 + 2 * fc + 1:
                                               fc * 16 + 2 * fc + 2], 1.0)
                last.then_inc(s_slf, 1)

            @b.tensor
            def _(t):
                for fc in range(8):
                    t.wait_ge(s_slf, 1 + 2 * fc)
                    t.matmul(P[6][0:16, 0:128], A["selfP"][:, fc * 16:
                                                           fc * 16 + 16],
                             A["identb"][:, :], start=True, stop=True
                             ).then_inc(s_slf, 1)

            @b.scalar
            def _(sc):
                for fc in range(8):
                    sc.wait_ge(s_slf, 2 + 2 * fc)
                    sc.activation(A["selfT"][:, fc * 128:(fc + 1) * 128],
                                  P[6][0:16, 0:128], AF.Copy
                                  ).then_inc(s_slf, 1)
        slf_tok = (s_slf, 17)
    else:
        slf_tok = wait

    with nc.Block(f"glaG{H}") as b:
        @b.gpsimd
        def _(g):
            g.wait_ge(*wait)
            g.dma_start(out=A["wg2s"][:, :], in_=dr["bWg2Tb"][0:16, :]
                        ).then_inc(GS0[0], 16)
            g.dma_start(out=A["bg2r"][:, :], in_=dr["bWg2Tb"][16:17, :]
                        ).then_inc(GS0[0], 16)
            g.dma_start(out=A["wg1s"][:, :].rearrange("p (c n) -> p c n", c=8),
                        in_=dr["bWg1T"][:, :].rearrange("(c p) n -> p c n",
                                                        p=128)
                        ).then_inc(GS0[0], 16)
            g.dma_start(out=A["bg1r"][:, :], in_=dr["bBg1r"][:, :]
                        ).then_inc(GS0[0], 16)
            g.dma_start(out=A["gwA"][:, :].rearrange("p (c n) -> p c n", c=8),
                        in_=dr["bWqT"][:, :].rearrange("(c p) n -> p c n",
                                                       p=128)
                        ).then_inc(GS1[0], 16)
            g.wait_ge(s_zg, 16)
            g.dma_start(out=A["gwB"][:, :].rearrange("p (c n) -> p c n", c=8),
                        in_=dr["bWkT"][:, :].rearrange("(c p) n -> p c n",
                                                       p=128)
                        ).then_inc(GS0[0], 16)

        @b.tensor
        def _(t):
            t.wait_ge(*wait)
            t.wait_ge(GS0[0], b0 + 64)
            for t8 in range(8):
                if t8 >= 1:
                    t.wait_ge(s_ghv, t8)
                for kc in range(8):
                    t.matmul(P[5][:, 0:16],
                             fm(A["xhat"], kc, t8 * 256 + hx,
                                t8 * 256 + hx + 128),
                             A["wg1s"][:, kc * 16:kc * 16 + 16],
                             start=(kc == 0), stop=False)
                t.matmul(P[5][:, 0:16], A["onesb"][0:1, 0:128],
                         A["bg1r"][0:1, :], start=False, stop=True
                         ).then_inc(s_gh, 1)
            for t8 in range(8):
                t.wait_ge(s_ghv, t8 + 1)
                if t8 >= 1:
                    t.wait_ge(s_ghtc, t8)
                t.matmul(P[6][0:16, 0:128], A["ghb"][:, t8 * 16:t8 * 16 + 16],
                         A["identb"][:, :], start=True, stop=True
                         ).then_inc(s_ght, 1)
            t.wait_ge(s_ghtc, 8)
            for jj in range(2):
                for fc in range(8):
                    zi = jj * 8 + fc
                    ps = P[zi % 4]
                    if zi >= 4:
                        t.wait_ge(s_ze, zi - 3)
                    t.matmul(ps[:, :], A["wg2s"][0:16, fc * 128:
                                                 (fc + 1) * 128],
                             A["ghT"][0:16, jj * 512:(jj + 1) * 512],
                             start=True, stop=False)
                    t.matmul(ps[:, :], A["bg2r"][0:1, fc * 128:
                                                 (fc + 1) * 128],
                             A["ones5"][0:1, 0:512], start=False, stop=True
                             ).then_inc(s_zg, 1)

        @b.scalar
        def _(sc):
            sc.wait_ge(*slf_tok)
            for t8 in range(8):
                sc.wait_ge(s_gh, t8 + 1)
                if t8 >= 1:
                    sc.wait_ge(s_ghv, t8)
                sc.activation(A["sg16"][:, :], P[5][:, 0:16], AF.Sigmoid
                              ).then_inc(s_ghe, 1)
            for t8 in range(8):
                sc.wait_ge(s_ght, t8 + 1)
                sc.activation(A["ghT"][:, t8 * 128:(t8 + 1) * 128],
                              P[6][0:16, 0:128], AF.Copy).then_inc(s_ghtc, 1)
            for jj in range(2):
                for fc in range(8):
                    zi = jj * 8 + fc
                    sc.wait_ge(s_zg, zi + 1)
                    sc.activation(A["zgs"][:, fc * 1024 + jj * 512:
                                           fc * 1024 + jj * 512 + 512],
                                  P[zi % 4][:, :], AF.Copy).then_inc(s_ze, 1)
            sc.wait_ge(s_ze, 16)
            for i in range(4):
                sc.activation(A["asb"][:, i * 2048:(i + 1) * 2048],
                              A["zgs"][:, i * 2048:(i + 1) * 2048],
                              AF.Sigmoid, scale=1.0 / 16.0).then_inc(s_sgb, 1)
            for t8 in range(8):
                sc.wait_ge(s_ra, (t8 + 1) * 8)
                ar = A["aroll0"] if t8 % 2 == 0 else A["aroll1"]
                sc.activation(A["cumA"][:, :].rearrange(
                                  "p (f t c) -> p f t c", f=8, t=8)[:, :, t8, :],
                              ar[:, :].rearrange("p (f c) -> p f c", f=8),
                              AF.Copy).then_inc(s_rac, 1)
            sc.wait_ge(s_ra, 64)
            for i in range(4):
                sc.activation(A["asb"][:, i * 2048:(i + 1) * 2048],
                              A["zgs"][:, i * 2048:(i + 1) * 2048],
                              AF.Exp, scale=-1.0 / 16.0).then_inc(s_exb, 1)
            for t8 in range(8):
                sc.wait_ge(s_rr, (t8 + 1) * 8)
                rr = A["rroll0"] if t8 % 2 == 0 else A["rroll1"]
                sc.activation(A["cumR"][:, :].rearrange(
                                  "p (f t c) -> p f t c", f=8, t=8)[:, :, t8, :],
                              rr[:, :].rearrange("p (f c) -> p f c", f=8),
                              AF.Copy).then_inc(s_rrc, 1)

        @b.vector
        def _(v):
            if H == 0:
                v.wait_ge(*slf_tok)
            else:
                v.wait_ge(*wait)
            for t8 in range(8):
                v.wait_ge(s_ghe, t8 + 1)
                v.tensor_mul(A["ghb"][:, t8 * 16:t8 * 16 + 16],
                             P[5][:, 0:16], A["sg16"][:, :]).then_inc(s_ghv, 1)
            v.wait_ge(s_sgb, 4)
            for t8 in range(8):
                dst = A["aroll0"] if t8 % 2 == 0 else A["aroll1"]
                src = A["aroll1"] if t8 % 2 == 0 else A["aroll0"]
                if t8 >= 2:
                    v.wait_ge(s_rac, t8 - 1)
                for fc in range(8):
                    asl = A["asb"][:, fc * 1024 + t8 * 128:
                                   fc * 1024 + t8 * 128 + 128]
                    if t8 == 0:
                        v.tensor_copy(dst[:, fc * 128:(fc + 1) * 128], asl
                                      ).then_inc(s_ra, 1)
                    else:
                        v.tensor_mul(dst[:, fc * 128:(fc + 1) * 128],
                                     src[:, fc * 128:(fc + 1) * 128], asl
                                     ).then_inc(s_ra, 1)
                v.drain()
            v.wait_ge(s_exb, 4)
            for t8 in range(8):
                dst = A["rroll0"] if t8 % 2 == 0 else A["rroll1"]
                src = A["rroll1"] if t8 % 2 == 0 else A["rroll0"]
                if t8 >= 2:
                    v.wait_ge(s_rrc, t8 - 1)
                for fc in range(8):
                    asl = A["asb"][:, fc * 1024 + t8 * 128:
                                   fc * 1024 + t8 * 128 + 128]
                    if t8 == 0:
                        v.tensor_scalar(dst[:, fc * 128:(fc + 1) * 128],
                                        asl, 1.0, None, ALU.add
                                        ).then_inc(s_rr, 1)
                    else:
                        v.scalar_tensor_tensor(
                            dst[:, fc * 128:(fc + 1) * 128], asl, 1.0,
                            src[:, fc * 128:(fc + 1) * 128],
                            ALU.add, ALU.mult).then_inc(s_rr, 1)
                v.drain()

    if upto == "G":
        return s_rrc, 8

    wsel = [("q", A["gwA"], (GS1, b1 + 16)), ("k", A["gwB"], (GS0, b0 + 80)),
            ("v", A["gwA"], (GS1, b1 + 32)), ("og", A["gwB"], (GS2, b2 + 32))]
    with nc.Block(f"glaP{H}") as b:
        @b.gpsimd
        def _(g):
            g.wait_ge(s_pq, 16)
            g.dma_start(out=A["gwA"][:, :].rearrange("p (c n) -> p c n", c=8),
                        in_=dr["bWvT"][:, :].rearrange("(c p) n -> p c n",
                                                       p=128)
                        ).then_inc(GS1[0], 16)
            g.wait_ge(s_pq, 32)
            g.wait_ge(s_rr, 64)
            g.dma_start(out=A["ogbr"][:, :], in_=dr["bWogTb"][1024:1025, :]
                        ).then_inc(GS2[0], 16)
            g.dma_start(out=A["gwB"][:, :].rearrange("p (c n) -> p c n", c=8),
                        in_=dr["bWogTb"][0:1024, :].rearrange(
                            "(c p) n -> p c n", p=128)
                        ).then_inc(GS2[0], 16)
            g.wait_ge(s_pq, 48)
            g.dma_start(out=A["gwA"][:, :].rearrange("p (c n) -> p c n", c=8),
                        in_=dr["bWoT"][:, :].rearrange("(c p) n -> p c n",
                                                       p=128)
                        ).then_inc(GS1[0], 16)

        @b.tensor
        def _(t):
            for wi, (wn, wbuf, wcnt) in enumerate(wsel):
                t.wait_ge(wcnt[0][0], wcnt[1])
                if wn == "q":
                    t.wait_ge(s_rac, 8)
                if wn == "k":
                    t.wait_ge(s_rrc, 8)
                    t.wait_ge(s_peq, 16)
                if wn == "v":
                    t.wait_ge(s_exb, 4)
                    t.wait_ge(s_rr, 64)
                    t.wait_ge(s_peq, 32)
                if wn == "og":
                    t.wait_ge(s_pev2, 16)
                for jb in range(2):
                    for fo in range(8):
                        gi = wi * 16 + jb * 8 + fo
                        j = jb * 8 + fo
                        ps = P[gi % 4]
                        if j >= 4:
                            esem = {"q": s_peq, "k": s_peq, "v": s_pev2,
                                    "og": s_peo}[wn]
                            ebase = 16 if wn == "k" else 0
                            t.wait_ge(esem, ebase + j - 3)
                        for kc in range(8):
                            ins = t.matmul(ps[:, :],
                                           wbuf[:, kc * 1024 + fo * 128:
                                                kc * 1024 + fo * 128 + 128],
                                           xhat4[:, kc, 4 * jb:4 * jb + 4,
                                                 hx:hx + 128],
                                           start=(kc == 0),
                                           stop=(kc == 7 and wn != "og"))
                        if wn == "og":
                            ins = t.matmul(ps[:, :],
                                           A["ogbr"][0:1, fo * 128:
                                                     fo * 128 + 128],
                                           A["ones5"][0:1, 0:512],
                                           start=False, stop=True)
                        ins.then_inc(s_pq, 1)

        @b.scalar
        def _(sc):
            for jb in range(2):
                for fo in range(8):
                    gi = 32 + jb * 8 + fo
                    sc.wait_ge(s_pq, gi + 1)
                    sc.activation(A["sv2"][:, fo * 1024 + jb * 512:
                                           fo * 1024 + jb * 512 + 512],
                                  P[gi % 4][:, :], AF.Copy).then_inc(s_pev2, 1)
            for jb in range(2):
                for fo in range(8):
                    i = jb * 8 + fo
                    gi = 48 + i
                    sc.wait_ge(s_pq, gi + 1)
                    if i >= 1:
                        sc.wait_ge(s_peo, i)
                    sc.activation(A["ogstmp"][:, :], P[gi % 4][:, :],
                                  AF.Sigmoid).then_inc(s_oge, 1)

        @b.vector
        def _(v):
            for wi, wn in [(0, "q"), (1, "k")]:
                cum = A["cumA"] if wn == "q" else A["cumR"]
                dst = A["sq2"] if wn == "q" else A["sk2"]
                for jb in range(2):
                    for fo in range(8):
                        gi = wi * 16 + jb * 8 + fo
                        v.wait_ge(s_pq, gi + 1)
                        sl = slice(fo * 1024 + jb * 512,
                                   fo * 1024 + jb * 512 + 512)
                        v.tensor_mul(dst[:, sl], P[gi % 4][:, :], cum[:, sl]
                                     ).then_inc(s_peq, 1)
            for jb in range(2):
                for fo in range(8):
                    i = jb * 8 + fo
                    gi = 48 + i
                    v.wait_ge(s_oge, i + 1)
                    sl = slice(fo * 1024 + jb * 512,
                               fo * 1024 + jb * 512 + 512)
                    v.tensor_mul(A["og2"][:, sl], P[gi % 4][:, :],
                                 A["ogstmp"][:, :]).then_inc(s_peo, 1)

    if upto == "P":
        return s_peo, 16

    with nc.Block(f"glaQ{H}") as b:
        @b.vector
        def _(v):
            v.wait_ge(s_peq, 32)
            for idx, (t8, sp) in enumerate(GLA_PAIRS):
                mp = A["mtm"] if idx % 2 == 0 else A["mtm2"]
                if idx >= 2:
                    v.wait_ge(s_pp, 8 * (idx - 1))
                v.tensor_mul(mp[:, :].rearrange("p (f c) -> p f c", f=8),
                             sqkv_ap(t8, A["sq2"]), sqkv_ap(sp, A["sk2"])
                             ).then_inc(s_pm, 1)

        @b.tensor
        def _(t):
            for idx in range(36):
                t.wait_ge(s_pm, idx + 1)
                if idx % 4 == 0 and idx >= 4:
                    t.wait_ge(s_pev, idx // 4)
                mp = A["mtm"] if idx % 2 == 0 else A["mtm2"]
                for fc in range(8):
                    t.matmul(P[7][0:16, (idx % 4) * 128:(idx % 4) * 128 + 128],
                             A["selfP"][:, fc * 16:fc * 16 + 16],
                             mp[:, fc * 128:(fc + 1) * 128],
                             start=(idx % 4 == 0 and fc == 0),
                             stop=(idx % 4 == 3 and fc == 7),
                             skip_group_check=True).then_inc(s_pp, 1)

        @b.scalar
        def _(sc):
            for grp in range(9):
                sc.wait_ge(s_pp, 32 * (grp + 1))
                sc.activation(A["Pm"][:, grp * 512:(grp + 1) * 512],
                              P[7][0:16, :], AF.Copy).then_inc(s_pev, 1)

    with nc.Block(f"glaO{H}") as b:
        @b.tensor
        def _(t):
            for idx, (t8, sp) in enumerate(GLA_PAIRS):
                t.wait_ge(s_pev, idx // 4 + 1)
                if idx >= 1:
                    t.wait_ge(s_ov, idx)
                for fc in range(8):
                    bank = P[4] if fc < 4 else P[5]
                    col = (fc % 4) * 128
                    ins = t.matmul(bank[:, col:col + 128],
                                   A["selfT"][0:16, fc * 128:(fc + 1) * 128],
                                   A["Pm"][0:16, idx * 128:(idx + 1) * 128],
                                   start=(fc % 4 == 0), stop=(fc % 4 == 3),
                                   skip_group_check=True)
                ins.then_inc(s_ob, 1)

        @b.vector
        def _(v):
            v.wait_ge(s_pev2, 16)
            for idx, (t8, sp) in enumerate(GLA_PAIRS):
                v.wait_ge(s_ob, idx + 1)
                so_ap = A["so2"][:, :].rearrange(
                    "p (f t c) -> p f t c", f=8, t=8)[:, :, t8, :]
                svA = A["sv2"][:, :].rearrange(
                    "p (f t c) -> p f t c", f=8, t=8)[:, 0:4, sp, :]
                svB = A["sv2"][:, :].rearrange(
                    "p (f t c) -> p f t c", f=8, t=8)[:, 4:8, sp, :]
                if t8 == 0:
                    v.tensor_mul(so_ap[:, 0:4, :], P[4][:, :].rearrange(
                        "p (f c) -> p f c", f=4), svA)
                    v.tensor_mul(so_ap[:, 4:8, :], P[5][:, :].rearrange(
                        "p (f c) -> p f c", f=4), svB).then_inc(s_ov, 1)
                    v.drain()
                    continue
                if sp == 0:
                    dst = A["otA"]
                    v.tensor_mul(dst[:, 0:512], P[4][:, :].rearrange(
                        "p (f c) -> p f c", f=4), svA)
                    v.tensor_mul(dst[:, 512:1024], P[5][:, :].rearrange(
                        "p (f c) -> p f c", f=4), svB).then_inc(s_ov, 1)
                    v.drain()
                    continue
                v.tensor_mul(A["mscr"][:, 0:512], P[4][:, :].rearrange(
                    "p (f c) -> p f c", f=4), svA)
                v.tensor_mul(A["mscr"][:, 512:1024], P[5][:, :].rearrange(
                    "p (f c) -> p f c", f=4), svB)
                v.drain()
                prev = A["otA"] if sp % 2 == 1 else A["otB"]
                if sp == t8:
                    v.tensor_add(so_ap,
                                 prev[:, :].rearrange("p (f c) -> p f c", f=8),
                                 A["mscr"][:, :].rearrange(
                                     "p (f c) -> p f c", f=8)).then_inc(s_ov, 1)
                else:
                    nxt = A["otB"] if sp % 2 == 1 else A["otA"]
                    v.tensor_add(nxt[:, :], prev[:, :], A["mscr"][:, :]
                                 ).then_inc(s_ov, 1)
                v.drain()

    if upto == "O":
        return s_ov, 36

    with nc.Block(f"glaL{H}") as b:
        @b.scalar
        def _(sc):
            sc.wait_ge(s_ov, 36)
            for jj in range(2):
                for fc in range(8):
                    i = jj * 8 + fc
                    osq = A["mtm"] if i % 2 == 0 else A["mtm2"]
                    if i >= 2:
                        sc.wait_ge(s_m2, 8 * (i - 1))
                    sc.activation(osq[:, 0:512],
                                  A["so2"][:, fc * 1024 + jj * 512:
                                           fc * 1024 + jj * 512 + 512],
                                  AF.Square).then_inc(s_osq, 1)
                sc.wait_ge(s_m1, 8 * (jj + 1))
                sc.activation(A["m1r"][:, jj * 512:(jj + 1) * 512],
                              P[6][0:16, :], AF.Copy).then_inc(s_mre, 1)
                sc.wait_ge(s_m2, 64 * (jj + 1))
                sc.activation(A["m2r"][:, jj * 512:(jj + 1) * 512],
                              P[7][0:16, :], AF.Copy).then_inc(s_mre, 1)
            sc.wait_ge(s_rwm, 1)
            sc.activation(A["msr"][:, :], A["m2r"][:, :], AF.Ln,
                          bias=A["epsln"][0:16, 0:1])
            sc.drain()
            sc.activation(A["rstdr"][:, :], A["msr"][:, :], AF.Exp,
                          scale=-0.5).then_inc(s_rst, 1)
            sc.wait_ge(s_ms, 1)
            sc.activation(A["rstdb"][:, :], A["rstdr"][:, :], AF.Copy)
            sc.activation(A["msb"][:, :], A["m1r"][:, :], AF.Copy
                          ).then_inc(s_rb, 2)

        @b.tensor
        def _(t):
            for jj in range(2):
                if jj == 1:
                    t.wait_ge(s_mre, 2)
                t.wait_ge(s_ov, 36)
                for fc in range(8):
                    t.matmul(P[6][0:16, :], A["selfP"][:, fc * 16:fc * 16 + 16],
                             A["so2"][:, fc * 1024 + jj * 512:
                                      fc * 1024 + jj * 512 + 512],
                             start=(fc == 0), stop=(fc == 7)).then_inc(s_m1, 1)
                for fc in range(8):
                    i = jj * 8 + fc
                    t.wait_ge(s_osq, i + 1)
                    osq = A["mtm"] if i % 2 == 0 else A["mtm2"]
                    t.matmul(P[7][0:16, :], A["selfP"][:, fc * 16:fc * 16 + 16],
                             osq[:, 0:512],
                             start=(fc == 0), stop=(fc == 7)).then_inc(s_m2, 8)
            t.wait_ge(s_rb, 2)
            for jj in range(2):
                for fc in range(8):
                    ri = jj * 8 + fc
                    if ri >= 1:
                        t.wait_ge(s_apv, 3 * ri)
                    t.matmul(P[4][:, :], A["selfT"][0:16, fc * 128:
                                                    (fc + 1) * 128],
                             A["rstdb"][0:16, jj * 512:(jj + 1) * 512],
                             start=True, stop=True)
                    t.matmul(P[5][:, :], A["selfT"][0:16, fc * 128:
                                                    (fc + 1) * 128],
                             A["msb"][0:16, jj * 512:(jj + 1) * 512],
                             start=True, stop=True).then_inc(s_orp, 2)

        @b.vector
        def _(v):
            v.wait_ge(s_mre, 4)
            v.tensor_scalar_mul(A["meanr"][:, :], A["m1r"][:, :], 1.0 / 64)
            v.tensor_scalar_mul(A["msr"][:, :], A["m2r"][:, :], 1.0 / 64)
            v.drain()
            v.tensor_mul(A["m1r"][:, :], A["meanr"][:, :], A["meanr"][:, :])
            v.drain()
            v.tensor_sub(A["m2r"][:, :], A["msr"][:, :], A["m1r"][:, :]
                         ).then_inc(s_rwm, 1)
            v.wait_ge(s_rst, 1)
            v.drain()
            v.tensor_mul(A["m1r"][:, :], A["meanr"][:, :], A["rstdr"][:, :]
                         ).then_inc(s_ms, 1)
            for jj in range(2):
                for fc in range(8):
                    ri = jj * 8 + fc
                    v.wait_ge(s_orp, 2 * (ri + 1))
                    sl = slice(fc * 1024 + jj * 512, fc * 1024 + jj * 512 + 512)
                    v.tensor_mul(A["otA"][:, 0:512], A["so2"][:, sl],
                                 P[4][:, :])
                    v.drain()
                    v.tensor_sub(A["otB"][:, 0:512], A["otA"][:, 0:512],
                                 P[5][:, :])
                    v.drain()
                    v.tensor_mul(A["u2"][:, sl], A["otB"][:, 0:512],
                                 A["og2"][:, sl]).then_inc(s_apv, 3)
                    v.drain()

    if upto == "U":
        return s_apv, 48

    yr = [A["yr0"], A["yr1"]]
    zsl = [A["zsl0"], A["zsl1"]]
    with nc.Block(f"glaZ{H}") as b:
        @b.sync
        def _(sy):
            sy.wait_ge(s_apv, 48)
            for fo in range(8):
                if fo >= 2:
                    sy.wait_ge(s_zev, 2 * (fo - 1))
                sy.dma_start(
                    out=yr[fo % 2][:, :].rearrange("p (t c) -> p t c", t=8),
                    in_=dr["yd"][fo * 128:(fo + 1) * 128, :].rearrange(
                        "p (t c) -> p t c", t=8)[:, :, hx:hx + 128]
                    ).then_inc(s_yl[fo % 2], 16)
                wi0 = fo * 2
                sy.wait_ge(s_zev, wi0 + 2)
                if fo >= 2:
                    sy.wait_ge(s_zd[fo % 2], 16 * (fo // 2))
                sy.dma_start(
                    out=dr["z1d"][fo * 128:(fo + 1) * 128, :].rearrange(
                        "p (t c) -> p t c", t=8)[:, :, hx:hx + 128],
                    in_=zsl[fo % 2][:, :].rearrange("p (t c) -> p t c", t=8)
                    ).then_inc(s_zd[fo % 2], 16)

        @b.tensor
        def _(t):
            t.wait_ge(GS1[0], b1 + 48)
            t.wait_ge(s_apv, 48)
            for fo in range(8):
                for jb in range(2):
                    wi = fo * 2 + jb
                    ps = P[wi % 4]
                    if wi >= 4:
                        t.wait_ge(s_zev, wi - 3)
                    for kc in range(8):
                        ins = t.matmul(ps[:, :],
                                       A["gwA"][:, kc * 1024 + fo * 128:
                                                kc * 1024 + fo * 128 + 128],
                                       A["u2"][:, kc * 1024 + jb * 512:
                                               kc * 1024 + jb * 512 + 512],
                                       start=(kc == 0), stop=(kc == 7))
                    ins.then_inc(s_wm, 1)

        @b.vector
        def _(v):
            for fo in range(8):
                for jb in range(2):
                    wi = fo * 2 + jb
                    v.wait_ge(s_wm, wi + 1)
                    v.wait_ge(s_yl[fo % 2], 16 * (fo // 2 + 1))
                    if fo >= 2 and jb == 0:
                        v.wait_ge(s_zd[fo % 2], 16 * (fo // 2))
                    v.tensor_add(zsl[fo % 2][:, jb * 512:(jb + 1) * 512],
                                 P[wi % 4][:, :],
                                 yr[fo % 2][:, jb * 512:(jb + 1) * 512]
                                 ).then_inc(s_zev, 1)
            v.wait_ge(s_zd[0], 16 * 4)
            v.wait_ge(s_zd[1], 16 * 4)
            v.nop().then_inc(s_fin, 1)

    return s_fin, 1


def _glu(cx, wait):
    nc, A, P, dr = cx.nc, cx.A, cx.P, cx.dr
    sem = cx.sem
    s_hm = sem("ghm"); s_hs = sem("ghs"); s_hv = sem("ghv2")
    s_dm = sem("gdm"); s_dv = sem("gdv"); s_gfin = sem("gluf")
    s_wgu = [cx.sem_sw("wgua"), cx.sem_sw("wgub")]
    s_wd = [cx.sem_sw("wda"), cx.sem_sw("wdb")]
    s_x2r = [cx.sem_hw("x2ra"), cx.sem_hw("x2rb")]
    s_z1r = [cx.sem_hw("z1ra"), cx.sem_hw("z1rb")]
    s_go = [cx.sem_hw(f"go{i}") for i in range(4)]
    wgb = [A["wgb0"], A["wgb1"]]; wub = [A["wub0"], A["wub1"]]
    wdb = [A["wdb0"], A["wdb1"]]; sst = [A["sstmp0"], A["sstmp1"]]
    tgl = [A["tglu0"], A["tglu1"]]
    x2r = [A["x2r0"], A["x2r1"]]; z1r = [A["z1r0"], A["z1r1"]]

    def hbs(hc, jb):
        return A["hb2"][:, hc * 2048 + jb * 512:hc * 2048 + (jb + 1) * 512]

    with nc.Block("glu") as b:
        @b.gpsimd
        def _(g):
            g.wait_ge(*wait)
            for hc in range(32):
                if hc >= 2:
                    g.wait_ge(s_hm, 4 * (hc - 1))
                g.dma_start(
                    out=wgb[hc % 2][:, :].rearrange("p (c n) -> p c n", c=8),
                    in_=dr["bWgT"][:, hc * 128:(hc + 1) * 128].rearrange(
                        "(c p) n -> p c n", p=128)
                ).then_inc(s_wgu[hc % 2], 16)
                g.dma_start(
                    out=wub[hc % 2][:, :].rearrange("p (c n) -> p c n", c=8),
                    in_=dr["bWuT"][:, hc * 128:(hc + 1) * 128].rearrange(
                        "(c p) n -> p c n", p=128)
                ).then_inc(s_wgu[hc % 2], 16)
            for fo2 in range(4):
                for hc in range(32):
                    i = fo2 * 32 + hc
                    if i >= 2:
                        g.wait_ge(s_dm, i - 1)
                    g.dma_start(
                        out=wdb[i % 2][:, :],
                        in_=dr["bWdT"][hc * 128:(hc + 1) * 128,
                                       fo2 * 256:(fo2 + 1) * 256]
                    ).then_inc(s_wd[i % 2], 16)

        @b.tensor
        def _(t):
            for hc in range(32):
                t.wait_ge(s_wgu[hc % 2], 32 * (hc // 2 + 1))
                for jb in range(4):
                    i = hc * 4 + jb
                    bA = P[(2 * i) % 8]; bB = P[(2 * i + 1) % 8]
                    if i >= 4:
                        t.wait_ge(s_hv, i - 3)
                    for kc in range(8):
                        t.matmul(bA[:, :], wgb[hc % 2][:, kc * 128:
                                                       (kc + 1) * 128],
                                 A["nrm2"][:, kc * 2048 + jb * 512:
                                           kc * 2048 + jb * 512 + 512],
                                 start=(kc == 0), stop=(kc == 7))
                    for kc in range(8):
                        ins = t.matmul(bB[:, :], wub[hc % 2][:, kc * 128:
                                                             (kc + 1) * 128],
                                       A["nrm2"][:, kc * 2048 + jb * 512:
                                                 kc * 2048 + jb * 512 + 512],
                                       start=(kc == 0), stop=(kc == 7))
                    ins.then_inc(s_hm, 1)
            for fo2 in range(4):
                for hc in range(32):
                    i = fo2 * 32 + hc
                    t.wait_ge(s_wd[i % 2], 16 * (i // 2 + 1))
                    if hc == 0 and fo2 >= 1:
                        t.wait_ge(s_dv, 8 * fo2)
                    if fo2 == 0 and hc == 0:
                        t.wait_ge(s_hv, 128)
                    for fh in range(2):
                        for jb in range(4):
                            ins = t.matmul(P[fh * 4 + jb][:, :],
                                           wdb[i % 2][:, fh * 128:
                                                      (fh + 1) * 128],
                                           hbs(hc, jb),
                                           start=(hc == 0), stop=(hc == 31))
                    ins.then_inc(s_dm, 1)

        @b.scalar
        def _(sc):
            for hc in range(32):
                for jb in range(4):
                    i = hc * 4 + jb
                    sc.wait_ge(s_hm, i + 1)
                    if i >= 2:
                        sc.wait_ge(s_hv, i - 1)
                    bA = P[(2 * i) % 8]
                    sc.activation(sst[i % 2][:, :], bA[:, :], AF.Sigmoid
                                  ).then_inc(s_hs, 1)

        @b.vector
        def _(v):
            for hc in range(32):
                for jb in range(4):
                    i = hc * 4 + jb
                    v.wait_ge(s_hs, i + 1)
                    bA = P[(2 * i) % 8]; bB = P[(2 * i + 1) % 8]
                    v.tensor_mul(tgl[i % 2][:, :], sst[i % 2][:, :], bB[:, :])
                    v.drain()
                    v.tensor_mul(hbs(hc, jb), tgl[i % 2][:, :], bA[:, :]
                                 ).then_inc(s_hv, 1)
            for fo2 in range(4):
                for fh in range(2):
                    for jb in range(4):
                        k = fo2 * 8 + fh * 4 + jb
                        v.wait_ge(s_dm, 32 * (fo2 + 1))
                        v.wait_ge(s_x2r[k % 2], 32 * (k // 2 + 1))
                        v.wait_ge(s_z1r[k % 2], 16 * (k // 2 + 1))
                        if k >= 4:
                            v.wait_ge(s_go[k % 4], 16 * (k // 4))
                        v.tensor_add(tgl[k % 2][:, :], P[fh * 4 + jb][:, :],
                                     x2r[k % 2][:, :])
                        v.drain()
                        v.tensor_add(A["gslot"][:, (k % 4) * 512:
                                                (k % 4) * 512 + 512],
                                     tgl[k % 2][:, :], z1r[k % 2][:, :]
                                     ).then_inc(s_dv, 1)
                        v.drain()

        @b.sync
        def _(sy):
            sy.wait_ge(*wait)
            for k in range(33):
                if k < 32:
                    fo2, r = divmod(k, 8)
                    fh, jb = divmod(r, 4)
                    fo = fo2 * 2 + fh
                    if k >= 2:
                        sy.wait_ge(s_dv, k - 1)
                    for d2 in range(2):
                        sy.dma_start(
                            out=x2r[k % 2][:, d2 * 256:(d2 + 1) * 256],
                            in_=dr["x2mine"][2 * jb + d2,
                                             fo * 128:(fo + 1) * 128, :]
                        ).then_inc(s_x2r[k % 2], 16)
                    sy.dma_start(out=z1r[k % 2][:, :],
                                 in_=dr["z1d"][fo * 128:(fo + 1) * 128,
                                               jb * 512:(jb + 1) * 512]
                                 ).then_inc(s_z1r[k % 2], 16)
                if k >= 1:
                    kk = k - 1
                    fo2, r = divmod(kk, 8)
                    fh, jb = divmod(r, 4)
                    fo = fo2 * 2 + fh
                    sy.wait_ge(s_dv, kk + 1)
                    sy.dma_start(out=dr["x4d"][fo * 128:(fo + 1) * 128,
                                               jb * 512:(jb + 1) * 512],
                                 in_=A["gslot"][:, (kk % 4) * 512:
                                                (kk % 4) * 512 + 512]
                                 ).then_inc(s_go[kk % 4], 16)
            for q in range(4):
                sy.wait_ge(s_go[q], 16 * 8)
            sy.nop().then_inc(s_gfin, 1)

    return s_gfin


def _ffn(cx, wait, src_hbm, tag, out):
    nc, A, P, dr = cx.nc, cx.A, cx.P, cx.dr
    sem = cx.sem
    s_ln = _norm(cx, tag + "ln", src_hbm=src_hbm, src_sb=None, dst=A["xhat"],
                 kind="ln", wait=wait)

    if cx.ffn_sw is None:
        cx.ffn_sw = {k2: [cx.sem_sw("ffn" + k2), 0] for k2 in
                     ("rw1", "rw2", "w1a", "w1b", "w2a", "w2b")}
    FS = cx.ffn_sw
    s_rw1 = FS["rw1"]; s_rw2 = FS["rw2"]
    s_rt = sem(tag + "rt"); s_rv = sem(tag + "rv")
    s_rtt = sem(tag + "rtt"); s_rte = sem(tag + "rte")
    s_mrd = cx.sem_hw(tag + "mrd"); s_mrt = sem(tag + "mrt"); s_mrv = sem(tag + "mrv")
    s_w1 = [FS["w1a"], FS["w1b"]]
    s_hmm = sem(tag + "hmm"); s_hsc = sem(tag + "hsc"); s_hv = sem(tag + "hv")
    s_w2 = [FS["w2a"], FS["w2b"]]
    s_yg = sem(tag + "yg")
    s_yv = sem(tag + "yv")
    s_xr = [cx.sem_hw(tag + "xra"), cx.sem_hw(tag + "xrb")]
    s_yd = [cx.sem_hw(tag + "yd0"), cx.sem_hw(tag + "yd1"),
            cx.sem_hw(tag + "yd2"), cx.sem_hw(tag + "yd3")]
    s_fin = sem(tag + "fin")
    wb = [A["wbA"], A["wbB"]]

    def hbs(hc, nt):
        return A["hb"][:, hc * 1024 + nt * 512:hc * 1024 + (nt + 1) * 512]

    with nc.Block(tag) as b:
        @b.gpsimd
        def _(g):
            g.wait_ge(s_ln, 1)
            g.dma_start(out=A["rwT"][:, :].rearrange("p (c n) -> p c n", c=8),
                        in_=dr["fRWTb"][0:1024, :].rearrange(
                            "(c p) n -> p c n", p=128)).then_inc(s_rw1[0], 16)
            g.dma_start(out=A["rbrow"][:, :], in_=dr["fRWTb"][1024:1025, :]
                        ).then_inc(s_rw2[0], 16)
            for half in range(2):
                for grp in range(4):
                    gi = half * 4 + grp
                    if gi >= 2:
                        g.wait_ge(s_hv, (gi - 1) * 16)
                    if half == 1 and grp <= 1:
                        g.wait_ge(s_yg, 8)
                    g.dma_start(
                        out=wb[gi % 2][:, :].rearrange("p (c n) -> p c n", c=8),
                        in_=dr["fW1T"][:, grp * 1024:(grp + 1) * 1024].rearrange(
                            "(c p) n -> p c n", p=128)
                    ).then_inc(s_w1[gi % 2][0], 16)
                for q2 in range(2):
                    for grp in range(4):
                        gi2 = (half * 2 + q2) * 4 + grp
                        if gi2 >= 2:
                            g.wait_ge(s_yg, gi2 - 1)
                        if q2 == 0 and grp <= 1:
                            g.wait_ge(s_hmm, (half + 1) * 64)
                        g.dma_start(
                            out=wb[gi2 % 2][:, :].rearrange(
                                "p (c n) -> p c n", c=8),
                            in_=dr["fW2T"][grp * 1024:(grp + 1) * 1024, :]
                            .rearrange("(c p) n -> p c n", p=128)
                        ).then_inc(s_w2[gi2 % 2][0], 16)

        @b.scalar
        def _(s):
            for half in range(2):
                for grp in range(4):
                    for blk2 in range(4):
                        blk = grp * 4 + blk2
                        gb = half * 16 + blk
                        if gb == 0:
                            s.wait_ge(s_rte, 16)
                        if gb >= 1:
                            s.wait_ge(s_mrt, 2 * gb)
                        s.dma_start(out=A["maskrow"][:, :],
                                    in_=A["maskT"][blk:blk + 1, :]
                                    ).then_inc(s_mrd, 16)
                        for hc2 in range(2):
                            hc = blk * 2 + hc2
                            for nt in range(2):
                                i = gb * 4 + hc2 * 2 + nt
                                s.wait_ge(s_hmm, i + 1)
                                if i >= 2:
                                    s.wait_ge(s_hv, i - 1)
                                ps = P[i % 3]
                                ft = A["f_tmpA"] if i % 2 == 0 else A["f_tmpB"]
                                s.activation(ft[:, :], ps[:, :], AF.Relu,
                                             bias=A["biasf1"][:, hc:hc + 1]
                                             ).then_inc(s_hsc, 1)

        @b.sync
        def _(sy):
            sy.wait_ge(*wait)
            for j in range(33):
                if j < 32:
                    qn, oc = divmod(j, 8)
                    if j >= 2:
                        sy.wait_ge(s_yv, j - 1)
                    xr = A["frot0"] if j % 2 == 0 else A["frot1"]
                    sy.dma_start(out=xr[:, :],
                                 in_=src_hbm[oc * 128:(oc + 1) * 128,
                                             qn * 512:(qn + 1) * 512]
                                 ).then_inc(s_xr[j % 2], 16)
                if j >= 1:
                    k = j - 1
                    qn, oc = divmod(k, 8)
                    sy.wait_ge(s_yv, k + 1)
                    src = A["ytmp"][:, (k % 4) * 512:(k % 4) * 512 + 512]
                    if out == "cc":
                        for d2 in range(2):
                            d = qn * 2 + d2
                            sy.dma_start(out=dr["cc_in"][d, oc * 128:
                                                         (oc + 1) * 128, :],
                                         in_=src[:, d2 * 256:(d2 + 1) * 256]
                                         ).then_inc(s_yd[k % 4], 16)
                    else:
                        sy.dma_start(out=dr["yT"][oc * 128:(oc + 1) * 128,
                                                  qn * 512:(qn + 1) * 512],
                                     in_=src).then_inc(s_yd[k % 4], 16)
            per = (2 if out == "cc" else 1) * 16
            for q in range(4):
                sy.wait_ge(s_yd[q], 8 * per)
            sy.nop().then_inc(s_fin, 1)

        @b.tensor
        def _(t):
            t.wait_ge(s_rw1[0], s_rw1[1] + 16)
            t.wait_ge(s_rw2[0], s_rw2[1] + 16)
            for tc in range(16):
                ps = P[tc % 2]
                if tc >= 2:
                    t.wait_ge(s_rv, 4 * (tc - 1))
                for kc in range(8):
                    t.matmul(ps[:, 0:16], fm(A["xhat"], kc, tc * 128,
                                             (tc + 1) * 128),
                             A["rwT"][:, kc * 16:(kc + 1) * 16],
                             start=(kc == 0), stop=False)
                t.matmul(ps[:, 0:16], A["onesb"][0:1, :], A["rbrow"][0:1, :],
                         start=False, stop=True).then_inc(s_rt, 1)
            for tc in range(16):
                t.wait_ge(s_rv, 4 * tc + 4)
                if tc >= 2:
                    t.wait_ge(s_rte, tc - 1)
                t.matmul(P[4 + tc % 2][0:16, 0:128],
                         A["mtok"][:, tc * 16:(tc + 1) * 16],
                         A["identb"][:, :], start=True, stop=True
                         ).then_inc(s_rtt, 1)
            for half in range(2):
                for grp in range(4):
                    gi = half * 4 + grp
                    sw = s_w1[gi % 2]
                    t.wait_ge(sw[0], sw[1] + 16 * (gi // 2 + 1))
                    if half == 1 and grp == 0:
                        t.wait_ge(s_yv, 16)
                    for blk2 in range(4):
                        blk = grp * 4 + blk2
                        gb = half * 16 + blk
                        t.wait_ge(s_mrd, 16 * (blk + 1 if half == 0
                                               else 16 + blk + 1))
                        if gb >= 2:
                            t.wait_ge(s_mrv, 2 * (gb - 1))
                        for n2 in range(2):
                            t.matmul(P[4 + n2][:, :], A["onesb"][0:1, :],
                                     A["maskrow"][0:1, half * 1024 + n2 * 512:
                                                  half * 1024 + (n2 + 1) * 512],
                                     start=True, stop=True
                                     ).then_inc(s_mrt, 1)
                        for hc2 in range(2):
                            hc = blk * 2 + hc2
                            for nt in range(2):
                                i = gb * 4 + hc2 * 2 + nt
                                ps = P[i % 3]
                                if i >= 3:
                                    t.wait_ge(s_hv, i - 2)
                                ins = None
                                for kc in range(8):
                                    ins = t.matmul(
                                        ps[:, :],
                                        wb[gi % 2][:, kc * 1024 + (hc % 8) * 128:
                                                   kc * 1024 + (hc % 8) * 128 + 128],
                                        fm(A["xhat"], kc, half * 1024 + nt * 512,
                                           half * 1024 + (nt + 1) * 512),
                                        start=(kc == 0), stop=(kc == 7))
                                ins.then_inc(s_hmm, 1)
                for q2 in range(2):
                    qn = half * 2 + q2
                    for grp in range(4):
                        gi2 = qn * 4 + grp
                        sw2 = s_w2[gi2 % 2]
                        t.wait_ge(sw2[0], sw2[1] + 16 * (gi2 // 2 + 1))
                        if grp == 0:
                            t.wait_ge(s_hv, (half + 1) * 64)
                            if qn >= 1:
                                t.wait_ge(s_yv, 8 * qn)
                        ins = None
                        for oc in range(8):
                            for ic8 in range(8):
                                ic = grp * 8 + ic8
                                ins = t.matmul(
                                    P[oc][:, :],
                                    wb[gi2 % 2][:, ic8 * 1024 + oc * 128:
                                                ic8 * 1024 + oc * 128 + 128],
                                    hbs(ic, q2),
                                    start=(grp == 0 and ic8 == 0),
                                    stop=(grp == 3 and ic8 == 7),
                                    skip_group_check=True)
                        ins.then_inc(s_yg, 1)

        @b.vector
        def _(v):
            for tc in range(19):
                v.drain()
                if tc < 16:
                    v.wait_ge(s_rt, tc + 1)
                    sl = slice((tc % 4) * 16, (tc % 4) * 16 + 16)
                    v.tensor_copy(A["ltok"][:, sl], P[tc % 2][:, 0:16]
                                  ).then_inc(s_rv, 1)
                else:
                    v.nop().then_inc(s_rv, 1)
                if 1 <= tc and tc - 1 < 16:
                    tp = (tc - 1) % 4
                    v.max(A["m8"][:, tp * 8:tp * 8 + 8],
                          A["ltok"][:, tp * 16:tp * 16 + 16]).then_inc(s_rv, 1)
                else:
                    v.nop().then_inc(s_rv, 1)
                if 2 <= tc and tc - 2 < 16:
                    tp = (tc - 2) % 4
                    v.tensor_single_scalar(A["mtokf"][:, tp * 16:tp * 16 + 16],
                                           A["ltok"][:, tp * 16:tp * 16 + 16],
                                           A["m8"][:, tp * 8 + 3:tp * 8 + 4],
                                           ALU.is_ge).then_inc(s_rv, 1)
                else:
                    v.nop().then_inc(s_rv, 1)
                if 3 <= tc and tc - 3 < 16:
                    tp = (tc - 3) % 4
                    v.tensor_copy(A["mtok"][:, (tc - 3) * 16:(tc - 2) * 16],
                                  A["mtokf"][:, tp * 16:tp * 16 + 16]
                                  ).then_inc(s_rv, 1)
                else:
                    v.nop().then_inc(s_rv, 1)
            for tc in range(16):
                v.wait_ge(s_rtt, tc + 1)
                v.tensor_copy(A["maskT"][:, tc * 128:(tc + 1) * 128],
                              P[4 + tc % 2][0:16, 0:128]).then_inc(s_rte, 1)
            for half in range(2):
                for grp in range(4):
                    for blk2 in range(4):
                        blk = grp * 4 + blk2
                        gb = half * 16 + blk
                        v.drain()
                        for n2 in range(2):
                            v.wait_ge(s_mrt, 2 * gb + n2 + 1)
                            v.tensor_copy(A["mrepb"][:, n2 * 512:(n2 + 1) * 512],
                                          P[4 + n2][:, :]).then_inc(s_mrv, 1)
                        v.drain()
                        for hc2 in range(2):
                            hc = blk * 2 + hc2
                            for nt in range(2):
                                i = gb * 4 + hc2 * 2 + nt
                                v.wait_ge(s_hsc, i + 1)
                                ft = (A["f_tmpA"] if i % 2 == 0
                                      else A["f_tmpB"])
                                v.tensor_mul(hbs(hc, nt), ft[:, :],
                                             A["mrepb"][:, nt * 512:
                                                        (nt + 1) * 512]
                                             ).then_inc(s_hv, 1)
                for q2 in range(2):
                    qn = half * 2 + q2
                    v.wait_ge(s_yg, (qn + 1) * 4)
                    for oc in range(8):
                        j = qn * 8 + oc
                        v.wait_ge(s_xr[j % 2], 16 * (j // 2 + 1))
                        if j >= 4:
                            v.wait_ge(s_yd[j % 4], (2 if out == "cc" else 1)
                                      * 16 * (j // 4))
                        xr = A["frot0"] if j % 2 == 0 else A["frot1"]
                        v.scalar_tensor_tensor(
                            A["ytmp"][:, (j % 4) * 512:(j % 4) * 512 + 512],
                            P[oc][:, :], A["biasf2"][:, oc:oc + 1],
                            xr[:, :], ALU.add, ALU.add).then_inc(s_yv, 1)
    FS["rw1"][1] += 16; FS["rw2"][1] += 16
    FS["w1a"][1] += 64; FS["w1b"][1] += 64
    FS["w2a"][1] += 128; FS["w2b"][1] += 128
    return s_fin


# version 04 (final)
# speedup vs baseline: 1.0195x; 1.0195x over previous
import numpy as np
import ml_dtypes
import concourse.bass as bass
import concourse.mybir as mybir
from concourse.bass_utils import run_bass_kernel_spmd

F32 = mybir.dt.float32
BF16 = mybir.dt.bfloat16
AF = mybir.ActivationFunctionType
ALU = mybir.AluOpType
AX = mybir.AxisListType
BF16NP = ml_dtypes.bfloat16

D, T, H, DH, FC, TC = 1024, 2048, 16, 64, 8, 16
HID, NBLK, BLK, TOPK, U, NCORE = 4096, 16, 256, 4, 1228, 8


def _prep_shared(I):
    g = {}
    C = np.ascontiguousarray

    def chunkcol(b, n=8):
        return C(b.reshape(n, 128).T)

    g["ident_b"] = np.eye(128, dtype=np.float32)
    g["ones_b"] = np.ones((1, 128), dtype=np.float32)
    g["ones_bcol"] = np.ones((128, 1), dtype=np.float32)
    g["ones_fcol"] = np.ones((128, 1), dtype=np.float32)
    g["ones_frow"] = np.ones((1, 128), dtype=np.float32)

    g["aWq"] = C(I["L0_attn_a_Wq"]); g["aWk"] = C(I["L0_attn_a_Wk"])
    g["aBq"] = chunkcol(I["L0_attn_a_bq"]); g["aBk"] = chunkcol(I["L0_attn_a_bk"])
    g["aWvb"] = C(np.vstack([I["L0_attn_a_Wv"], I["L0_attn_a_bv"][None, :]]))
    g["aWo"] = C(I["L0_attn_a_Wo"]); g["aBo"] = chunkcol(I["L0_attn_a_bo"])

    g["fW1T"] = C(I["L0_ffn_w1"].T)
    g["fB1"] = C(I["L0_ffn_b1"].reshape(32, 128).T)
    g["fW2T"] = C(I["L0_ffn_w2"].T)
    g["fB2"] = chunkcol(I["L0_ffn_b2"])
    g["fRWTb"] = C(np.vstack([I["L0_ffn_router_w"].T,
                              I["L0_ffn_router_b"][None, :]]))

    g["bWqT"] = C(I["L0_attn_b_wq"].T)
    g["bWkT"] = C(I["L0_attn_b_wk"].T)
    g["bWvT"] = C(I["L0_attn_b_wv"].T)
    g["bWogTb"] = C(np.vstack([I["L0_attn_b_wog"].T,
                               I["L0_attn_b_bog"][None, :]]))
    g["bWg1T"] = C(I["L0_attn_b_wg1"].T)
    g["bBg1"] = C(I["L0_attn_b_bg1"].reshape(16, 1))
    g["bBg1r"] = C(I["L0_attn_b_bg1"].reshape(1, 16))
    g["bWg2Tb"] = C(np.vstack([I["L0_attn_b_wg2"].T,
                               I["L0_attn_b_bg2"][None, :]]))
    g["bWoT"] = C(I["L0_attn_b_wo"].T)
    g["bWgT"] = C(I["L0_attn_b_w_gate"].T)
    g["bWuT"] = C(I["L0_attn_b_w_up"].T)
    g["bWdT"] = C(I["L0_attn_b_w_down"].T)
    g["bS1"] = chunkcol(I["L0_attn_b_scale1"])
    g["bS2"] = chunkcol(I["L0_attn_b_scale2"])
    return g


def kernel(**inputs):
    return _run(inputs)


def _run(inputs, taps=None, stop_after=None):
    taps = taps or []
    shared = _prep_shared(inputs)
    x = inputs["L0_attn_a_x"]
    in_maps = []
    for c in range(NCORE):
        m = dict(shared)
        m["xT"] = np.ascontiguousarray(x[c].T)
        in_maps.append(m)

    nc = _build(taps=taps, stop_after=stop_after)
    res = run_bass_kernel_spmd(nc, in_maps, core_ids=list(range(NCORE)))
    if taps or stop_after:
        return res

    out = np.empty((8, 2048, 1024), dtype=np.float32)
    for c in range(NCORE):
        yT = res.results[c]["yT"]
        out[:, c * 256:(c + 1) * 256, :] = yT.T.reshape(8, 256, 1024)
    return out


BASE = 16384 + 1024
SB = dict(
    RA=BASE,
    RXH=BASE + 65536,
    RBV=BASE + 98304,
    RCC=BASE + 163840,
)
RCC_END = BASE + 196608


def _build(taps=(), stop_after=None):
    nc = bass.Bass("TRN2", target_bir_lowering=False)
    cx = Ctx()
    cx.nc = nc
    cx.taps = list(taps)
    cx.tap_outs = {}
    cx.stop_after = stop_after
    cx.semid = 0

    dr = {}
    dr["xT"] = nc.dram_tensor("xT", [1024, 2048], F32, kind="ExternalInput").ap()
    for n, shp in [
        ("ident_b", [128, 128]), ("ones_b", [1, 128]),
        ("ones_bcol", [128, 1]), ("ones_fcol", [128, 1]), ("ones_frow", [1, 128]),
        ("aWq", [1024, 1024]), ("aWk", [1024, 1024]),
        ("aBq", [128, 8]), ("aBk", [128, 8]),
        ("aWvb", [1025, 1024]), ("aWo", [1024, 1024]), ("aBo", [128, 8]),
        ("fW1T", [1024, 4096]), ("fB1", [128, 32]),
        ("fW2T", [4096, 1024]), ("fB2", [128, 8]), ("fRWTb", [1025, 16]),
        ("bWqT", [1024, 1024]), ("bWkT", [1024, 1024]), ("bWvT", [1024, 1024]),
        ("bWogTb", [1025, 1024]), ("bWg1T", [1024, 16]), ("bBg1", [16, 1]),
        ("bBg1r", [1, 16]),
        ("bWg2Tb", [17, 1024]), ("bWoT", [1024, 1024]),
        ("bWgT", [1024, 4096]), ("bWuT", [1024, 4096]), ("bWdT", [4096, 1024]),
        ("bS1", [128, 8]), ("bS2", [128, 8]),
    ]:
        dr[n] = nc.dram_tensor(n, shp, F32, kind="ExternalInput").ap()
    dr["yT"] = nc.dram_tensor("yT", [1024, 2048], F32, kind="ExternalOutput").ap()
    dr["x1d"] = nc.dram_tensor("x1d", [1024, 2048], F32, kind="Internal").ap()
    dr["yd"] = nc.dram_tensor("yd", [1024, 2048], F32, kind="Internal").ap()
    dr["z1d"] = nc.dram_tensor("z1d", [1024, 2048], F32, kind="Internal").ap()
    dr["x4d"] = nc.dram_tensor("x4d", [1024, 2048], F32, kind="Internal").ap()
    dr["cc_in"] = nc.dram_tensor("cc_in", [8, 1024, 256], F32, kind="Internal").ap()
    dr["cc_out"] = nc.dram_tensor("cc_out", [8, 8, 1024, 256], F32,
                                  kind="Internal", addr_space="Shared").ap()
    dr["x2mine"] = nc.dram_tensor("x2mine", [8, 1024, 256], F32,
                                  kind="Internal").ap()
    dr["bar_i"] = nc.dram_tensor("bar_i", [1, 8], F32, kind="Internal").ap()
    dr["bar_o"] = nc.dram_tensor("bar_o", [8, 8], F32, kind="Internal",
                                 addr_space="Shared").ap()
    cx.dr = dr

    A = {}
    def at(name, off, shape, dtype):
        A[name] = nc.alloc_sbuf_tensor_at(name, shape, dtype, offset=off)
        return A[name]

    at("Qfm", SB["RA"], [128, 16384], BF16)
    at("Kfm", SB["RA"] + 32768, [128, 16384], BF16)
    at("cmask", SB["RA"] + 32768, [128, 16384], BF16)
    at("cfm", SB["RCC"], [128, 16384], BF16)

    at("xhat", SB["RXH"], [128, 16384], BF16)
    at("wbX", SB["RXH"], [128, 8192], BF16)
    at("wbY", SB["RXH"] + 16384, [128, 8192], BF16)
    at("Vtok", SB["RBV"], [128, 16640], BF16)
    at("hb", SB["RBV"], [128, 32768], BF16)
    at("PT0", SB["RBV"] + 33280, [128, 4096], BF16)
    at("PT1", SB["RBV"] + 41472, [128, 4096], BF16)
    at("worot0", SB["RXH"] + 16384, [128, 512], F32)
    at("worot1", SB["RXH"] + 18432, [128, 512], F32)
    at("lnr0", SB["RBV"] + 33280, [128, 2048], F32)
    at("lnr1", SB["RBV"] + 41472, [128, 2048], F32)
    at("cct", SB["RCC"], [128, 16384], BF16)
    at("rrep", SB["RCC"], [128, 2048], F32)
    at("mrep", SB["RCC"] + 8192, [128, 2048], F32)
    at("tmpF", SB["RCC"] + 16384, [128, 2048], F32)
    at("tmpG", SB["RCC"] + 24576, [128, 2048], F32)
    at("wbA", SB["RCC"], [128, 8192], BF16)
    at("wbB2", SB["RCC"] + 16384, [128, 8192], BF16)
    at("tmpA", SB["RXH"] + 24576, [128, 512], BF16)
    at("tmpB", SB["RXH"] + 25600, [128, 512], BF16)
    at("tmpC", SB["RXH"] + 26624, [128, 512], BF16)
    at("tmpD", SB["RXH"] + 27648, [128, 512], BF16)
    at("wbB", SB["RA"], [128, 8192], BF16)
    at("mrepb", SB["RA"] + 16384, [128, 2048], BF16)
    at("maskT", SB["RA"] + 20480, [16, 2048], BF16)
    at("f_tmpA", SB["RA"] + 24576, [128, 512], BF16)
    at("f_tmpB", SB["RA"] + 25600, [128, 512], BF16)
    at("maskrow", SB["RA"] + 26624, [1, 2048], BF16)
    at("ytmp", SB["RA"] + 30720, [128, 2048], F32)
    at("frot0", SB["RA"] + 38912, [128, 512], F32)
    at("frot1", SB["RA"] + 40960, [128, 512], F32)
    at("sq2", SB["RA"], [128, 8192], BF16)
    at("msr", SB["RA"], [16, 1024], F32)
    at("sk2", SB["RA"] + 16384, [128, 8192], BF16)
    at("rstdb", SB["RA"] + 16384, [16, 1024], BF16)
    at("msb", SB["RA"] + 18432, [16, 1024], BF16)
    at("sv2", SB["RA"] + 32768, [128, 8192], BF16)
    at("zgs", SB["RA"] + 32768, [128, 8192], BF16)
    at("m1r", SB["RA"] + 32768, [16, 1024], F32)
    at("m2r", SB["RA"] + 36864, [16, 1024], F32)
    at("meanr", SB["RA"] + 40960, [16, 1024], F32)
    at("rstdr", SB["RA"] + 45056, [16, 1024], F32)
    at("so2", SB["RA"] + 49152, [128, 8192], BF16)
    at("aroll0", SB["RA"] + 49152, [128, 1024], F32)
    at("aroll1", SB["RA"] + 53248, [128, 1024], F32)
    at("rroll0", SB["RA"] + 57344, [128, 1024], F32)
    at("rroll1", SB["RA"] + 61440, [128, 1024], F32)
    at("ogbr", SB["RA"] + 49152, [1, 1024], BF16)
    at("ogstmp", SB["RA"] + 51200, [128, 512], BF16)
    at("ghb", SB["RBV"] + 32768, [128, 128], BF16)
    at("sg16", SB["RBV"] + 33024, [128, 16], BF16)
    at("cumA", SB["RBV"], [128, 8192], BF16)
    at("Pm", SB["RBV"], [16, 4608], BF16)
    at("cumR", SB["RBV"] + 16384, [128, 8192], BF16)
    at("mtm", SB["RBV"] + 16384, [128, 1024], BF16)
    at("mtm2", SB["RBV"] + 18432, [128, 1024], BF16)
    at("otA", SB["RBV"] + 20480, [128, 1024], F32)
    at("otB", SB["RBV"] + 24576, [128, 1024], F32)
    at("mscr", SB["RBV"] + 28672, [128, 1024], F32)
    at("og2", SB["RBV"] + 32768, [128, 8192], BF16)
    at("asb", SB["RBV"] + 32768, [128, 8192], BF16)
    at("zsl0", SB["RBV"] + 32768, [128, 1024], F32)
    at("zsl1", SB["RBV"] + 36864, [128, 1024], F32)
    at("yr0", SB["RBV"] + 40960, [128, 1024], F32)
    at("yr1", SB["RBV"] + 45056, [128, 1024], F32)
    at("u2", SB["RBV"] + 49152, [128, 8192], BF16)
    at("hb2", BASE, [128, 65536], BF16)
    at("gr0", BASE, [128, 2048], F32)
    at("gr1", BASE + 8192, [128, 2048], F32)
    at("nrm2", SB["RBV"] + 32768, [128, 16384], BF16)
    at("gslot", SB["RCC"], [128, 2048], F32)
    at("wgb0", SB["RCC"] + 8192, [128, 1024], BF16)
    at("wgb1", SB["RCC"] + 10240, [128, 1024], BF16)
    at("wub0", SB["RCC"] + 12288, [128, 1024], BF16)
    at("wub1", SB["RCC"] + 14336, [128, 1024], BF16)
    at("wdb0", SB["RCC"] + 16384, [128, 256], BF16)
    at("wdb1", SB["RCC"] + 16896, [128, 256], BF16)
    at("sstmp0", SB["RCC"] + 17408, [128, 512], BF16)
    at("sstmp1", SB["RCC"] + 18432, [128, 512], BF16)
    at("x2r0", SB["RCC"] + 19456, [128, 512], F32)
    at("x2r1", SB["RCC"] + 21504, [128, 512], F32)
    at("z1r0", SB["RCC"] + 23552, [128, 512], F32)
    at("z1r1", SB["RCC"] + 25600, [128, 512], F32)
    at("tglu0", SB["RCC"] + 27648, [128, 512], F32)
    at("tglu1", SB["RCC"] + 29696, [128, 512], F32)
    at("wdb2", SB["RCC"] + 31744, [128, 256], BF16)
    at("wdb3", SB["RCC"] + 32256, [128, 256], BF16)
    at("gwA", SB["RCC"], [128, 8192], BF16)
    at("gwB", SB["RCC"] + 16384, [128, 8192], BF16)
    at("wg2s", SB["RCC"] + 16384, [16, 1024], BF16)
    at("bg2r", SB["RCC"] + 18432, [1, 1024], BF16)

    cx.mo = RCC_END
    def misc(name, shape, dtype):
        nb = int(np.prod(shape[1:])) * mybir.dt.size(dtype)
        nb = (nb + 31) // 32 * 32
        A[name] = nc.alloc_sbuf_tensor_at(name, shape, dtype, offset=cx.mo)
        cx.mo += nb
        return A[name]
    misc("identb", [128, 128], BF16)
    misc("onesb", [1, 128], BF16)
    misc("onesbc", [128, 1], BF16)
    misc("onesf", [128, 1], F32)
    misc("biasq", [128, 8], F32)
    misc("biask", [128, 8], F32)
    misc("biaso", [128, 8], F32)
    misc("biasf1", [128, 32], F32)
    misc("biasf2", [128, 8], F32)
    misc("sc1", [128, 8], F32)
    misc("sc2", [128, 8], F32)
    misc("bg1", [16, 1], F32)
    misc("kbar", [128, 16], F32)
    misc("kbarb", [128, 16], BF16)
    misc("cbcol", [128, 2], BF16)
    misc("c1b", [1, 2], BF16)
    misc("cntrow", [1, 16], F32)
    misc("onesfr", [1, 128], F32)
    misc("bisx", [128, 160], F32)
    misc("m_all", [128, 256], F32)
    misc("sprs", [128, 256], F32)
    misc("amask", [128, 256], F32)
    misc("bis", [128, 64], F32)
    misc("negC", [128, 16], F32)
    misc("gtmp", [128, 32], F32)
    misc("rwT", [128, 128], BF16)
    misc("rbrow", [1, 16], BF16)
    misc("ltok", [128, 64], F32)
    misc("m8", [128, 32], F32)
    misc("mtokf", [128, 64], F32)
    misc("mtok", [128, 256], BF16)
    misc("c16", [16, 2], F32)
    misc("meancols", [128, 256], F32)
    misc("epsln", [128, 1], F32)
    misc("epsrms", [128, 1], F32)
    misc("selfP", [128, 128], BF16)
    misc("selfT", [16, 1024], BF16)
    misc("ghT", [16, 1024], BF16)
    misc("wg1s", [128, 128], BF16)
    misc("bg1r", [1, 16], BF16)
    misc("ones5", [1, 512], BF16)
    assert cx.mo <= 229376, f"misc overflow {cx.mo}"
    A["rowsb"] = None
    cx.A = A


    P = {}
    for i in range(8):
        P[i] = nc.alloc_psum_tensor(f"pb{i}", [128, 512], F32)
    cx.P = P

    cx.pools = {
        "eng": [nc.semaphore(f"pe{i}").__enter__() for i in range(56)],
        "hw": [nc.semaphore(f"ph{i}").__enter__() for i in range(20)],
    }
    cx.pool_i = {"eng": 0, "hw": 0}
    cx.sw_n = 0
    cx.s_reset = nc.semaphore("sreset").__enter__()
    cx.s_gather = nc.semaphore("sgather").__enter__()
    cx.nresets = 0

    def _sem(kind, tag):
        i = cx.pool_i[kind]
        assert i < len(cx.pools[kind]), f"{kind} sem pool exhausted at {tag}"
        cx.pool_i[kind] += 1
        return cx.pools[kind][i]
    cx.sem = lambda tag: _sem("eng", tag)
    cx.sem_hw = lambda tag: _sem("hw", tag)

    def sem_sw(tag):
        cx.sw_n += 1
        assert cx.sw_n <= 24, "sw sems exhausted"
        return nc.semaphore(f"sw{cx.sw_n}_{tag}").__enter__()
    cx.sem_sw = sem_sw
    cx.ffn_sw = None
    cx.gla_sw = None

    def stage_boundary(after):
        used = {k: cx.pool_i[k] for k in cx.pools}
        cx.nresets += 1
        k = cx.nresets
        with nc.Block(f"rst{k}") as b:
            for eng in ("scalar", "tensor"):
                def mk(engname):
                    def f(e):
                        e.wait_ge(*after)
                        e.nop().then_inc(cx.s_gather, 1)
                    return f
                getattr(b, eng)(mk(eng))

            @b.vector
            def _(v):
                v.wait_ge(*after)
                v.nop().then_inc(cx.s_gather, 1)
                v.wait_ge(cx.s_gather, 5 * k)
                for i in range(used["eng"]):
                    v.sem_clear(cx.pools["eng"][i])
                v.nop().then_inc(cx.s_reset, 1)

            @b.sync
            def _(sy):
                sy.wait_ge(*after)
                sy.nop().then_inc(cx.s_gather, 1)
                sy.wait_ge(cx.s_gather, 5 * k)
                for i in range(used["hw"]):
                    sy.sem_clear(cx.pools["hw"][i])
                sy.nop().then_inc(cx.s_reset, 1)

            @b.gpsimd
            def _(g):
                g.wait_ge(*after)
                g.nop().then_inc(cx.s_gather, 1)
                g.wait_ge(cx.s_gather, 5 * k)
                g.nop().then_inc(cx.s_reset, 1)
        for kind in cx.pools:
            cx.pool_i[kind] = 0
        return (cx.s_reset, 3 * k)
    cx.stage_boundary = stage_boundary

    _emit(cx)
    return nc


class Ctx:
    pass


def _tap(cx, name, src_ap, shape, dtype=F32):
    if name not in cx.taps:
        return
    t = cx.nc.dram_tensor("tap_" + name, shape, dtype, kind="ExternalOutput").ap()
    cx.tap_outs[name] = (t, src_ap)


def _finish(cx, done_sem, done_val):
    nc = cx.nc
    if not cx.tap_outs:
        return
    st = cx.sem_sw("tapst")
    with nc.Block("taps") as b:
        @b.gpsimd
        def _(g):
            g.wait_ge(done_sem, done_val)
            n = 0
            for name, (t, src) in cx.tap_outs.items():
                g.dma_start(out=t, in_=src).then_inc(st, 16)
                n += 1
            if n:
                g.wait_ge(st, 16 * n)


def fm(t, fc, a, b):
    return t[:, fc * 2048 + a: fc * 2048 + b]


def _emit(cx):
    nc, A, P, dr = cx.nc, cx.A, cx.P, cx.dr

    s_const = cx.sem_hw("const")
    s_constg = cx.sem_sw("constg")
    s_init = cx.sem("init2")
    with nc.Block("init") as b:
        @b.sync
        def _(sy):
            for src, dst in [("ones_fcol", "onesf"), ("ones_frow", "onesfr"),
                             ("aBq", "biasq"), ("aBk", "biask"), ("aBo", "biaso"),
                             ("fB1", "biasf1"), ("fB2", "biasf2"),
                             ("bS1", "sc1"), ("bS2", "sc2"), ("bBg1", "bg1")]:
                sy.dma_start(out=A[dst][:, :], in_=dr[src][:, :]).then_inc(s_const, 16)

        @b.gpsimd
        def _(g):
            for src, dst in [("ident_b", "identb"), ("ones_b", "onesb"),
                             ("ones_bcol", "onesbc")]:
                g.dma_start(out=A[dst][:, :], in_=dr[src][:, :]).then_inc(s_constg, 16)

        @b.vector
        def _(v):
            v.wait_ge(s_const, 10 * 16)
            v.wait_ge(s_constg, 3 * 16)
            ones_view = A["Vtok"][:, :].rearrange(
                "p (tc h e) -> p tc h e", tc=16, h=16, e=65)[:, :, :, 64:65]
            v.memset(ones_view, 1.0)
            v.memset(A["epsln"][:, :], 1e-5)
            v.memset(A["epsrms"][:, :], 1e-6)
            v.tensor_scalar_mul(A["biasq"][:, :], A["biasq"][:, :], 0.125)
            v.nop().then_inc(s_init, 1)

    s_ln = _norm(cx, "ln1", src_hbm=dr["xT"], src_sb=None, dst=A["xhat"],
                 kind="ln", wait=(s_init, 1))
    _tap(cx, "xhat", A["xhat"][:, :], [128, 16384])
    if cx.stop_after == "A":
        _finish(cx, s_ln, 1); return

    s_qkv = _qkv(cx, s_ln)
    _tap(cx, "Qfm", A["Qfm"][:, :], [128, 16384])
    _tap(cx, "Kfm", A["Kfm"][:, :], [128, 16384])
    _tap(cx, "Vtok", A["Vtok"][:, :], [128, 16640])
    if cx.stop_after == "B":
        _finish(cx, s_qkv, 1); return

    s_x1 = _attention(cx, s_qkv)
    _tap(cx, "x1", dr["x1d"][:, :], [1024, 2048])
    _tap(cx, "m_all", A["m_all"][:, :], [128, 256])
    _tap(cx, "sprs", A["sprs"][:, :], [128, 256])
    _tap(cx, "amask", A["amask"][:, :], [128, 256])
    _tap(cx, "cmask", A["cmask"][:, :], [128, 16384])
    if cx.stop_after == "D":
        _finish(cx, s_x1, 1); return

    rt1 = cx.stage_boundary((s_x1, 1))
    s_x2 = _ffn(cx, wait=rt1, src_hbm=dr["x1d"], tag="f1", out="cc")
    _tap(cx, "x2", dr["cc_in"][:, :, :], [8, 1024, 256])
    if cx.stop_after == "E":
        _finish(cx, s_x2, 1); return

    rt2 = cx.stage_boundary((s_x2, 1))
    s_cc = cx.sem_sw("cc")
    s_ccp = cx.sem_sw("ccp")
    s_cw = cx.sem_sw("ccw")
    with nc.Block("reshard") as b:
        @b.gpsimd
        def _(g):
            g.wait_ge(*rt2)
            g.collective_compute(
                "AllGather", ALU.bypass,
                ins=[dr["cc_in"][:, :, :]], outs=[dr["cc_out"][:, :, :, :]],
                replica_groups=[[0, 1, 2, 3, 4, 5, 6, 7]]).then_inc(s_cc, 1)
            g.wait_ge(s_cc, 1)
            for arm in g.switch_core_id(128):
                try:
                    lg = arm.logical % 8
                except Exception:
                    lg = arm % 8
                for bb in range(8):
                    g.dma_start(out=dr["x2mine"][bb, :, :],
                                in_=dr["cc_out"][bb, lg, :, :]
                                ).then_inc(s_ccp, 16)

    s_y2 = _norm(cx, "lny2", src_hbm=dr["x2mine"], src_sb=None, dst=A["xhat"],
                 kind="ln", wait=(s_ccp, 8 * 16), cc_src=True, y_out=dr["yd"])
    _tap(cx, "y2", dr["yd"][:, :], [1024, 2048])
    rt3 = cx.stage_boundary((s_y2, 1))
    s_rms1 = _norm(cx, "rms1", src_hbm=dr["yd"], src_sb=None, dst=A["xhat"],
                   kind="rms", wait=rt3, scale_col=A["sc1"])
    _tap(cx, "xg", A["xhat"][:, :], [128, 16384])
    if cx.stop_after == "G":
        _finish(cx, s_rms1, 1); return

    rt4 = cx.stage_boundary((s_rms1, 1))
    if cx.stop_after in ("H0G", "H0P", "H0O", "H0U"):
        upto = {"H0G": "G", "H0P": "P", "H0O": "O", "H0U": "U"}[cx.stop_after]
        fs, fv = _gla_half(cx, 0, rt4, upto=upto)
        _tap(cx, "cumA", A["cumA"][:, :], [128, 8192])
        _tap(cx, "cumR", A["cumR"][:, :], [128, 8192])
        _tap(cx, "sq2", A["sq2"][:, :], [128, 8192])
        _tap(cx, "sk2", A["sk2"][:, :], [128, 8192])
        _tap(cx, "sv2", A["sv2"][:, :], [128, 8192])
        _tap(cx, "og2", A["og2"][:, :], [128, 8192])
        _tap(cx, "Pm", A["Pm"][:, :], [16, 4608])
        _tap(cx, "so2", A["so2"][:, :], [128, 8192])
        _tap(cx, "u2", A["u2"][:, :], [128, 8192])
        _finish(cx, fs, fv)
        return
    sA = _gla_half(cx, 0, rt4)
    if cx.stop_after == "H0":
        _tap(cx, "z1h", dr["z1d"][:, :].rearrange(
            "p (t c) -> p t c", t=8)[:, :, 0:128], [1024, 8, 128])
        _finish(cx, *sA)
        return
    rt5 = cx.stage_boundary(sA)
    sB = _gla_half(cx, 1, rt5)
    _tap(cx, "z1", dr["z1d"][:, :], [1024, 2048])
    if cx.stop_after == "H":
        _finish(cx, *sB)
        return

    rt6 = cx.stage_boundary(sB)
    s_n2 = _norm(cx, "rms2", src_hbm=dr["z1d"], src_sb=None, dst=A["nrm2"],
                 kind="rms", wait=rt6, scale_col=A["sc2"], rots=("gr0", "gr1"))
    s_glu = _glu(cx, (s_n2, 1))
    _tap(cx, "x4", dr["x4d"][:, :], [1024, 2048])
    if cx.stop_after == "I":
        _finish(cx, s_glu, 1)
        return

    rt7 = cx.stage_boundary((s_glu, 1))
    s_out = _ffn(cx, wait=rt7, src_hbm=dr["x4d"], tag="f2", out="yT")
    _finish(cx, s_out, 1)


def _norm(cx, name, src_hbm, src_sb, dst, kind, wait, scale_col=None,
          cc_src=False, y_out=None, rots=("lnr0", "lnr1")):
    nc, A, P = cx.nc, cx.A, cx.P
    rows = A["rrep"]
    rowsb = A["mrep"][:, :].bitcast(BF16)
    s_ld0 = cx.sem_hw(name + "ld0")
    s_ld1 = cx.sem_hw(name + "ld1")
    s_ld = [s_ld0, s_ld1]
    s_sq = cx.sem(name + "sq")
    s_xb = cx.sem(name + "xb")
    s_mm = cx.sem(name + "mm")
    s_mmf = cx.sem(name + "mmf")
    s_rw = cx.sem(name + "rw")
    s_done = cx.sem(name + "dn")
    xr0 = A[rots[0]][:, :]
    xr1 = A[rots[1]][:, :]

    def emit_load(sy, fc, nt, xr, ld_sem):
        if not cc_src:
            sy.dma_start(out=xr[:, nt * 512:(nt + 1) * 512],
                         in_=src_hbm[fc * 128:(fc + 1) * 128,
                                     nt * 512:(nt + 1) * 512]
                         ).then_inc(ld_sem, 16)
        else:
            for d2 in range(2):
                b = nt * 2 + d2
                sy.dma_start(out=xr[:, nt * 512 + d2 * 256:
                                    nt * 512 + (d2 + 1) * 256],
                             in_=src_hbm[b, fc * 128:(fc + 1) * 128, :]
                             ).then_inc(ld_sem, 16)
    ldn = 2 if cc_src else 1

    def src_tile(fc, nt):
        if src_sb is not None:
            return fm(src_sb, fc, nt * 512, (nt + 1) * 512)
        xr = xr0 if fc % 2 == 0 else xr1
        return xr[:, nt * 512:(nt + 1) * 512]

    with nc.Block(name) as b:
        if src_hbm is not None:
            @b.sync
            def _(sy):
                sy.wait_ge(*wait)
                for i, (nt, fc) in enumerate([(nt, fc) for nt in range(4)
                                              for fc in range(8)]):
                    if i >= 2:
                        sy.wait_ge(s_sq, i - 1)
                        sy.wait_ge(s_xb, i - 1)
                    xr = xr0 if fc % 2 == 0 else xr1
                    emit_load(sy, fc, nt, xr, s_ld[i % 2])

        @b.scalar
        def _(s):
            if src_hbm is None:
                s.wait_ge(*wait)
            for i, (nt, fc) in enumerate([(nt, fc) for nt in range(4)
                                          for fc in range(8)]):
                if src_hbm is not None:
                    s.wait_ge(s_ld[i % 2], 16 * ldn * (i // 2 + 1))
                if i >= 2:
                    s.wait_ge(s_mmf, i - 1)
                sq = A["tmpA"] if fc % 2 == 0 else A["tmpB"]
                s.activation(sq[:, :], src_tile(fc, nt), AF.Square).then_inc(s_sq, 1)

        @b.vector
        def _(v):
            if src_hbm is None:
                v.wait_ge(*wait)
            for i, (nt, fc) in enumerate([(nt, fc) for nt in range(4)
                                          for fc in range(8)]):
                if src_hbm is not None:
                    v.wait_ge(s_ld[i % 2], 16 * ldn * (i // 2 + 1))
                if i >= 2:
                    v.wait_ge(s_mmf, i - 1)
                xb = A["tmpC"][:, :] if fc % 2 == 0 else A["tmpD"][:, :]
                v.tensor_copy(xb, src_tile(fc, nt)).then_inc(s_xb, 1)

        @b.tensor
        def _(t):
            for nt in range(4):
                psA = [P[6], P[4], P[2], P[0]][nt]
                psB = [P[7], P[5], P[3], P[1]][nt]
                for fc in range(8):
                    i = nt * 8 + fc
                    t.wait_ge(s_sq, i + 1)
                    t.wait_ge(s_xb, i + 1)
                    xb = A["tmpC"][:, :] if fc % 2 == 0 else A["tmpD"][:, :]
                    sq = A["tmpA"] if fc % 2 == 0 else A["tmpB"]
                    if kind == "ln":
                        t.matmul(psA[0:1, :], A["onesbc"][:, :], xb,
                                 start=(fc == 0), stop=(fc == 7))
                    t.matmul(psB[0:1, :], A["onesbc"][:, :], sq[:, :],
                             start=(fc == 0), stop=(fc == 7)).then_inc(s_mmf, 1)

        tF, tG = A["tmpF"], A["tmpG"]
        @b.vector
        def _(v):
            for nt in range(4):
                v.wait_ge(s_mmf, 8 * (nt + 1))
                psA = [P[6], P[4], P[2], P[0]][nt]
                sl = slice(nt * 512, (nt + 1) * 512)
                if kind == "ln":
                    v.tensor_scalar_mul(rows[0:1, sl], psA[0:1, :], 1.0 / D)
            if kind == "ln":
                v.drain()
                for nt in range(4):
                    sl = slice(nt * 512, (nt + 1) * 512)
                    v.tensor_mul(tF[0:1, sl], rows[0:1, sl], rows[0:1, sl])
                v.drain()
                for nt in range(4):
                    psB = [P[7], P[5], P[3], P[1]][nt]
                    sl = slice(nt * 512, (nt + 1) * 512)
                    ins = v.scalar_tensor_tensor(tG[0:1, sl], psB[0:1, :], 1.0 / D,
                                                 tF[0:1, sl], ALU.mult, ALU.subtract)
                    if nt == 3:
                        ins.then_inc(s_rw, 14)
            else:
                v.nop().then_inc(s_rw, 14)

        @b.scalar
        def _(s):
            s.wait_ge(s_rw, 14)
            if kind == "ln":
                for nt in range(4):
                    sl = slice(nt * 512, (nt + 1) * 512)
                    s.activation(tF[0:1, sl], tG[0:1, sl], AF.Ln,
                                 bias=A["epsln"][0:1, 0:1])
                s.drain()
                for nt in range(4):
                    sl = slice(nt * 512, (nt + 1) * 512)
                    ins = s.activation(tG[0:1, sl], tF[0:1, sl], AF.Exp, scale=-0.5)
                    if nt == 3:
                        ins.then_inc(s_rw, 100)
            else:
                for nt in range(4):
                    psB = [P[7], P[5], P[3], P[1]][nt]
                    sl = slice(nt * 512, (nt + 1) * 512)
                    s.activation(tF[0:1, sl], psB[0:1, :], AF.Sqrt, scale=1.0 / D)
                s.drain()
                for nt in range(4):
                    sl = slice(nt * 512, (nt + 1) * 512)
                    s.activation(tG[0:1, sl], tF[0:1, sl], AF.Ln,
                                 bias=A["epsrms"][0:1, 0:1])
                s.drain()
                for nt in range(4):
                    sl = slice(nt * 512, (nt + 1) * 512)
                    ins = s.activation(tF[0:1, sl], tG[0:1, sl], AF.Exp, scale=-1.0)
                    if nt == 3:
                        ins.then_inc(s_rw, 100)

        @b.vector
        def _(v):
            v.wait_ge(s_rw, 114)
            rstd_row = tG if kind == "ln" else tF
            last = None
            for nt in range(4):
                sl = slice(nt * 512, (nt + 1) * 512)
                sl2 = slice(2048 + nt * 512, 2048 + (nt + 1) * 512)
                last = v.tensor_copy(rowsb[0:1, sl2], rstd_row[0:1, sl])
            if kind == "ln":
                for nt in range(4):
                    sl = slice(nt * 512, (nt + 1) * 512)
                    last = v.tensor_mul(rowsb[0:1, sl], rows[0:1, sl],
                                        rstd_row[0:1, sl])
            last.then_inc(s_rw, 100)

        s_rp = cx.sem(name + "rp")
        s_evn = cx.sem(name + "evn")
        @b.tensor
        def _(t):
            t.wait_ge(s_rw, 214)
            for nt in range(4):
                ins = t.matmul(P[nt][:, :], A["onesb"][0:1, 0:128],
                               rowsb[0:1, 2048 + nt * 512:2048 + (nt + 1) * 512],
                               start=True, stop=True)
                if kind == "ln":
                    ins = t.matmul(P[4 + nt][:, :], A["onesb"][0:1, 0:128],
                                   rowsb[0:1, nt * 512:(nt + 1) * 512],
                                   start=True, stop=True)
                ins.then_inc(s_rp, 1)

        @b.vector
        def _(v):
            v.wait_ge(s_rp, 4)
            for nt in range(4):
                sl = slice(nt * 512, (nt + 1) * 512)
                last = v.tensor_copy(A["rrep"][:, sl], P[nt][:, :])
                if kind == "ln":
                    last = v.tensor_copy(A["mrep"][:, sl], P[4 + nt][:, :])
                last.then_inc(s_evn, 1)

    s_ld2a = cx.sem_hw(name + "ld2a")
    s_ld2b = cx.sem_hw(name + "ld2b")
    s_ld2 = [s_ld2a, s_ld2b]
    s_ap = cx.sem(name + "ap")
    s_yo = [cx.sem_hw(name + "yoA"), cx.sem_hw(name + "yoB")]
    with nc.Block(name + "_apply") as b:
        if src_hbm is not None:
            @b.sync
            def _(sy):
                sy.wait_ge(s_evn, 4)
                for i, (fc, nt) in enumerate([(fc, nt) for fc in range(8)
                                              for nt in range(4)]):
                    if i >= 2:
                        sy.wait_ge(s_ap, 2 * (i - 1))
                    xr = xr0 if fc % 2 == 0 else xr1
                    emit_load(sy, fc, nt, xr, s_ld2[i % 2])
                    if y_out is not None and i >= 1:
                        k = i - 1
                        fck, ntk = divmod(k, 4)
                        sy.wait_ge(s_ap, 2 * (k + 1))
                        if k >= 2:
                            sy.wait_ge(s_yo[k % 2], 16 * (k // 2))
                        sy.dma_start(out=y_out[fck * 128:(fck + 1) * 128,
                                               ntk * 512:(ntk + 1) * 512],
                                     in_=A["ytmp"][:, (k % 4) * 512:
                                                   (k % 4) * 512 + 512]
                                     ).then_inc(s_yo[k % 2], 16)
                if y_out is not None:
                    k = 31
                    sy.wait_ge(s_ap, 2 * (k + 1))
                    sy.wait_ge(s_yo[k % 2], 16 * (k // 2))
                    sy.dma_start(out=y_out[(k // 4) * 128:(k // 4 + 1) * 128,
                                           (k % 4) * 512:(k % 4 + 1) * 512],
                                 in_=A["ytmp"][:, (k % 4) * 512:
                                               (k % 4) * 512 + 512]
                                 ).then_inc(s_yo[k % 2], 16)

        @b.vector
        def _(v):
            v.wait_ge(s_evn, 4)
            for i, (fc, nt) in enumerate([(fc, nt) for fc in range(8)
                                          for nt in range(4)]):
                if src_hbm is not None:
                    v.wait_ge(s_ld2[i % 2], 16 * ldn * (i // 2 + 1))
                sl = slice(nt * 512, (nt + 1) * 512)
                xt = src_tile(fc, nt)
                dtile = fm(dst, fc, nt * 512, (nt + 1) * 512)
                tsl = slice((i % 4) * 512, (i % 4) * 512 + 512)
                ttile = A["tmpF"][:, tsl]
                if y_out is not None:
                    if i >= 4:
                        v.wait_ge(s_yo[i % 2], 16 * (i // 2 - 1))
                    yt = A["ytmp"][:, (i % 4) * 512:(i % 4) * 512 + 512]
                    v.tensor_mul(ttile, xt, A["rrep"][:, sl])
                    v.drain()
                    v.tensor_sub(yt, ttile, A["mrep"][:, sl])
                    v.drain()
                    v.tensor_copy(dtile, yt).then_inc(s_ap, 2)
                elif kind == "ln":
                    v.tensor_mul(ttile, xt, A["rrep"][:, sl])
                    v.drain()
                    v.tensor_sub(dtile, ttile, A["mrep"][:, sl]).then_inc(s_ap, 2)
                else:
                    v.tensor_mul(ttile, xt, A["rrep"][:, sl])
                    v.drain()
                    v.tensor_scalar(dtile, ttile,
                                    scale_col[:, fc:fc + 1], None,
                                    ALU.mult).then_inc(s_ap, 2)

    s_fin = cx.sem(name + "fin")
    with nc.Block(name + "_fin") as b:
        @b.vector
        def _(v):
            v.wait_ge(s_ap, 64)
            if y_out is not None:
                v.wait_ge(s_yo[0], 16 * 16)
                v.wait_ge(s_yo[1], 16 * 16)
            v.nop().then_inc(s_fin, 1)
    return s_fin


def _qkv(cx, s_ln):
    nc, A, P, dr = cx.nc, cx.A, cx.P, cx.dr
    vbias = A["PT0"][0:1, 0:1024]
    s_wlA = cx.sem_sw("qkvwlA")
    s_wlB = cx.sem_sw("qkvwlB")
    s_wlC = cx.sem_sw("qkvwlC")
    s_mm = cx.sem("qkvmm")
    s_evQK = cx.sem("qkvevqk")
    s_evV = cx.sem("qkvevv")
    s_done = cx.sem("qkvdn")

    with nc.Block("qkv") as b:
        @b.gpsimd
        def _(g):
            g.wait_ge(s_ln, 1)
            g.dma_start(out=A["wbA"][:, :].rearrange("p (c n) -> p c n", c=8),
                        in_=dr["aWq"][:, :].rearrange("(c p) n -> p c n", p=128)
                        ).then_inc(s_wlA, 16)
            g.dma_start(out=A["wbB2"][:, :].rearrange("p (c n) -> p c n", c=8),
                        in_=dr["aWk"][:, :].rearrange("(c p) n -> p c n", p=128)
                        ).then_inc(s_wlB, 16)
            g.wait_ge(s_evQK, 32)
            g.dma_start(out=A["wbA"][:, :].rearrange("p (c n) -> p c n", c=8),
                        in_=dr["aWvb"][0:1024, :].rearrange("(c p) n -> p c n", p=128)
                        ).then_inc(s_wlA, 16)
            g.dma_start(out=vbias, in_=dr["aWvb"][1024:1025, :]
                        ).then_inc(s_wlC, 16)

        @b.tensor
        def _(t):
            for wi, (wbuf, wsem) in enumerate([(A["wbA"], s_wlA), (A["wbB2"], s_wlB)]):
                t.wait_ge(wsem, 16)
                for oc in range(8):
                    for nt in range(4):
                        idx = wi * 32 + oc * 4 + nt
                        ps = P[idx % 4]
                        if idx >= 4:
                            t.wait_ge(s_evQK, idx - 3)
                        for kc in range(8):
                            ins = t.matmul(ps[:, :],
                                     wbuf[:, kc * 1024 + oc * 128:kc * 1024 + oc * 128 + 128],
                                     fm(A["xhat"], kc, nt * 512, (nt + 1) * 512),
                                     start=(kc == 0), stop=(kc == 7))
                        ins.then_inc(s_mm, 1)
            t.wait_ge(s_wlA, 32)
            t.wait_ge(s_wlC, 16)
            for tc in range(16):
                for half in range(2):
                    vi = tc * 2 + half
                    ps = P[vi % 4]
                    if vi < 4:
                        t.wait_ge(s_evQK, 61 + vi)
                    else:
                        t.wait_ge(s_evV, vi - 3)
                    for kc in range(8):
                        t.matmul(ps[:, :],
                                 fm(A["xhat"], kc, tc * 128, (tc + 1) * 128),
                                 A["wbA"][:, kc * 1024 + half * 512:
                                          kc * 1024 + half * 512 + 512],
                                 start=(kc == 0), stop=False)
                    t.matmul(ps[:, :], A["onesb"][0:1, 0:128],
                             vbias[0:1, half * 512:(half + 1) * 512],
                             start=False, stop=True).then_inc(s_mm, 1)

        @b.scalar
        def _(s):
            for wi, (dstt, biast, scl) in enumerate(
                    [(A["Qfm"], A["biasq"], 0.125), (A["Kfm"], A["biask"], 1.0)]):
                for oc in range(8):
                    for nt in range(4):
                        idx = wi * 32 + oc * 4 + nt
                        s.wait_ge(s_mm, idx + 1)
                        ps = P[idx % 4]
                        s.activation(fm(dstt, oc, nt * 512, (nt + 1) * 512),
                                     ps[:, :], AF.Identity,
                                     bias=biast[:, oc:oc + 1], scale=scl
                                     ).then_inc(s_evQK, 1)

        @b.vector
        def _(v):
            for tc in range(16):
                for half in range(2):
                    idx = 64 + tc * 2 + half
                    v.wait_ge(s_mm, idx + 1)
                    ps = P[idx % 4]
                    dst = A["Vtok"][:, :].rearrange(
                        "p (tcc h e) -> p tcc h e", tcc=16, h=16, e=65)[
                        :, tc, half * 8:(half + 1) * 8, 0:64]
                    v.tensor_copy(dst, ps[:, :].rearrange("p (h e) -> p h e", h=8)
                                  ).then_inc(s_evV, 1)
            v.wait_ge(s_evQK, 64)
            v.wait_ge(s_evV, 32)
            v.nop().then_inc(s_done, 1)
    return s_done


def _attention(cx, s_qkv):
    nc, A, P, dr = cx.nc, cx.A, cx.P, cx.dr
    sem = cx.sem
    s_red = sem("ared")
    s_p1t = sem("ap1t")
    s_mean = sem("amean")
    s_mev = sem("amev")
    s_spr = sem("aspr")
    s_cv1 = sem("acv1"); s_ct1 = sem("act1")
    s_cv2 = sem("acv2"); s_ct2 = sem("act2"); s_cv3 = sem("acv3")
    s_p2t = sem("ap2t")
    s_exp = sem("aexp")
    s_avt = sem("aavt")
    s_avq = sem("aavq")
    s_ave = sem("aave")
    s_kb = sem("akb")
    s_bv = sem("abv"); s_bt1 = sem("abt1"); s_bv2 = sem("abv2")
    s_bt2 = sem("abt2"); s_bpred = sem("abpred")
    s_msk = sem("amsk")
    s_tev = sem("atev")
    s_ttr = sem("attr")
    s_wo = cx.sem_sw("awo")
    s_xr = [cx.sem_hw("axr0"), cx.sem_hw("axr1")]
    s_womm = sem("awomm")
    s_woev = sem("awoev")
    s_fin = sem("afin")
    NB = 36

    Qf, Kf, Vt = A["Qfm"], A["Kfm"], A["Vtok"]
    PT = [A["PT0"], A["PT1"]]
    gt = A["gtmp"]
    bx = A["bisx"]

    def qslice(h, a, b):
        return Qf[(h % 2) * 64:(h % 2) * 64 + 64,
                  (h // 2) * 2048 + a:(h // 2) * 2048 + b]

    def kslice(h, a, b):
        return Kf[(h % 2) * 64:(h % 2) * 64 + 64,
                  (h // 2) * 2048 + a:(h // 2) * 2048 + b]

    with nc.Block("attn") as b:
        @b.gpsimd
        def _(g):
            g.wait_ge(s_qkv, 1)
            g.dma_start(out=A["wbX"][:, :].rearrange("p (c n) -> p c n", c=8),
                        in_=dr["aWo"][:, :].rearrange("(c p) n -> p c n", p=128)
                        ).then_inc(s_wo, 16)

        s_wod = [cx.sem_hw("awod0"), cx.sem_hw("awod1"), cx.sem_hw("awod2"), cx.sem_hw("awod3")]
        @b.sync
        def _(sy):
            for i in range(33):
                if i < 32:
                    oc, nt = divmod(i, 4)
                    if i >= 2:
                        sy.wait_ge(s_woev, i - 1)
                    xr = A["worot0"] if i % 2 == 0 else A["worot1"]
                    sy.dma_start(out=xr[:, :],
                                 in_=dr["xT"][oc * 128:(oc + 1) * 128,
                                              nt * 512:(nt + 1) * 512]
                                 ).then_inc(s_xr[i % 2], 16)
                if i >= 1:
                    j = i - 1
                    oc2, nt2 = divmod(j, 4)
                    sy.wait_ge(s_woev, j + 1)
                    sy.dma_start(out=dr["x1d"][oc2 * 128:(oc2 + 1) * 128,
                                               nt2 * 512:(nt2 + 1) * 512],
                                 in_=A["ytmp"][:, (j % 4) * 512:
                                               (j % 4) * 512 + 512]
                                 ).then_inc(s_wod[j % 4], 16)
            for q in range(4):
                sy.wait_ge(s_wod[q], 8 * 16)
            sy.nop().then_inc(s_fin, 1)

        @b.tensor
        def _(t):
            t.wait_ge(s_qkv, 1)

            def pass2av_qt(h, qt):
                gq = h * 8 + qt
                for j in range(9):
                    gp = h * 64 + qt * 8 + j
                    if j < 8:
                        if gp >= 1:
                            t.wait_ge(s_exp, gp)
                        t.matmul(P[6][:, 0:256], kslice(h, 2 * j * 128,
                                                        2 * j * 128 + 128),
                                 qslice(h, qt * 256, (qt + 1) * 256),
                                 start=True, stop=True)
                        t.matmul(P[6][:, 256:512],
                                 kslice(h, (2 * j + 1) * 128,
                                        (2 * j + 1) * 128 + 128),
                                 qslice(h, qt * 256, (qt + 1) * 256),
                                 start=False, stop=True,
                                 skip_group_check=True).then_inc(s_p2t, 1)
                    if j >= 1:
                        jj = j - 1
                        t.wait_ge(s_exp, h * 64 + qt * 8 + jj + 1)
                        if jj == 0 and gq >= 1:
                            t.wait_ge(s_ave, gq)
                        ins = None
                        for kk2 in range(2):
                            kk = 2 * jj + kk2
                            for qc2 in range(2):
                                ins = t.matmul(
                                    P[7][:, qc2 * 256:qc2 * 256 + 65],
                                    PT[qt % 2][:, kk * 256 + qc2 * 128:
                                                kk * 256 + qc2 * 128 + 128],
                                    Vt[:, kk * 1040 + h * 65:
                                       kk * 1040 + h * 65 + 65],
                                    start=(kk == 0 and qc2 == 0), stop=True,
                                    skip_group_check=True)
                        if jj == 7:
                            ins.then_inc(s_avq, 1)
                        else:
                            ins.then_inc(s_avt, 1)

            def bis_unit(k):
                t.wait_ge(s_bv, k + 1)
                t.matmul(P[3][0:1, 0:16], A["onesf"][:, :], bx[:, 96:112],
                         start=True, stop=True).then_inc(s_bt1, 1)
                t.wait_ge(s_bv2, k + 1)
                t.matmul(P[3][:, 16:32], A["onesfr"][0:1, :],
                         A["cntrow"][0:1, :],
                         start=True, stop=True).then_inc(s_bt2, 1)

            def cchain_t(h):
                t.wait_ge(s_cv1, h + 1)
                t.matmul(P[3][0:1, 0:128], A["cbcol"][:, 0:1],
                         A["identb"][:, :], start=True, stop=True
                         ).then_inc(s_ct1, 1)
                t.wait_ge(s_cv2, h + 1)
                t.matmul(P[3][:, 128:129], A["onesb"][0:1, :],
                         A["c1b"][0:1, 0:1], start=True, stop=True
                         ).then_inc(s_ct2, 1)

            bis_done = 0
            for h in range(16):
                for qc in range(16):
                    if qc == 4 and h >= 1:
                        cchain_t(h - 1)
                    ti0 = (h * 16 + qc) * 4
                    for nt in range(4):
                        ti = ti0 + nt
                        if ti >= 3:
                            t.wait_ge(s_red, ti - 2)
                        t.matmul(P[ti % 3][:, :], qslice(h, qc * 128,
                                                         (qc + 1) * 128),
                                 kslice(h, nt * 512, (nt + 1) * 512),
                                 start=True, stop=True).then_inc(s_p1t, 1)
                    gqc = h * 16 + qc
                    if gqc >= 2:
                        t.wait_ge(s_mev, gqc - 1)
                    if h == 0 and qc == 0:
                        t.wait_ge(s_kb, 1)
                    t.matmul(P[4 + gqc % 2][:, 0:1], qslice(h, qc * 128,
                                                            (qc + 1) * 128),
                             A["kbarb"][(h % 2) * 64:(h % 2) * 64 + 64,
                                        h:h + 1],
                             start=True, stop=True).then_inc(s_mean, 1)
                    if h >= 1 and 6 <= qc <= 13:
                        pass2av_qt(h - 1, qc - 6)
            cchain_t(15)
            for qt in range(8):
                pass2av_qt(15, qt)
            while bis_done < NB:
                bis_unit(bis_done); bis_done += 1

            for tc in range(16):
                t.wait_ge(s_msk, 16)
                for gjj in range(2):
                    gi = tc * 2 + gjj
                    if gi >= 2:
                        t.wait_ge(s_tev, gi - 1)
                    ins = None
                    for j4 in range(4):
                        j = gjj * 4 + j4
                        ins = t.matmul(P[6 + gi % 2][:, j4 * 128:(j4 + 1) * 128],
                                       A["cmask"][:, tc * 1024 + j * 128:
                                                  tc * 1024 + j * 128 + 128],
                                       A["identb"][:, :],
                                       start=(j4 == 0), stop=True,
                                       skip_group_check=True)
                    ins.then_inc(s_ttr, 1)

            t.wait_ge(s_wo, 16)
            for i in range(32):
                oc, nt = divmod(i, 4)
                ps = P[i % 3]
                if i >= 3:
                    t.wait_ge(s_woev, i - 2)
                ins = None
                for kc in range(8):
                    ins = t.matmul(ps[:, :],
                                   A["wbX"][:, kc * 1024 + oc * 128:
                                            kc * 1024 + oc * 128 + 128],
                                   A["cfm"][:, kc * 2048 + nt * 512:
                                            kc * 2048 + (nt + 1) * 512],
                                   start=(kc == 0), stop=(kc == 7))
                ins.then_inc(s_womm, 1)

        @b.scalar
        def _(s):
            for h in range(16):
                s.wait_ge(s_cv3, h + 1)
                for qt in range(8):
                    for j in range(8):
                        gp = h * 64 + qt * 8 + j
                        s.wait_ge(s_p2t, gp + 1)
                        if h * 8 + qt >= 2 and j == 0:
                            s.wait_ge(s_avt, 7 * (h * 8 + qt - 1))
                            s.wait_ge(s_avq, h * 8 + qt - 1)
                        s.activation(PT[qt % 2][:, j * 512:(j + 1) * 512],
                                     P[6][:, :], AF.Exp,
                                     bias=A["negC"][:, h:h + 1]
                                     ).then_inc(s_exp, 1)

        @b.vector
        def _(v):
            v.wait_ge(s_qkv, 1)
            v.memset(A["kbar"][:, :], 0.0)
            v.drain()
            for h in range(16):
                r0 = (h % 2) * 64
                v.reduce_sum(A["kbar"][r0:r0 + 64, h:h + 1],
                             kslice(h, 0, 2048), axis=AX.X)
            v.drain()
            ins = v.tensor_copy(A["kbarb"][:, :], A["kbar"][:, :])
            ins.then_inc(s_kb, 1)


            def do_stage2(gqc):
                p = gqc % 3
                v.tensor_max(gt[:, 12 + 2 * p:13 + 2 * p],
                             gt[:, 4 * p + 0:4 * p + 1],
                             gt[:, 4 * p + 1:4 * p + 2])
                v.tensor_max(gt[:, 13 + 2 * p:14 + 2 * p],
                             gt[:, 4 * p + 2:4 * p + 3],
                             gt[:, 4 * p + 3:4 * p + 4])

            def do_stage3(gqc):
                p = gqc % 3
                v.tensor_max(A["m_all"][:, gqc:gqc + 1],
                             gt[:, 12 + 2 * p:13 + 2 * p],
                             gt[:, 13 + 2 * p:14 + 2 * p])

            def do_stage4(gqc):
                v.scalar_tensor_tensor(A["sprs"][:, gqc:gqc + 1],
                                       A["meancols"][:, gqc:gqc + 1],
                                       -1.0 / 2048,
                                       A["m_all"][:, gqc:gqc + 1],
                                       ALU.mult, ALU.add).then_inc(s_spr, 1)

            def do_mev(gqc):
                v.wait_ge(s_mean, gqc + 1)
                v.tensor_copy(A["meancols"][:, gqc:gqc + 1],
                              P[4 + gqc % 2][:, 0:1]).then_inc(s_mev, 1)

            def do_cchain(h):
                v.reduce_max(gt[:, 18:19], A["m_all"][:, h * 16:(h + 1) * 16],
                             axis=AX.X)
                v.drain()
                v.tensor_copy(A["cbcol"][:, 0:1], gt[:, 18:19]).then_inc(s_cv1, 1)
                v.wait_ge(s_ct1, h + 1)
                v.reduce_max(A["c16"][0:1, 0:1], P[3][0:1, 0:128], axis=AX.X)
                v.drain()
                v.tensor_copy(A["c1b"][0:1, 0:1], A["c16"][0:1, 0:1]
                              ).then_inc(s_cv2, 1)
                v.wait_ge(s_ct2, h + 1)
                v.tensor_scalar_mul(A["negC"][:, h:h + 1], P[3][:, 128:129],
                                    -1.0).then_inc(s_cv3, 1)

            def do_epi(h, qt):
                gq = h * 8 + qt
                v.wait_ge(s_avq, gq + 1)
                p = gq % 2
                v.reciprocal(gt[:, 20 + 2 * p:21 + 2 * p], P[7][:, 64:65])
                v.reciprocal(gt[:, 21 + 2 * p:22 + 2 * p], P[7][:, 320:321])
                v.drain()
                v.tensor_scalar(A["cct"][:, (qt * 2) * 1024 + h * 64:
                                         (qt * 2) * 1024 + h * 64 + 64],
                                P[7][:, 0:64], gt[:, 20 + 2 * p:21 + 2 * p],
                                None, ALU.mult)
                v.tensor_scalar(A["cct"][:, (qt * 2 + 1) * 1024 + h * 64:
                                         (qt * 2 + 1) * 1024 + h * 64 + 64],
                                P[7][:, 256:320],
                                gt[:, 21 + 2 * p:22 + 2 * p],
                                None, ALU.mult).then_inc(s_ave, 1)

            for r in range(261):
                if r >= 255:
                    v.drain()
                if r >= 23 and (r - 7) % 16 <= 7 and 1 <= (r - 7) // 16 <= 15:
                    do_epi((r - 7) // 16 - 1, (r - 7) % 16)
                if 1 <= r and r - 1 < 256:
                    do_mev(r - 1)
                if 4 <= r and r - 4 < 256:
                    do_stage4(r - 4)
                if 3 <= r and r - 3 < 256:
                    do_stage3(r - 3)
                if 2 <= r and r - 2 < 256:
                    do_stage2(r - 2)
                if r < 256:
                    p = r % 3
                    for nt in range(4):
                        ti = r * 4 + nt
                        v.wait_ge(s_p1t, ti + 1)
                        v.reduce_max(gt[:, 4 * p + nt:4 * p + nt + 1],
                                     P[ti % 3][:, :], axis=AX.X
                                     ).then_inc(s_red, 1)
                if r >= 19 and (r - 19) % 16 == 0:
                    do_cchain((r - 19) // 16)

            for qt in range(8):
                do_epi(15, qt)

            v.memset(bx[:, 0:16], 0.0)
            v.memset(bx[:, 16:32], 1024.0)
            v.drain()
            for k in range(NB):
                lo = bx[:, 32 * (k % 2):32 * (k % 2) + 16]
                hi = bx[:, 32 * (k % 2) + 16:32 * (k % 2) + 32]
                lon = bx[:, 32 * ((k + 1) % 2):32 * ((k + 1) % 2) + 16]
                hin = bx[:, 32 * ((k + 1) % 2) + 16:32 * ((k + 1) % 2) + 32]
                mid = bx[:, 64:80]; pred = bx[:, 80:96]
                t1 = bx[:, 96:112]; u = bx[:, 112:128]
                v.tensor_add(t1, lo, hi)
                v.drain()
                v.tensor_scalar_mul(mid, t1, 0.5)
                v.drain()
                sp3 = A["sprs"][:, :].rearrange("p (hh q) -> p hh q", hh=16)
                v.tensor_tensor(A["amask"][:, :].rearrange(
                    "p (hh q) -> p hh q", hh=16), sp3,
                    mid.unsqueeze(2).broadcast_to([128, 16, 16]), ALU.is_ge)
                v.drain()
                v.reduce_sum(bx[:, 96:112].unsqueeze(2),
                             A["amask"][:, :].rearrange(
                                 "p (hh q) -> p hh q", hh=16),
                             axis=AX.X).then_inc(s_bv, 1)
                v.wait_ge(s_bt1, k + 1)
                v.tensor_copy(A["cntrow"][0:1, :], P[3][0:1, 0:16]
                              ).then_inc(s_bv2, 1)
                v.wait_ge(s_bt2, k + 1)
                v.tensor_single_scalar(pred, P[3][:, 16:32], float(U), ALU.is_ge)
                v.drain()
                v.tensor_sub(t1, mid, lo)
                v.tensor_sub(u, mid, hi)
                v.drain()
                v.tensor_mul(bx[:, 128:144], pred, t1)
                v.tensor_mul(bx[:, 144:160], pred, u)
                v.drain()
                v.tensor_add(lon, lo, bx[:, 128:144])
                v.tensor_sub(hin, mid, bx[:, 144:160])
                v.drain()
            lof = bx[:, 32 * (NB % 2):32 * (NB % 2) + 16]
            sp3 = A["sprs"][:, :].rearrange("p (hh q) -> p hh q", hh=16)
            v.tensor_tensor(A["amask"][:, :].rearrange(
                "p (hh q) -> p hh q", hh=16), sp3,
                lof.unsqueeze(2).broadcast_to([128, 16, 16]), ALU.is_ge)
            v.drain()

            for tc in range(16):
                if tc == 0:
                    v.wait_ge(s_ave, 128)
                msk3 = A["amask"][:, :].rearrange(
                    "p (hh q) -> p hh q", hh=16)[:, :, tc:tc + 1]
                v.tensor_tensor(
                    A["cmask"][:, tc * 1024:(tc + 1) * 1024].rearrange(
                        "p (hh e) -> p hh e", hh=16),
                    A["cct"][:, tc * 1024:(tc + 1) * 1024].rearrange(
                        "p (hh e) -> p hh e", hh=16),
                    msk3.broadcast_to([128, 16, 64]), ALU.mult
                ).then_inc(s_msk, 1)

            for gi in range(32):
                tc, gjj = divmod(gi, 2)
                v.wait_ge(s_ttr, gi + 1)
                dst = A["cfm"][:, :].rearrange(
                    "p (c tt) -> p c tt", c=8)[:, gjj * 4:(gjj + 1) * 4,
                                               tc * 128:(tc + 1) * 128]
                v.tensor_copy(dst, P[6 + gi % 2][:, :].rearrange(
                    "p (j e) -> p j e", j=4)).then_inc(s_tev, 1)

            for i in range(32):
                oc, nt = divmod(i, 4)
                v.wait_ge(s_womm, i + 1)
                v.wait_ge(s_xr[i % 2], 16 * (i // 2 + 1))
                if i >= 4:
                    v.wait_ge(s_wod[i % 4], 16 * (i // 4))
                ps = P[i % 3]
                xr = A["worot0"] if i % 2 == 0 else A["worot1"]
                v.scalar_tensor_tensor(A["ytmp"][:, (i % 4) * 512:
                                                 (i % 4) * 512 + 512],
                                       ps[:, :], A["biaso"][:, oc:oc + 1],
                                       xr[:, :], ALU.add, ALU.add
                                       ).then_inc(s_woev, 1)
    return s_fin


GLA_PAIRS = [(t, sp) for t in range(8) for sp in range(t + 1)]


def _gla_half(cx, half, wait, upto=None):
    nc, A, P, dr = cx.nc, cx.A, cx.P, cx.dr
    H = half
    hx = H * 128

    if cx.gla_sw is None:
        cx.gla_sw = [[cx.sem_sw(f"glawl{i}"), 0] for i in range(3)]
    GS0, GS1, GS2 = cx.gla_sw
    b0, b1, b2 = GS0[1], GS1[1], GS2[1]

    sem = cx.sem
    s_slf = sem("slf")
    s_gh = sem("gh"); s_ghe = sem("ghe"); s_ghv = sem("ghv")
    s_ght = sem("ght"); s_ghtc = sem("ghtc")
    s_zg = sem("zg"); s_ze = sem("ze")
    s_sgb = sem("sgb"); s_ra = sem("ra"); s_rac = sem("rac")
    s_exb = sem("exb"); s_rr = sem("rr"); s_rrc = sem("rrc")
    s_pq = sem("pq"); s_peq = sem("peq"); s_pev2 = sem("pev2")
    s_peo = sem("peo"); s_oge = sem("oge")
    s_pm = sem("pm"); s_pp = sem("pp"); s_pev = sem("pev")
    s_ob = sem("ob"); s_ov = sem("ov"); s_oa = sem("oa")
    s_osq = sem("osq"); s_m1 = sem("m1"); s_m2 = sem("m2")
    s_mre = sem("mre"); s_rwm = sem("rwm"); s_rst = sem("rst")
    s_ms = sem("ms"); s_rb = sem("rb")
    s_orp = sem("orp"); s_apv = sem("apv")
    s_wm = sem("wm"); s_zev = sem("zev")
    s_yl = [cx.sem_hw("yla"), cx.sem_hw("ylb")]
    s_zd = [cx.sem_hw("zda"), cx.sem_hw("zdb")]
    s_fin = sem("gfin")

    xhat4 = A["xhat"][:, :].rearrange("p (f t c) -> p f t c", f=8, t=8)

    def sqkv_ap(t8, which):
        return which[:, :].rearrange("p (f t c) -> p f t c", f=8, t=8)[
            :, :, t8, :]

    GS0[1] = b0 + 80
    GS1[1] = b1 + 48
    GS2[1] = b2 + 32

    if H == 0:
        with nc.Block("glaself") as b:
            @b.vector
            def _(v):
                v.wait_ge(*wait)
                v.memset(A["selfP"][:, :], 0.0)
                v.memset(A["ones5"][0:1, :], 1.0)
                v.drain()
                last = None
                for fc in range(8):
                    v.memset(A["selfP"][0:64, fc * 16 + 2 * fc:
                                        fc * 16 + 2 * fc + 1], 1.0)
                    last = v.memset(A["selfP"][64:128, fc * 16 + 2 * fc + 1:
                                               fc * 16 + 2 * fc + 2], 1.0)
                last.then_inc(s_slf, 1)

            @b.tensor
            def _(t):
                for fc in range(8):
                    t.wait_ge(s_slf, 1 + 2 * fc)
                    t.matmul(P[6][0:16, 0:128], A["selfP"][:, fc * 16:
                                                           fc * 16 + 16],
                             A["identb"][:, :], start=True, stop=True
                             ).then_inc(s_slf, 1)

            @b.scalar
            def _(sc):
                for fc in range(8):
                    sc.wait_ge(s_slf, 2 + 2 * fc)
                    sc.activation(A["selfT"][:, fc * 128:(fc + 1) * 128],
                                  P[6][0:16, 0:128], AF.Copy
                                  ).then_inc(s_slf, 1)
        slf_tok = (s_slf, 17)
    else:
        slf_tok = wait

    with nc.Block(f"glaG{H}") as b:
        @b.gpsimd
        def _(g):
            g.wait_ge(*wait)
            g.dma_start(out=A["wg2s"][:, :], in_=dr["bWg2Tb"][0:16, :]
                        ).then_inc(GS0[0], 16)
            g.dma_start(out=A["bg2r"][:, :], in_=dr["bWg2Tb"][16:17, :]
                        ).then_inc(GS0[0], 16)
            g.dma_start(out=A["wg1s"][:, :].rearrange("p (c n) -> p c n", c=8),
                        in_=dr["bWg1T"][:, :].rearrange("(c p) n -> p c n",
                                                        p=128)
                        ).then_inc(GS0[0], 16)
            g.dma_start(out=A["bg1r"][:, :], in_=dr["bBg1r"][:, :]
                        ).then_inc(GS0[0], 16)
            g.dma_start(out=A["gwA"][:, :].rearrange("p (c n) -> p c n", c=8),
                        in_=dr["bWqT"][:, :].rearrange("(c p) n -> p c n",
                                                       p=128)
                        ).then_inc(GS1[0], 16)
            g.wait_ge(s_zg, 16)
            g.dma_start(out=A["gwB"][:, :].rearrange("p (c n) -> p c n", c=8),
                        in_=dr["bWkT"][:, :].rearrange("(c p) n -> p c n",
                                                       p=128)
                        ).then_inc(GS0[0], 16)

        @b.tensor
        def _(t):
            t.wait_ge(*wait)
            t.wait_ge(GS0[0], b0 + 64)
            for t8 in range(8):
                if t8 >= 1:
                    t.wait_ge(s_ghv, t8)
                for kc in range(8):
                    t.matmul(P[5][:, 0:16],
                             fm(A["xhat"], kc, t8 * 256 + hx,
                                t8 * 256 + hx + 128),
                             A["wg1s"][:, kc * 16:kc * 16 + 16],
                             start=(kc == 0), stop=False)
                t.matmul(P[5][:, 0:16], A["onesb"][0:1, 0:128],
                         A["bg1r"][0:1, :], start=False, stop=True
                         ).then_inc(s_gh, 1)
            for t8 in range(8):
                t.wait_ge(s_ghv, t8 + 1)
                if t8 >= 1:
                    t.wait_ge(s_ghtc, t8)
                t.matmul(P[6][0:16, 0:128], A["ghb"][:, t8 * 16:t8 * 16 + 16],
                         A["identb"][:, :], start=True, stop=True
                         ).then_inc(s_ght, 1)
            t.wait_ge(s_ghtc, 8)
            for jj in range(2):
                for fc in range(8):
                    zi = jj * 8 + fc
                    ps = P[zi % 4]
                    if zi >= 4:
                        t.wait_ge(s_ze, zi - 3)
                    t.matmul(ps[:, :], A["wg2s"][0:16, fc * 128:
                                                 (fc + 1) * 128],
                             A["ghT"][0:16, jj * 512:(jj + 1) * 512],
                             start=True, stop=False)
                    t.matmul(ps[:, :], A["bg2r"][0:1, fc * 128:
                                                 (fc + 1) * 128],
                             A["ones5"][0:1, 0:512], start=False, stop=True
                             ).then_inc(s_zg, 1)

        @b.scalar
        def _(sc):
            sc.wait_ge(*slf_tok)
            for t8 in range(8):
                sc.wait_ge(s_gh, t8 + 1)
                if t8 >= 1:
                    sc.wait_ge(s_ghv, t8)
                sc.activation(A["sg16"][:, :], P[5][:, 0:16], AF.Sigmoid
                              ).then_inc(s_ghe, 1)
            for t8 in range(8):
                sc.wait_ge(s_ght, t8 + 1)
                sc.activation(A["ghT"][:, t8 * 128:(t8 + 1) * 128],
                              P[6][0:16, 0:128], AF.Copy).then_inc(s_ghtc, 1)
            for jj in range(2):
                for fc in range(8):
                    zi = jj * 8 + fc
                    sc.wait_ge(s_zg, zi + 1)
                    sc.activation(A["zgs"][:, fc * 1024 + jj * 512:
                                           fc * 1024 + jj * 512 + 512],
                                  P[zi % 4][:, :], AF.Copy).then_inc(s_ze, 1)
            sc.wait_ge(s_ze, 16)
            for i in range(4):
                sc.activation(A["asb"][:, i * 2048:(i + 1) * 2048],
                              A["zgs"][:, i * 2048:(i + 1) * 2048],
                              AF.Sigmoid, scale=1.0 / 16.0).then_inc(s_sgb, 1)
            for t8 in range(8):
                sc.wait_ge(s_ra, (t8 + 1) * 8)
                ar = A["aroll0"] if t8 % 2 == 0 else A["aroll1"]
                sc.activation(A["cumA"][:, :].rearrange(
                                  "p (f t c) -> p f t c", f=8, t=8)[:, :, t8, :],
                              ar[:, :].rearrange("p (f c) -> p f c", f=8),
                              AF.Copy).then_inc(s_rac, 1)
            sc.wait_ge(s_ra, 64)
            for i in range(4):
                sc.activation(A["asb"][:, i * 2048:(i + 1) * 2048],
                              A["zgs"][:, i * 2048:(i + 1) * 2048],
                              AF.Exp, scale=-1.0 / 16.0).then_inc(s_exb, 1)
            for t8 in range(8):
                sc.wait_ge(s_rr, (t8 + 1) * 8)
                rr = A["rroll0"] if t8 % 2 == 0 else A["rroll1"]
                sc.activation(A["cumR"][:, :].rearrange(
                                  "p (f t c) -> p f t c", f=8, t=8)[:, :, t8, :],
                              rr[:, :].rearrange("p (f c) -> p f c", f=8),
                              AF.Copy).then_inc(s_rrc, 1)

        @b.vector
        def _(v):
            if H == 0:
                v.wait_ge(*slf_tok)
            else:
                v.wait_ge(*wait)
            for t8 in range(8):
                v.wait_ge(s_ghe, t8 + 1)
                v.tensor_mul(A["ghb"][:, t8 * 16:t8 * 16 + 16],
                             P[5][:, 0:16], A["sg16"][:, :]).then_inc(s_ghv, 1)
            v.wait_ge(s_sgb, 4)
            for t8 in range(8):
                dst = A["aroll0"] if t8 % 2 == 0 else A["aroll1"]
                src = A["aroll1"] if t8 % 2 == 0 else A["aroll0"]
                if t8 >= 2:
                    v.wait_ge(s_rac, t8 - 1)
                for fc in range(8):
                    asl = A["asb"][:, fc * 1024 + t8 * 128:
                                   fc * 1024 + t8 * 128 + 128]
                    if t8 == 0:
                        v.tensor_copy(dst[:, fc * 128:(fc + 1) * 128], asl
                                      ).then_inc(s_ra, 1)
                    else:
                        v.tensor_mul(dst[:, fc * 128:(fc + 1) * 128],
                                     src[:, fc * 128:(fc + 1) * 128], asl
                                     ).then_inc(s_ra, 1)
                v.drain()
            v.wait_ge(s_exb, 4)
            for t8 in range(8):
                dst = A["rroll0"] if t8 % 2 == 0 else A["rroll1"]
                src = A["rroll1"] if t8 % 2 == 0 else A["rroll0"]
                if t8 >= 2:
                    v.wait_ge(s_rrc, t8 - 1)
                for fc in range(8):
                    asl = A["asb"][:, fc * 1024 + t8 * 128:
                                   fc * 1024 + t8 * 128 + 128]
                    if t8 == 0:
                        v.tensor_scalar(dst[:, fc * 128:(fc + 1) * 128],
                                        asl, 1.0, None, ALU.add
                                        ).then_inc(s_rr, 1)
                    else:
                        v.scalar_tensor_tensor(
                            dst[:, fc * 128:(fc + 1) * 128], asl, 1.0,
                            src[:, fc * 128:(fc + 1) * 128],
                            ALU.add, ALU.mult).then_inc(s_rr, 1)
                v.drain()

    if upto == "G":
        return s_rrc, 8

    wsel = [("q", A["gwA"], (GS1, b1 + 16)), ("k", A["gwB"], (GS0, b0 + 80)),
            ("v", A["gwA"], (GS1, b1 + 32)), ("og", A["gwB"], (GS2, b2 + 32))]
    with nc.Block(f"glaP{H}") as b:
        @b.gpsimd
        def _(g):
            g.wait_ge(s_pq, 16)
            g.dma_start(out=A["gwA"][:, :].rearrange("p (c n) -> p c n", c=8),
                        in_=dr["bWvT"][:, :].rearrange("(c p) n -> p c n",
                                                       p=128)
                        ).then_inc(GS1[0], 16)
            g.wait_ge(s_pq, 32)
            g.wait_ge(s_rr, 64)
            g.dma_start(out=A["ogbr"][:, :], in_=dr["bWogTb"][1024:1025, :]
                        ).then_inc(GS2[0], 16)
            g.dma_start(out=A["gwB"][:, :].rearrange("p (c n) -> p c n", c=8),
                        in_=dr["bWogTb"][0:1024, :].rearrange(
                            "(c p) n -> p c n", p=128)
                        ).then_inc(GS2[0], 16)
            g.wait_ge(s_pq, 48)
            g.dma_start(out=A["gwA"][:, :].rearrange("p (c n) -> p c n", c=8),
                        in_=dr["bWoT"][:, :].rearrange("(c p) n -> p c n",
                                                       p=128)
                        ).then_inc(GS1[0], 16)

        @b.tensor
        def _(t):
            for wi, (wn, wbuf, wcnt) in enumerate(wsel):
                t.wait_ge(wcnt[0][0], wcnt[1])
                if wn == "q":
                    t.wait_ge(s_rac, 8)
                if wn == "k":
                    t.wait_ge(s_rrc, 8)
                    t.wait_ge(s_peq, 16)
                if wn == "v":
                    t.wait_ge(s_exb, 4)
                    t.wait_ge(s_rr, 64)
                    t.wait_ge(s_peq, 32)
                if wn == "og":
                    t.wait_ge(s_pev2, 16)
                for jb in range(2):
                    for fo in range(8):
                        gi = wi * 16 + jb * 8 + fo
                        j = jb * 8 + fo
                        ps = P[gi % 4]
                        if j >= 4:
                            esem = {"q": s_peq, "k": s_peq, "v": s_pev2,
                                    "og": s_peo}[wn]
                            ebase = 16 if wn == "k" else 0
                            t.wait_ge(esem, ebase + j - 3)
                        for kc in range(8):
                            ins = t.matmul(ps[:, :],
                                           wbuf[:, kc * 1024 + fo * 128:
                                                kc * 1024 + fo * 128 + 128],
                                           xhat4[:, kc, 4 * jb:4 * jb + 4,
                                                 hx:hx + 128],
                                           start=(kc == 0),
                                           stop=(kc == 7 and wn != "og"))
                        if wn == "og":
                            ins = t.matmul(ps[:, :],
                                           A["ogbr"][0:1, fo * 128:
                                                     fo * 128 + 128],
                                           A["ones5"][0:1, 0:512],
                                           start=False, stop=True)
                        ins.then_inc(s_pq, 1)

        @b.scalar
        def _(sc):
            for jb in range(2):
                for fo in range(8):
                    gi = 32 + jb * 8 + fo
                    sc.wait_ge(s_pq, gi + 1)
                    sc.activation(A["sv2"][:, fo * 1024 + jb * 512:
                                           fo * 1024 + jb * 512 + 512],
                                  P[gi % 4][:, :], AF.Copy).then_inc(s_pev2, 1)
            for jb in range(2):
                for fo in range(8):
                    i = jb * 8 + fo
                    gi = 48 + i
                    sc.wait_ge(s_pq, gi + 1)
                    if i >= 1:
                        sc.wait_ge(s_peo, i)
                    sc.activation(A["ogstmp"][:, :], P[gi % 4][:, :],
                                  AF.Sigmoid).then_inc(s_oge, 1)

        @b.vector
        def _(v):
            for wi, wn in [(0, "q"), (1, "k")]:
                cum = A["cumA"] if wn == "q" else A["cumR"]
                dst = A["sq2"] if wn == "q" else A["sk2"]
                for jb in range(2):
                    for fo in range(8):
                        gi = wi * 16 + jb * 8 + fo
                        v.wait_ge(s_pq, gi + 1)
                        sl = slice(fo * 1024 + jb * 512,
                                   fo * 1024 + jb * 512 + 512)
                        v.tensor_mul(dst[:, sl], P[gi % 4][:, :], cum[:, sl]
                                     ).then_inc(s_peq, 1)
            for jb in range(2):
                for fo in range(8):
                    i = jb * 8 + fo
                    gi = 48 + i
                    v.wait_ge(s_oge, i + 1)
                    sl = slice(fo * 1024 + jb * 512,
                               fo * 1024 + jb * 512 + 512)
                    v.tensor_mul(A["og2"][:, sl], P[gi % 4][:, :],
                                 A["ogstmp"][:, :]).then_inc(s_peo, 1)

    if upto == "P":
        return s_peo, 16

    with nc.Block(f"glaQ{H}") as b:
        @b.vector
        def _(v):
            v.wait_ge(s_peq, 32)
            for idx, (t8, sp) in enumerate(GLA_PAIRS):
                mp = A["mtm"] if idx % 2 == 0 else A["mtm2"]
                if idx >= 2:
                    v.wait_ge(s_pp, 8 * (idx - 1))
                v.tensor_mul(mp[:, :].rearrange("p (f c) -> p f c", f=8),
                             sqkv_ap(t8, A["sq2"]), sqkv_ap(sp, A["sk2"])
                             ).then_inc(s_pm, 1)

        @b.tensor
        def _(t):
            for idx in range(36):
                t.wait_ge(s_pm, idx + 1)
                if idx % 4 == 0 and idx >= 4:
                    t.wait_ge(s_pev, idx // 4)
                mp = A["mtm"] if idx % 2 == 0 else A["mtm2"]
                for fc in range(8):
                    t.matmul(P[7][0:16, (idx % 4) * 128:(idx % 4) * 128 + 128],
                             A["selfP"][:, fc * 16:fc * 16 + 16],
                             mp[:, fc * 128:(fc + 1) * 128],
                             start=(idx % 4 == 0 and fc == 0),
                             stop=(idx % 4 == 3 and fc == 7),
                             skip_group_check=True).then_inc(s_pp, 1)

        @b.scalar
        def _(sc):
            for grp in range(9):
                sc.wait_ge(s_pp, 32 * (grp + 1))
                sc.activation(A["Pm"][:, grp * 512:(grp + 1) * 512],
                              P[7][0:16, :], AF.Copy).then_inc(s_pev, 1)

    with nc.Block(f"glaO{H}") as b:
        @b.tensor
        def _(t):
            for idx, (t8, sp) in enumerate(GLA_PAIRS):
                t.wait_ge(s_pev, idx // 4 + 1)
                if idx >= 1:
                    t.wait_ge(s_ov, idx)
                for fc in range(8):
                    bank = P[4] if fc < 4 else P[5]
                    col = (fc % 4) * 128
                    ins = t.matmul(bank[:, col:col + 128],
                                   A["selfT"][0:16, fc * 128:(fc + 1) * 128],
                                   A["Pm"][0:16, idx * 128:(idx + 1) * 128],
                                   start=(fc % 4 == 0), stop=(fc % 4 == 3),
                                   skip_group_check=True)
                ins.then_inc(s_ob, 1)

        @b.vector
        def _(v):
            v.wait_ge(s_pev2, 16)
            for idx, (t8, sp) in enumerate(GLA_PAIRS):
                v.wait_ge(s_ob, idx + 1)
                so_ap = A["so2"][:, :].rearrange(
                    "p (f t c) -> p f t c", f=8, t=8)[:, :, t8, :]
                svA = A["sv2"][:, :].rearrange(
                    "p (f t c) -> p f t c", f=8, t=8)[:, 0:4, sp, :]
                svB = A["sv2"][:, :].rearrange(
                    "p (f t c) -> p f t c", f=8, t=8)[:, 4:8, sp, :]
                if t8 == 0:
                    v.tensor_mul(so_ap[:, 0:4, :], P[4][:, :].rearrange(
                        "p (f c) -> p f c", f=4), svA)
                    v.tensor_mul(so_ap[:, 4:8, :], P[5][:, :].rearrange(
                        "p (f c) -> p f c", f=4), svB).then_inc(s_ov, 1)
                    v.drain()
                    continue
                if sp == 0:
                    dst = A["otA"]
                    v.tensor_mul(dst[:, 0:512], P[4][:, :].rearrange(
                        "p (f c) -> p f c", f=4), svA)
                    v.tensor_mul(dst[:, 512:1024], P[5][:, :].rearrange(
                        "p (f c) -> p f c", f=4), svB).then_inc(s_ov, 1)
                    v.drain()
                    continue
                v.tensor_mul(A["mscr"][:, 0:512], P[4][:, :].rearrange(
                    "p (f c) -> p f c", f=4), svA)
                v.tensor_mul(A["mscr"][:, 512:1024], P[5][:, :].rearrange(
                    "p (f c) -> p f c", f=4), svB)
                v.drain()
                prev = A["otA"] if sp % 2 == 1 else A["otB"]
                if sp == t8:
                    v.tensor_add(so_ap,
                                 prev[:, :].rearrange("p (f c) -> p f c", f=8),
                                 A["mscr"][:, :].rearrange(
                                     "p (f c) -> p f c", f=8)).then_inc(s_ov, 1)
                else:
                    nxt = A["otB"] if sp % 2 == 1 else A["otA"]
                    v.tensor_add(nxt[:, :], prev[:, :], A["mscr"][:, :]
                                 ).then_inc(s_ov, 1)
                v.drain()

    if upto == "O":
        return s_ov, 36

    with nc.Block(f"glaL{H}") as b:
        @b.scalar
        def _(sc):
            sc.wait_ge(s_ov, 36)
            for jj in range(2):
                for fc in range(8):
                    i = jj * 8 + fc
                    osq = A["mtm"] if i % 2 == 0 else A["mtm2"]
                    if i >= 2:
                        sc.wait_ge(s_m2, 8 * (i - 1))
                    sc.activation(osq[:, 0:512],
                                  A["so2"][:, fc * 1024 + jj * 512:
                                           fc * 1024 + jj * 512 + 512],
                                  AF.Square).then_inc(s_osq, 1)
                sc.wait_ge(s_m1, 8 * (jj + 1))
                sc.activation(A["m1r"][:, jj * 512:(jj + 1) * 512],
                              P[6][0:16, :], AF.Copy).then_inc(s_mre, 1)
                sc.wait_ge(s_m2, 64 * (jj + 1))
                sc.activation(A["m2r"][:, jj * 512:(jj + 1) * 512],
                              P[7][0:16, :], AF.Copy).then_inc(s_mre, 1)
            sc.wait_ge(s_rwm, 1)
            sc.activation(A["msr"][:, :], A["m2r"][:, :], AF.Ln,
                          bias=A["epsln"][0:16, 0:1])
            sc.drain()
            sc.activation(A["rstdr"][:, :], A["msr"][:, :], AF.Exp,
                          scale=-0.5).then_inc(s_rst, 1)
            sc.wait_ge(s_ms, 1)
            sc.activation(A["rstdb"][:, :], A["rstdr"][:, :], AF.Copy)
            sc.activation(A["msb"][:, :], A["m1r"][:, :], AF.Copy
                          ).then_inc(s_rb, 2)

        @b.tensor
        def _(t):
            for jj in range(2):
                if jj == 1:
                    t.wait_ge(s_mre, 2)
                t.wait_ge(s_ov, 36)
                for fc in range(8):
                    t.matmul(P[6][0:16, :], A["selfP"][:, fc * 16:fc * 16 + 16],
                             A["so2"][:, fc * 1024 + jj * 512:
                                      fc * 1024 + jj * 512 + 512],
                             start=(fc == 0), stop=(fc == 7)).then_inc(s_m1, 1)
                for fc in range(8):
                    i = jj * 8 + fc
                    t.wait_ge(s_osq, i + 1)
                    osq = A["mtm"] if i % 2 == 0 else A["mtm2"]
                    t.matmul(P[7][0:16, :], A["selfP"][:, fc * 16:fc * 16 + 16],
                             osq[:, 0:512],
                             start=(fc == 0), stop=(fc == 7)).then_inc(s_m2, 8)
            t.wait_ge(s_rb, 2)
            for jj in range(2):
                for fc in range(8):
                    ri = jj * 8 + fc
                    if ri >= 1:
                        t.wait_ge(s_apv, 3 * ri)
                    t.matmul(P[4][:, :], A["selfT"][0:16, fc * 128:
                                                    (fc + 1) * 128],
                             A["rstdb"][0:16, jj * 512:(jj + 1) * 512],
                             start=True, stop=True)
                    t.matmul(P[5][:, :], A["selfT"][0:16, fc * 128:
                                                    (fc + 1) * 128],
                             A["msb"][0:16, jj * 512:(jj + 1) * 512],
                             start=True, stop=True).then_inc(s_orp, 2)

        @b.vector
        def _(v):
            v.wait_ge(s_mre, 4)
            v.tensor_scalar_mul(A["meanr"][:, :], A["m1r"][:, :], 1.0 / 64)
            v.tensor_scalar_mul(A["msr"][:, :], A["m2r"][:, :], 1.0 / 64)
            v.drain()
            v.tensor_mul(A["m1r"][:, :], A["meanr"][:, :], A["meanr"][:, :])
            v.drain()
            v.tensor_sub(A["m2r"][:, :], A["msr"][:, :], A["m1r"][:, :]
                         ).then_inc(s_rwm, 1)
            v.wait_ge(s_rst, 1)
            v.drain()
            v.tensor_mul(A["m1r"][:, :], A["meanr"][:, :], A["rstdr"][:, :]
                         ).then_inc(s_ms, 1)
            for jj in range(2):
                for fc in range(8):
                    ri = jj * 8 + fc
                    v.wait_ge(s_orp, 2 * (ri + 1))
                    sl = slice(fc * 1024 + jj * 512, fc * 1024 + jj * 512 + 512)
                    v.tensor_mul(A["otA"][:, 0:512], A["so2"][:, sl],
                                 P[4][:, :])
                    v.drain()
                    v.tensor_sub(A["otB"][:, 0:512], A["otA"][:, 0:512],
                                 P[5][:, :])
                    v.drain()
                    v.tensor_mul(A["u2"][:, sl], A["otB"][:, 0:512],
                                 A["og2"][:, sl]).then_inc(s_apv, 3)
                    v.drain()

    if upto == "U":
        return s_apv, 48

    yr = [A["yr0"], A["yr1"]]
    zsl = [A["zsl0"], A["zsl1"]]
    with nc.Block(f"glaZ{H}") as b:
        @b.sync
        def _(sy):
            sy.wait_ge(s_apv, 48)
            for fo in range(8):
                if fo >= 2:
                    sy.wait_ge(s_zev, 2 * (fo - 1))
                sy.dma_start(
                    out=yr[fo % 2][:, :].rearrange("p (t c) -> p t c", t=8),
                    in_=dr["yd"][fo * 128:(fo + 1) * 128, :].rearrange(
                        "p (t c) -> p t c", t=8)[:, :, hx:hx + 128]
                    ).then_inc(s_yl[fo % 2], 16)
                wi0 = fo * 2
                sy.wait_ge(s_zev, wi0 + 2)
                if fo >= 2:
                    sy.wait_ge(s_zd[fo % 2], 16 * (fo // 2))
                sy.dma_start(
                    out=dr["z1d"][fo * 128:(fo + 1) * 128, :].rearrange(
                        "p (t c) -> p t c", t=8)[:, :, hx:hx + 128],
                    in_=zsl[fo % 2][:, :].rearrange("p (t c) -> p t c", t=8)
                    ).then_inc(s_zd[fo % 2], 16)

        @b.tensor
        def _(t):
            t.wait_ge(GS1[0], b1 + 48)
            t.wait_ge(s_apv, 48)
            for fo in range(8):
                for jb in range(2):
                    wi = fo * 2 + jb
                    ps = P[wi % 4]
                    if wi >= 4:
                        t.wait_ge(s_zev, wi - 3)
                    for kc in range(8):
                        ins = t.matmul(ps[:, :],
                                       A["gwA"][:, kc * 1024 + fo * 128:
                                                kc * 1024 + fo * 128 + 128],
                                       A["u2"][:, kc * 1024 + jb * 512:
                                               kc * 1024 + jb * 512 + 512],
                                       start=(kc == 0), stop=(kc == 7))
                    ins.then_inc(s_wm, 1)

        @b.vector
        def _(v):
            for fo in range(8):
                for jb in range(2):
                    wi = fo * 2 + jb
                    v.wait_ge(s_wm, wi + 1)
                    v.wait_ge(s_yl[fo % 2], 16 * (fo // 2 + 1))
                    if fo >= 2 and jb == 0:
                        v.wait_ge(s_zd[fo % 2], 16 * (fo // 2))
                    v.tensor_add(zsl[fo % 2][:, jb * 512:(jb + 1) * 512],
                                 P[wi % 4][:, :],
                                 yr[fo % 2][:, jb * 512:(jb + 1) * 512]
                                 ).then_inc(s_zev, 1)
            v.wait_ge(s_zd[0], 16 * 4)
            v.wait_ge(s_zd[1], 16 * 4)
            v.nop().then_inc(s_fin, 1)

    return s_fin, 1


def _glu(cx, wait):
    nc, A, P, dr = cx.nc, cx.A, cx.P, cx.dr
    sem = cx.sem
    s_hm = sem("ghm"); s_hs = sem("ghs"); s_hv = sem("ghv2")
    s_dm = sem("gdm"); s_dv = sem("gdv"); s_gfin = sem("gluf")
    s_wgu = [cx.sem_sw("wgua"), cx.sem_sw("wgub")]
    s_wd = [cx.sem_sw(f"wd{i}") for i in range(4)]
    s_x2r = [cx.sem_hw("x2ra"), cx.sem_hw("x2rb")]
    s_z1r = [cx.sem_hw("z1ra"), cx.sem_hw("z1rb")]
    s_go = [cx.sem_hw(f"go{i}") for i in range(4)]
    wgb = [A["wgb0"], A["wgb1"]]; wub = [A["wub0"], A["wub1"]]
    wdb = [A["wdb0"], A["wdb1"], A["wdb2"], A["wdb3"]]
    sst = [A["sstmp0"], A["sstmp1"]]
    tgl = [A["tglu0"], A["tglu1"]]
    x2r = [A["x2r0"], A["x2r1"]]; z1r = [A["z1r0"], A["z1r1"]]

    def hbs(hc, jb):
        return A["hb2"][:, hc * 2048 + jb * 512:hc * 2048 + (jb + 1) * 512]

    with nc.Block("glu") as b:
        @b.gpsimd
        def _(g):
            g.wait_ge(*wait)
            for hc in range(32):
                if hc >= 2:
                    g.wait_ge(s_hm, 4 * (hc - 1))
                g.dma_start(
                    out=wgb[hc % 2][:, :].rearrange("p (c n) -> p c n", c=8),
                    in_=dr["bWgT"][:, hc * 128:(hc + 1) * 128].rearrange(
                        "(c p) n -> p c n", p=128)
                ).then_inc(s_wgu[hc % 2], 16)
                g.dma_start(
                    out=wub[hc % 2][:, :].rearrange("p (c n) -> p c n", c=8),
                    in_=dr["bWuT"][:, hc * 128:(hc + 1) * 128].rearrange(
                        "(c p) n -> p c n", p=128)
                ).then_inc(s_wgu[hc % 2], 16)
            for fo2 in range(4):
                for hc in range(32):
                    i = fo2 * 32 + hc
                    if i >= 4:
                        g.wait_ge(s_dm, i - 3)
                    g.dma_start(
                        out=wdb[i % 4][:, :],
                        in_=dr["bWdT"][hc * 128:(hc + 1) * 128,
                                       fo2 * 256:(fo2 + 1) * 256]
                    ).then_inc(s_wd[i % 4], 16)

        @b.tensor
        def _(t):
            for hc in range(32):
                t.wait_ge(s_wgu[hc % 2], 32 * (hc // 2 + 1))
                for jb in range(4):
                    i = hc * 4 + jb
                    bA = P[(2 * i) % 8]; bB = P[(2 * i + 1) % 8]
                    if i >= 4:
                        t.wait_ge(s_hv, i - 3)
                    for kc in range(8):
                        t.matmul(bA[:, :], wgb[hc % 2][:, kc * 128:
                                                       (kc + 1) * 128],
                                 A["nrm2"][:, kc * 2048 + jb * 512:
                                           kc * 2048 + jb * 512 + 512],
                                 start=(kc == 0), stop=(kc == 7))
                    for kc in range(8):
                        ins = t.matmul(bB[:, :], wub[hc % 2][:, kc * 128:
                                                             (kc + 1) * 128],
                                       A["nrm2"][:, kc * 2048 + jb * 512:
                                                 kc * 2048 + jb * 512 + 512],
                                       start=(kc == 0), stop=(kc == 7))
                    ins.then_inc(s_hm, 1)
            for fo2 in range(4):
                for hc in range(32):
                    i = fo2 * 32 + hc
                    t.wait_ge(s_wd[i % 4], 16 * (i // 4 + 1))
                    if hc == 0 and fo2 >= 1:
                        t.wait_ge(s_dv, 8 * fo2)
                    if fo2 == 0 and hc == 0:
                        t.wait_ge(s_hv, 128)
                    for fh in range(2):
                        for jb in range(4):
                            ins = t.matmul(P[fh * 4 + jb][:, :],
                                           wdb[i % 4][:, fh * 128:
                                                      (fh + 1) * 128],
                                           hbs(hc, jb),
                                           start=(hc == 0), stop=(hc == 31))
                    ins.then_inc(s_dm, 1)

        @b.scalar
        def _(sc):
            for hc in range(32):
                for jb in range(4):
                    i = hc * 4 + jb
                    sc.wait_ge(s_hm, i + 1)
                    if i >= 2:
                        sc.wait_ge(s_hv, i - 1)
                    bA = P[(2 * i) % 8]
                    sc.activation(sst[i % 2][:, :], bA[:, :], AF.Sigmoid
                                  ).then_inc(s_hs, 1)

        @b.vector
        def _(v):
            for hc in range(32):
                for jb in range(4):
                    i = hc * 4 + jb
                    v.wait_ge(s_hs, i + 1)
                    bA = P[(2 * i) % 8]; bB = P[(2 * i + 1) % 8]
                    v.tensor_mul(tgl[i % 2][:, :], sst[i % 2][:, :], bB[:, :])
                    v.drain()
                    v.tensor_mul(hbs(hc, jb), tgl[i % 2][:, :], bA[:, :]
                                 ).then_inc(s_hv, 1)
            for fo2 in range(4):
                for fh in range(2):
                    for jb in range(4):
                        k = fo2 * 8 + fh * 4 + jb
                        v.wait_ge(s_dm, 32 * (fo2 + 1))
                        v.wait_ge(s_x2r[k % 2], 32 * (k // 2 + 1))
                        v.wait_ge(s_z1r[k % 2], 16 * (k // 2 + 1))
                        if k >= 4:
                            v.wait_ge(s_go[k % 4], 16 * (k // 4))
                        v.tensor_add(tgl[k % 2][:, :], P[fh * 4 + jb][:, :],
                                     x2r[k % 2][:, :])
                        v.drain()
                        v.tensor_add(A["gslot"][:, (k % 4) * 512:
                                                (k % 4) * 512 + 512],
                                     tgl[k % 2][:, :], z1r[k % 2][:, :]
                                     ).then_inc(s_dv, 1)
                        v.drain()

        @b.sync
        def _(sy):
            sy.wait_ge(*wait)
            for k in range(33):
                if k < 32:
                    fo2, r = divmod(k, 8)
                    fh, jb = divmod(r, 4)
                    fo = fo2 * 2 + fh
                    if k >= 2:
                        sy.wait_ge(s_dv, k - 1)
                    for d2 in range(2):
                        sy.dma_start(
                            out=x2r[k % 2][:, d2 * 256:(d2 + 1) * 256],
                            in_=dr["x2mine"][2 * jb + d2,
                                             fo * 128:(fo + 1) * 128, :]
                        ).then_inc(s_x2r[k % 2], 16)
                    sy.dma_start(out=z1r[k % 2][:, :],
                                 in_=dr["z1d"][fo * 128:(fo + 1) * 128,
                                               jb * 512:(jb + 1) * 512]
                                 ).then_inc(s_z1r[k % 2], 16)
                if k >= 1:
                    kk = k - 1
                    fo2, r = divmod(kk, 8)
                    fh, jb = divmod(r, 4)
                    fo = fo2 * 2 + fh
                    sy.wait_ge(s_dv, kk + 1)
                    sy.dma_start(out=dr["x4d"][fo * 128:(fo + 1) * 128,
                                               jb * 512:(jb + 1) * 512],
                                 in_=A["gslot"][:, (kk % 4) * 512:
                                                (kk % 4) * 512 + 512]
                                 ).then_inc(s_go[kk % 4], 16)
            for q in range(4):
                sy.wait_ge(s_go[q], 16 * 8)
            sy.nop().then_inc(s_gfin, 1)

    return s_gfin


def _ffn(cx, wait, src_hbm, tag, out):
    nc, A, P, dr = cx.nc, cx.A, cx.P, cx.dr
    sem = cx.sem
    s_ln = _norm(cx, tag + "ln", src_hbm=src_hbm, src_sb=None, dst=A["xhat"],
                 kind="ln", wait=wait)

    if cx.ffn_sw is None:
        cx.ffn_sw = {k2: [cx.sem_sw("ffn" + k2), 0] for k2 in
                     ("rw1", "rw2", "w1a", "w1b", "w2a", "w2b")}
    FS = cx.ffn_sw
    s_rw1 = FS["rw1"]; s_rw2 = FS["rw2"]
    s_rt = sem(tag + "rt"); s_rv = sem(tag + "rv")
    s_rtt = sem(tag + "rtt"); s_rte = sem(tag + "rte")
    s_mrd = cx.sem_hw(tag + "mrd"); s_mrt = sem(tag + "mrt"); s_mrv = sem(tag + "mrv")
    s_w1 = [FS["w1a"], FS["w1b"]]
    s_hmm = sem(tag + "hmm"); s_hsc = sem(tag + "hsc"); s_hv = sem(tag + "hv")
    s_w2 = [FS["w2a"], FS["w2b"]]
    s_yg = sem(tag + "yg")
    s_yv = sem(tag + "yv")
    s_xr = [cx.sem_hw(tag + "xra"), cx.sem_hw(tag + "xrb")]
    s_yd = [cx.sem_hw(tag + "yd0"), cx.sem_hw(tag + "yd1"),
            cx.sem_hw(tag + "yd2"), cx.sem_hw(tag + "yd3")]
    s_fin = sem(tag + "fin")
    wb = [A["wbA"], A["wbB"]]

    def hbs(hc, nt):
        return A["hb"][:, hc * 1024 + nt * 512:hc * 1024 + (nt + 1) * 512]

    with nc.Block(tag) as b:
        @b.gpsimd
        def _(g):
            g.wait_ge(s_ln, 1)
            g.dma_start(out=A["rwT"][:, :].rearrange("p (c n) -> p c n", c=8),
                        in_=dr["fRWTb"][0:1024, :].rearrange(
                            "(c p) n -> p c n", p=128)).then_inc(s_rw1[0], 16)
            g.dma_start(out=A["rbrow"][:, :], in_=dr["fRWTb"][1024:1025, :]
                        ).then_inc(s_rw2[0], 16)
            for half in range(2):
                for grp in range(4):
                    gi = half * 4 + grp
                    if gi >= 2:
                        g.wait_ge(s_hv, (gi - 1) * 16)
                    if half == 1 and grp <= 1:
                        g.wait_ge(s_yg, 8)
                    g.dma_start(
                        out=wb[gi % 2][:, :].rearrange("p (c n) -> p c n", c=8),
                        in_=dr["fW1T"][:, grp * 1024:(grp + 1) * 1024].rearrange(
                            "(c p) n -> p c n", p=128)
                    ).then_inc(s_w1[gi % 2][0], 16)
                for q2 in range(2):
                    for grp in range(4):
                        gi2 = (half * 2 + q2) * 4 + grp
                        if gi2 >= 2:
                            g.wait_ge(s_yg, gi2 - 1)
                        if q2 == 0 and grp <= 1:
                            g.wait_ge(s_hmm, (half + 1) * 64)
                        g.dma_start(
                            out=wb[gi2 % 2][:, :].rearrange(
                                "p (c n) -> p c n", c=8),
                            in_=dr["fW2T"][grp * 1024:(grp + 1) * 1024, :]
                            .rearrange("(c p) n -> p c n", p=128)
                        ).then_inc(s_w2[gi2 % 2][0], 16)

        @b.scalar
        def _(s):
            for half in range(2):
                for grp in range(4):
                    for blk2 in range(4):
                        blk = grp * 4 + blk2
                        gb = half * 16 + blk
                        if gb == 0:
                            s.wait_ge(s_rte, 16)
                        if gb >= 1:
                            s.wait_ge(s_mrt, 2 * gb)
                        s.dma_start(out=A["maskrow"][:, :],
                                    in_=A["maskT"][blk:blk + 1, :]
                                    ).then_inc(s_mrd, 16)
                        for hc2 in range(2):
                            hc = blk * 2 + hc2
                            for nt in range(2):
                                i = gb * 4 + hc2 * 2 + nt
                                s.wait_ge(s_hmm, i + 1)
                                if i >= 2:
                                    s.wait_ge(s_hv, i - 1)
                                ps = P[i % 3]
                                ft = A["f_tmpA"] if i % 2 == 0 else A["f_tmpB"]
                                s.activation(ft[:, :], ps[:, :], AF.Relu,
                                             bias=A["biasf1"][:, hc:hc + 1]
                                             ).then_inc(s_hsc, 1)

        @b.sync
        def _(sy):
            sy.wait_ge(*wait)
            for j in range(33):
                if j < 32:
                    qn, oc = divmod(j, 8)
                    if j >= 2:
                        sy.wait_ge(s_yv, j - 1)
                    xr = A["frot0"] if j % 2 == 0 else A["frot1"]
                    sy.dma_start(out=xr[:, :],
                                 in_=src_hbm[oc * 128:(oc + 1) * 128,
                                             qn * 512:(qn + 1) * 512]
                                 ).then_inc(s_xr[j % 2], 16)
                if j >= 1:
                    k = j - 1
                    qn, oc = divmod(k, 8)
                    sy.wait_ge(s_yv, k + 1)
                    src = A["ytmp"][:, (k % 4) * 512:(k % 4) * 512 + 512]
                    if out == "cc":
                        for d2 in range(2):
                            d = qn * 2 + d2
                            sy.dma_start(out=dr["cc_in"][d, oc * 128:
                                                         (oc + 1) * 128, :],
                                         in_=src[:, d2 * 256:(d2 + 1) * 256]
                                         ).then_inc(s_yd[k % 4], 16)
                    else:
                        sy.dma_start(out=dr["yT"][oc * 128:(oc + 1) * 128,
                                                  qn * 512:(qn + 1) * 512],
                                     in_=src).then_inc(s_yd[k % 4], 16)
            per = (2 if out == "cc" else 1) * 16
            for q in range(4):
                sy.wait_ge(s_yd[q], 8 * per)
            sy.nop().then_inc(s_fin, 1)

        @b.tensor
        def _(t):
            t.wait_ge(s_rw1[0], s_rw1[1] + 16)
            t.wait_ge(s_rw2[0], s_rw2[1] + 16)
            for tc in range(16):
                ps = P[tc % 2]
                if tc >= 2:
                    t.wait_ge(s_rv, 4 * (tc - 1))
                for kc in range(8):
                    t.matmul(ps[:, 0:16], fm(A["xhat"], kc, tc * 128,
                                             (tc + 1) * 128),
                             A["rwT"][:, kc * 16:(kc + 1) * 16],
                             start=(kc == 0), stop=False)
                t.matmul(ps[:, 0:16], A["onesb"][0:1, :], A["rbrow"][0:1, :],
                         start=False, stop=True).then_inc(s_rt, 1)
            for tc in range(16):
                t.wait_ge(s_rv, 4 * tc + 4)
                if tc >= 2:
                    t.wait_ge(s_rte, tc - 1)
                t.matmul(P[4 + tc % 2][0:16, 0:128],
                         A["mtok"][:, tc * 16:(tc + 1) * 16],
                         A["identb"][:, :], start=True, stop=True
                         ).then_inc(s_rtt, 1)
            for half in range(2):
                for grp in range(4):
                    gi = half * 4 + grp
                    sw = s_w1[gi % 2]
                    t.wait_ge(sw[0], sw[1] + 16 * (gi // 2 + 1))
                    if half == 1 and grp == 0:
                        t.wait_ge(s_yv, 16)
                    for blk2 in range(4):
                        blk = grp * 4 + blk2
                        gb = half * 16 + blk
                        t.wait_ge(s_mrd, 16 * (blk + 1 if half == 0
                                               else 16 + blk + 1))
                        if gb >= 2:
                            t.wait_ge(s_mrv, 2 * (gb - 1))
                        for n2 in range(2):
                            t.matmul(P[4 + n2][:, :], A["onesb"][0:1, :],
                                     A["maskrow"][0:1, half * 1024 + n2 * 512:
                                                  half * 1024 + (n2 + 1) * 512],
                                     start=True, stop=True
                                     ).then_inc(s_mrt, 1)
                        for hc2 in range(2):
                            hc = blk * 2 + hc2
                            for nt in range(2):
                                i = gb * 4 + hc2 * 2 + nt
                                ps = P[i % 3]
                                if i >= 3:
                                    t.wait_ge(s_hv, i - 2)
                                ins = None
                                for kc in range(8):
                                    ins = t.matmul(
                                        ps[:, :],
                                        wb[gi % 2][:, kc * 1024 + (hc % 8) * 128:
                                                   kc * 1024 + (hc % 8) * 128 + 128],
                                        fm(A["xhat"], kc, half * 1024 + nt * 512,
                                           half * 1024 + (nt + 1) * 512),
                                        start=(kc == 0), stop=(kc == 7))
                                ins.then_inc(s_hmm, 1)
                for q2 in range(2):
                    qn = half * 2 + q2
                    for grp in range(4):
                        gi2 = qn * 4 + grp
                        sw2 = s_w2[gi2 % 2]
                        t.wait_ge(sw2[0], sw2[1] + 16 * (gi2 // 2 + 1))
                        if grp == 0:
                            t.wait_ge(s_hv, (half + 1) * 64)
                            if qn >= 1:
                                t.wait_ge(s_yv, 8 * qn)
                        ins = None
                        for oc in range(8):
                            for ic8 in range(8):
                                ic = grp * 8 + ic8
                                ins = t.matmul(
                                    P[oc][:, :],
                                    wb[gi2 % 2][:, ic8 * 1024 + oc * 128:
                                                ic8 * 1024 + oc * 128 + 128],
                                    hbs(ic, q2),
                                    start=(grp == 0 and ic8 == 0),
                                    stop=(grp == 3 and ic8 == 7),
                                    skip_group_check=True)
                        ins.then_inc(s_yg, 1)

        @b.vector
        def _(v):
            for tc in range(19):
                v.drain()
                if tc < 16:
                    v.wait_ge(s_rt, tc + 1)
                    sl = slice((tc % 4) * 16, (tc % 4) * 16 + 16)
                    v.tensor_copy(A["ltok"][:, sl], P[tc % 2][:, 0:16]
                                  ).then_inc(s_rv, 1)
                else:
                    v.nop().then_inc(s_rv, 1)
                if 1 <= tc and tc - 1 < 16:
                    tp = (tc - 1) % 4
                    v.max(A["m8"][:, tp * 8:tp * 8 + 8],
                          A["ltok"][:, tp * 16:tp * 16 + 16]).then_inc(s_rv, 1)
                else:
                    v.nop().then_inc(s_rv, 1)
                if 2 <= tc and tc - 2 < 16:
                    tp = (tc - 2) % 4
                    v.tensor_single_scalar(A["mtokf"][:, tp * 16:tp * 16 + 16],
                                           A["ltok"][:, tp * 16:tp * 16 + 16],
                                           A["m8"][:, tp * 8 + 3:tp * 8 + 4],
                                           ALU.is_ge).then_inc(s_rv, 1)
                else:
                    v.nop().then_inc(s_rv, 1)
                if 3 <= tc and tc - 3 < 16:
                    tp = (tc - 3) % 4
                    v.tensor_copy(A["mtok"][:, (tc - 3) * 16:(tc - 2) * 16],
                                  A["mtokf"][:, tp * 16:tp * 16 + 16]
                                  ).then_inc(s_rv, 1)
                else:
                    v.nop().then_inc(s_rv, 1)
            for tc in range(16):
                v.wait_ge(s_rtt, tc + 1)
                v.tensor_copy(A["maskT"][:, tc * 128:(tc + 1) * 128],
                              P[4 + tc % 2][0:16, 0:128]).then_inc(s_rte, 1)
            for half in range(2):
                for grp in range(4):
                    for blk2 in range(4):
                        blk = grp * 4 + blk2
                        gb = half * 16 + blk
                        v.drain()
                        for n2 in range(2):
                            v.wait_ge(s_mrt, 2 * gb + n2 + 1)
                            v.tensor_copy(A["mrepb"][:, n2 * 512:(n2 + 1) * 512],
                                          P[4 + n2][:, :]).then_inc(s_mrv, 1)
                        v.drain()
                        for hc2 in range(2):
                            hc = blk * 2 + hc2
                            for nt in range(2):
                                i = gb * 4 + hc2 * 2 + nt
                                v.wait_ge(s_hsc, i + 1)
                                ft = (A["f_tmpA"] if i % 2 == 0
                                      else A["f_tmpB"])
                                v.tensor_mul(hbs(hc, nt), ft[:, :],
                                             A["mrepb"][:, nt * 512:
                                                        (nt + 1) * 512]
                                             ).then_inc(s_hv, 1)
                for q2 in range(2):
                    qn = half * 2 + q2
                    v.wait_ge(s_yg, (qn + 1) * 4)
                    for oc in range(8):
                        j = qn * 8 + oc
                        v.wait_ge(s_xr[j % 2], 16 * (j // 2 + 1))
                        if j >= 4:
                            v.wait_ge(s_yd[j % 4], (2 if out == "cc" else 1)
                                      * 16 * (j // 4))
                        xr = A["frot0"] if j % 2 == 0 else A["frot1"]
                        v.scalar_tensor_tensor(
                            A["ytmp"][:, (j % 4) * 512:(j % 4) * 512 + 512],
                            P[oc][:, :], A["biasf2"][:, oc:oc + 1],
                            xr[:, :], ALU.add, ALU.add).then_inc(s_yv, 1)
    FS["rw1"][1] += 16; FS["rw2"][1] += 16
    FS["w1a"][1] += 64; FS["w1b"][1] += 64
    FS["w2a"][1] += 128; FS["w2b"][1] += 128
    return s_fin
